# version 08
# baseline (speedup 1.0000x reference)
import numpy as np
import concourse.bass as bass
import concourse.mybir as mybir
from concourse.alu_op_type import AluOpType as AL

F32 = mybir.dt.float32
I16 = mybir.dt.int16
U16 = mybir.dt.uint16
AF = mybir.ActivationFunctionType
AX = mybir.AxisListType

ENGINES = ("sync", "tensor", "vector", "scalar", "gpsimd")
DSEMS = tuple(f"sd{i}" for i in range(8))
SEM_OWNER = {**{s: "sync" for s in DSEMS},
             "st": "tensor", "sv": "vector", "ss": "scalar", "sg": "gpsimd"}

D = 1024
NFT = 8
H = 16
DH = 64
FF = 3072
NIT = FF // 128
L = 64
QMAX = 32767.0
EPS = 1e-5


class Cfg:
    def __init__(self, TOWN=2048, TWARM=1024, TB=512, n_cores=8):
        assert TB == 512, "buffer layout assumes TB == 512"
        assert TOWN % TB == 0 and (TWARM % TB == 0 or TWARM == 0)
        self.TOWN, self.TWARM, self.TB, self.n_cores = TOWN, TWARM, TB, n_cores
        self.NOP = TOWN + 128
        self.NWP = TWARM + 128 if TWARM > 0 else 128
        self.TC = self.NWP + self.NOP
        self.n_own_tiles = self.NOP // 128
        self.n_warm_tiles = self.NWP // 128
        self.NTB = TOWN // 512 if TOWN % 512 == 0 else None
        assert TOWN % 512 == 0 or TOWN < 512
        self.FTB = 512 if TOWN >= 512 else TOWN
        self.n_ftb = TOWN // self.FTB


class Emitter:

    def __init__(self, nc):
        self.nc = nc
        self.items = {e: [] for e in ENGINES}
        self.counts = {}
        self.touched = {e: [] for e in ENGINES}
        self.unfenced = {e: 0 for e in ENGINES}
        self.nbar = 0
        self.maxcount = {}

    def new_sem(self, name):
        self.counts[name] = 0
        self.maxcount[name] = 0
        return name

    @staticmethod
    def _rng(ap):
        try:
            t = ap.tensor.name
            lo = int(ap.offset)
            sz = 1
            for st, ct in ap.ap:
                sz += abs(int(st)) * (int(ct) - 1)
            return (t, lo, lo + sz)
        except Exception:
            return (str(ap), 0, 1 << 60)

    def _hazard(self, eng, reads, writes):
        tl = self.touched[eng]
        def ov(a, b):
            return a[0] == b[0] and a[1] < b[2] and b[1] < a[2]
        for ap in reads:
            r = self._rng(ap)
            for (k, lo, hi, w) in tl:
                if w and ov(r, (k, lo, hi)):
                    return True
        for ap in writes:
            r = self._rng(ap)
            for (k, lo, hi, w) in tl:
                if ov(r, (k, lo, hi)):
                    return True
        return False

    def _note(self, eng, reads, writes):
        tl = self.touched[eng]
        for ap in reads:
            k, lo, hi = self._rng(ap)
            tl.append((k, lo, hi, False))
        for ap in writes:
            k, lo, hi = self._rng(ap)
            tl.append((k, lo, hi, True))

    def op(self, eng, fn, reads=(), writes=(), inc=None, inc_by=1):
        reads, writes = list(reads), list(writes)
        if eng != "sync" and self._hazard(eng, reads, writes):
            self.items[eng].append(("drain",))
            self.touched[eng] = []
        self._note(eng, reads, writes)
        if inc is not None:
            self.counts[inc] += inc_by
            self.maxcount[inc] = max(self.maxcount[inc], self.counts[inc])
            assert self.counts[inc] <= 240, (
                f"sem {inc} exceeds 240 within an epoch ({self.counts[inc]}); "
                f"insert a barrier (epoch {self.nbar})")
            self.items[eng].append(("op", fn, inc, inc_by))
        else:
            self.items[eng].append(("op", fn, None, 0))
            self.unfenced[eng] += 1

    def fence(self, eng, sem):
        if self.unfenced[eng] == 0:
            return
        self.unfenced[eng] = 0
        self.counts[sem] += 1
        self.maxcount[sem] = max(self.maxcount[sem], self.counts[sem])
        assert self.counts[sem] <= 240, f"sem {sem} epoch overflow"
        self.items[eng].append(("fenceinc", sem))

    def wait(self, eng, sem, val):
        if val > 0:
            self.items[eng].append(("wait", sem, val))

    def wait_now(self, eng, sem):
        if sem == "sd":
            for s in DSEMS:
                self.wait(eng, s, self.counts[s])
            return
        own = SEM_OWNER[sem]
        if self.unfenced[own] > 0:
            self.fence(own, sem)
        self.wait(eng, sem, self.counts[sem])

    def barrier(self):
        tok = dict(self.counts)
        k = self.nbar
        self.nbar += 1
        assert self.nbar <= 125, "too many barriers for 8-bit epoch sems"
        for e in ENGINES:
            self.items[e].append(("barrier", k, tok))
            self.touched[e] = []
            self.unfenced[e] = 0
        for s in self.counts:
            self.counts[s] = 0

    def replay(self, block, sems, esems, owner):
        em = self

        def mk(eng_name):
            def body(proxy):
                last = [None]
                last_inc = [False]

                def emit_fence(s):
                    proxy.drain()
                    proxy.nop().then_inc(sems[s], 1)
                    last[0] = None
                    last_inc[0] = False

                for it in em.items[eng_name]:
                    kind = it[0]
                    if kind == "drain":
                        proxy.drain()
                        last[0] = None
                        last_inc[0] = False
                    elif kind == "wait":
                        proxy.wait_ge(sems[it[1]], it[2])
                        last[0] = None
                        last_inc[0] = False
                    elif kind == "fenceinc":
                        emit_fence(it[1])
                    elif kind == "ewait":
                        proxy.wait_ge(esems[eng_name], it[1])
                        last[0] = None
                        last_inc[0] = False
                    elif kind == "barrier":
                        k, tok = it[1], it[2]
                        for s, v in tok.items():
                            if v > 0:
                                proxy.wait_ge(sems[s], v)
                        proxy.nop().then_inc(esems[eng_name], 1)
                        for f in ENGINES:
                            proxy.wait_ge(esems[f], 2 * k + 1)
                        for s, own in owner.items():
                            if own == eng_name:
                                proxy.sem_clear(sems[s])
                        proxy.drain()
                        proxy.nop().then_inc(esems[eng_name], 1)
                        for f in ENGINES:
                            proxy.wait_ge(esems[f], 2 * k + 2)
                        last[0] = None
                        last_inc[0] = False
                    else:
                        _, fn, b, c = it
                        instr = fn(proxy)
                        last[0] = instr
                        last_inc[0] = False
                        if b is not None and instr is not None:
                            instr.then_inc(sems[b], c)
            return body

        for e in ENGINES:
            getattr(block, e)(mk(e))


def build(cfg: Cfg, debug=False):
    nc = bass.Bass("TRN2", target_bir_lowering=False)
    TOWN, TWARM, TB = cfg.TOWN, cfg.TWARM, cfg.TB
    NOP, NWP, TC = cfg.NOP, cfg.NWP, cfg.TC
    FTB, NFTB = cfg.FTB, cfg.n_ftb

    dt = nc.dram_tensor
    x_own = dt("x_own", [NOP, D], F32, kind="ExternalInput").ap()
    x_warm = dt("x_warm", [NWP, D], F32, kind="ExternalInput").ap()
    WT = {n: dt(n, [D, D], F32, kind="ExternalInput").ap()
          for n in ("WrT", "WkT", "WvT", "WgT", "WoT")}
    Wup = dt("Wup", [FF, D], F32, kind="ExternalInput").ap()
    Wgate = dt("Wgate", [FF, D], F32, kind="ExternalInput").ap()
    Wdown = dt("Wdown", [D, FF], F32, kind="ExternalInput").ap()
    vec_cols = dt("vec_cols", [128, 7 * NFT], F32, kind="ExternalInput").ap()
    td_row = dt("td_row", [1, D], F32, kind="ExternalInput").ap()
    ident_i = dt("ident_c", [128, 128], F32, kind="ExternalInput").ap()
    mask64_i = dt("mask64_c", [64, 64], F32, kind="ExternalInput").ap()
    iota64_i = dt("iota64_c", [1, 64], F32, kind="ExternalInput").ap()
    ones_row_i = dt("ones_row_c", [1, 128], F32, kind="ExternalInput").ap()
    meanw_i = dt("meanw_c", [128, 1], F32, kind="ExternalInput").ap()
    y_out = dt("y", [TOWN, D], F32, kind="ExternalOutput").ap()

    din = lambda n, s, dtp=F32: dt(n, s, dtp, kind="Internal").ap()
    sp_xn = din("sp_xn", [D, TC])
    sp_k = din("sp_k", [D, TWARM + TOWN])
    sp_v = din("sp_v", [D, TWARM + TOWN])
    sp_r = din("sp_r", [D, TOWN])
    sp_g = din("sp_g", [D, TOWN])
    sp_s = din("sp_s", [D, TOWN])
    sp_upr = din("sp_upr", [FF, TOWN])
    sp_gater = din("sp_gater", [FF, TOWN])
    sp_upq = din("sp_upq", [FF, TOWN], I16)
    sp_gs = din("sp_gs", [FF, TOWN])
    sp_o = din("sp_o", [FF, TOWN])
    sp_wd = din("sp_wd", [FF, D])
    cc_in = [din(f"cc_in{i}", [1, 2]) for i in range(4)]
    cc_out = [din(f"cc_out{i}", [1, 2]) for i in range(4)]

    def alloc(name, cols, dtp=F32, parts=128):
        return nc.alloc_sbuf_tensor(name, [parts, cols], dtp).ap()

    ident = alloc("c_ident", 128)
    mask64 = alloc("c_mask", 64, parts=64)
    ones_row = alloc("c_ones", 128, parts=1)
    meanw = alloc("c_meanw", 1)
    vecs = alloc("c_vecs", 7 * NFT)

    e_c = alloc("c_e", 2 * NFT)
    iota_bc = alloc("c_iotabc", 64)
    SE = alloc("c_SE", 512)
    M_km = alloc("c_Mkm", 512)
    M_out = alloc("c_Mout", 512)
    M_stL = alloc("c_MstL", 512)
    M_k2L = alloc("c_Mk2L", D)
    SCAL = alloc("c_scal", 32)
    cEPS = alloc("c_eps", 1)
    AMAX = alloc("c_amax", 8)
    rowtmp = alloc("c_rowtmp", 1024, parts=1)
    St = alloc("St", 512)
    stats = alloc("p_stats", 16)

    B_W = alloc("b_w", NFT * D)
    B0 = alloc("b_0", NFT * (TB + 1))
    B1 = alloc("b_1", NFT * TB)
    B2 = alloc("b_2", NFT * TB)
    B3 = alloc("b_3", NFT * TB)
    B4 = alloc("b_4", NFT * TB)
    B5 = alloc("b_5", NFT * TB)
    B6 = alloc("b_6", NFT * TB)
    B7 = alloc("b_7", NFT * TB)
    B8 = alloc("b_8", 4 * D)
    B9 = alloc("b_9", 2 * D)

    pb = [nc.alloc_psum_tensor(f"pb{i}", [128, 512], F32).ap() for i in range(8)]

    em = Emitter(nc)
    for _s in DSEMS:
        em.new_sem(_s)
    ST = em.new_sem("st")
    SV = em.new_sem("sv")
    SS = em.new_sem("ss")
    SG = em.new_sem("sg")
    SD = "sd"

    V, Tn, Sc, Gp, Sy = "vector", "tensor", "scalar", "gpsimd", "sync"
    esem = {V: SV, Tn: ST, Sc: SS, Gp: SG, Sy: SD}

    _dma_i = [0]

    def dma(out, in_):
        s = DSEMS[_dma_i[0] % len(DSEMS)]
        _dma_i[0] += 1
        em.op(Sy, lambda E, o=out, i=in_: E.dma_start(out=o, in_=i), inc=s, inc_by=16)
        em.unfenced["sync"] = 0

    def barrier():
        em.barrier()

    def TT(out, a, b, op, eng=V):
        em.op(eng, lambda E, o=out, x=a, y=b: E.tensor_tensor(o, x, y, op=op),
              [a, b], [out])

    def TS(out, a, s1, op0, s2=None, op1=None, eng=V):
        rd = [a] + [s for s in (s1, s2) if isinstance(s, bass.AP)]
        if s2 is None:
            em.op(eng, lambda E, o=out, x=a: E.tensor_scalar(o, x, s1, None, op0=op0),
                  rd, [out])
        else:
            em.op(eng, lambda E, o=out, x=a: E.tensor_scalar(o, x, s1, s2, op0=op0, op1=op1),
                  rd, [out])

    def STT(out, in0, scal, in1, op0, op1, eng=V):
        rd = [in0, in1] + ([scal] if isinstance(scal, bass.AP) else [])
        em.op(eng, lambda E, o=out, a=in0, b=in1: E.scalar_tensor_tensor(
            o, a, scal, b, op0=op0, op1=op1), rd, [out])

    def RED(out, a, op, absval=False, eng=V):
        em.op(eng, lambda E, o=out, x=a: E.tensor_reduce(
            o, x, axis=AX.X, op=op, apply_absolute_value=absval),
            [a], [out])

    def CP(out, a, eng=V):
        em.op(eng, lambda E, o=out, x=a: E.tensor_copy(o, x), [a], [out])

    def ACT(out, a, func, bias=0.0, scale=1.0):
        rd = [a] + ([bias] if isinstance(bias, bass.AP) else [])
        em.op(Sc, lambda E, o=out, x=a: E.activation(o, x, func, bias=bias, scale=scale),
              rd, [out])

    def RECIP(out, a, eng=V):
        em.op(eng, lambda E, o=out, x=a: E.reciprocal(o, x), [a], [out])

    def MM(out, lhsT, rhs, start, stop):
        em.op(Tn, lambda E, o=out, l=lhsT, r=rhs: E.matmul(o, l, r, start=start, stop=stop),
              [lhsT, rhs], [out])

    def TR(out, in_):
        em.op(Tn, lambda E, o=out, i=in_: E.transpose(o, i, ident[:, :]),
              [in_, ident], [out])

    def MEMSET(buf, val=0.0, eng=V):
        em.op(eng, lambda E, o=buf: E.memset(o, val), [], [buf])

    ft_ = lambda buf, j, w: buf[:, j * w:(j + 1) * w]

    dma(ident[:, :], ident_i)
    dma(mask64[0:64, :], mask64_i)
    dma(rowtmp[0:1, 0:64], iota64_i)
    dma(ones_row[0:1, :], ones_row_i)
    dma(meanw[:, 0:1], meanw_i)
    dma(vecs[:, :], vec_cols)
    dma(B8[0:1, 0:D], td_row)
    for e in (V, Sc, Tn):
        em.wait_now(e, SD)

    c_td = vecs[:, 0:NFT]
    c_tmk = vecs[:, NFT:2 * NFT]
    c_tmv = vecs[:, 2 * NFT:3 * NFT]
    c_tmr = vecs[:, 3 * NFT:4 * NFT]
    c_tmg = vecs[:, 4 * NFT:5 * NFT]
    c_gln = vecs[:, 5 * NFT:6 * NFT]
    c_bln = vecs[:, 6 * NFT:7 * NFT]
    tdrow = B8[0:1, 0:D]
    erow = B8[0:1, D:2 * D]
    wLrow = B8[0:1, 2 * D:3 * D]

    MEMSET(cEPS[:, 0:1], EPS)
    ACT(e_c[:, 0:NFT], c_td, AF.Exp)
    ACT(e_c[:, NFT:2 * NFT], e_c[:, 0:NFT], AF.Exp, scale=float(-L))
    ACT(erow, tdrow, AF.Exp)
    ACT(wLrow, erow, AF.Exp, scale=float(-L))
    em.wait_now(Tn, SS)
    MM(pb[0][:, 0:64], ones_row[0:1, :], rowtmp[0:1, 0:64], True, True)
    em.wait_now(V, ST)
    CP(iota_bc[:, :], pb[0][:, 0:64])
    em.wait_now(V, SS)
    for j in range(NFT):
        TS(ft_(SE, j, 64), iota_bc[:, :], e_c[:, j:j + 1], AL.mult)
        TS(ft_(M_stL, j, 64), iota_bc[:, :], 0.0, AL.mult, e_c[:, NFT + j:NFT + j + 1], AL.add)
    em.wait_now(Sc, SV)
    ACT(M_km[:, :], SE[:, :], AF.Exp)
    ACT(M_out[:, :], SE[:, :], AF.Exp, scale=-1.0)
    em.wait_now(Tn, SS)
    MM(pb[1][:, 0:512], ones_row[0:1, :], wLrow[0:1, 0:512], True, True)
    MM(pb[2][:, 0:512], ones_row[0:1, :], wLrow[0:1, 512:1024], True, True)
    em.wait_now(V, ST)
    CP(M_k2L[:, 0:512], pb[1][:, 0:512])
    CP(M_k2L[:, 512:1024], pb[2][:, 0:512])
    MEMSET(St[:, :], 0.0)
    MEMSET(AMAX[:, :], 0.0)
    barrier()

    xt = B_W[:, 0:D]
    xm = B_W[:, D:2 * D]
    sqt = B_W[:, 2 * D:3 * D]
    xnt = B_W[:, 3 * D:4 * D]
    stage = B9[:, 0:D]

    def ln_tiles(src_ap, n_tiles, col0):
        for ti in range(n_tiles):
            dma(xt, src_ap[ti * 128:(ti + 1) * 128, :])
            em.wait_now(V, SD)
            RED(stats[:, 0:1], xt, AL.add)
            TS(stats[:, 1:2], stats[:, 0:1], 1.0 / D, AL.mult)
            TS(xm, xt, stats[:, 1:2], AL.subtract)
            em.wait_now(Sc, SV)
            ACT(sqt, xm, AF.Square)
            em.wait_now(V, SS)
            RED(stats[:, 2:3], sqt, AL.add)
            em.wait_now(Sc, SV)
            ACT(stats[:, 3:4], stats[:, 2:3], AF.Sqrt, bias=cEPS[:, 0:1], scale=1.0 / D)
            em.wait_now(V, SS)
            RECIP(stats[:, 4:5], stats[:, 3:4])
            TS(xnt, xm, stats[:, 4:5], AL.mult)
            RED(stats[:, 5:6], xnt, AL.max, absval=True)
            TT(AMAX[:, 0:1], AMAX[:, 0:1], stats[:, 5:6], AL.max)
            em.wait_now(Tn, SV)
            for j in range(NFT):
                TR(pb[j // 4][:, (j % 4) * 128:(j % 4 + 1) * 128], xnt[:, j * 128:(j + 1) * 128])
            em.wait_now(V, ST)
            CP(stage[:, 0:512], pb[0][:, 0:512])
            CP(stage[:, 512:1024], pb[1][:, 0:512])
            em.wait_now(Sy, SV)
            bc = (col0 + ti) * 128
            dma(sp_xn[:, bc:bc + 128].rearrange("(j f) c -> f j c", f=128),
                stage[:, :].rearrange("p (j c) -> p j c", c=128))
            em.wait_now(V, SD)
            em.wait_now(Tn, SD)
            if ti % 6 == 5:
                barrier()

    ln_tiles(x_warm, cfg.n_warm_tiles, 0)
    ln_tiles(x_own, cfg.n_own_tiles, cfg.n_warm_tiles)
    barrier()

    def cross_part_max(src_col_ap, dst_row_off):
        TR(pb[0][0:1, 0:128], src_col_ap)
        em.wait_now(V, ST)
        CP(rowtmp[0:1, 64:192], pb[0][0:1, 0:128])
        RED(rowtmp[0:1, dst_row_off:dst_row_off + 1], rowtmp[0:1, 64:192], AL.max)
        em.wait_now(Tn, SV)

    def allreduce_scales(idx, src_off, nvals, sc_cols):
        em.wait_now(Sy, SV)
        dma(cc_in[idx][0:1, 0:nvals], rowtmp[0:1, src_off:src_off + nvals])
        em.wait_now(Gp, SD)
        em.op(Gp, lambda E: E.collective_compute(
            "AllReduce", AL.max,
            replica_groups=[list(range(cfg.n_cores))],
            ins=[cc_in[idx][0:1, 0:nvals]],
            outs=[cc_out[idx][0:1, 0:nvals]]), inc=SG)
        em.wait_now(Sy, SG)
        dma(rowtmp[0:1, 200:200 + nvals], cc_out[idx][0:1, 0:nvals])
        em.wait_now(V, SD)
        for c in range(nvals):
            TS(rowtmp[0:1, 210 + c:211 + c], rowtmp[0:1, 200 + c:201 + c],
               1.0 / QMAX, AL.mult, 1e-12, AL.max)
        em.wait_now(Tn, SV)
        for c in range(nvals):
            MM(pb[0][:, c:c + 1], ones_row[0:1, :], rowtmp[0:1, 210 + c:211 + c], True, True)
        em.wait_now(V, ST)
        for c in range(nvals):
            col = sc_cols[c]
            CP(SCAL[:, col:col + 1], pb[0][:, c:c + 1])
            RECIP(SCAL[:, col + 1:col + 2], SCAL[:, col:col + 1])
            TS(SCAL[:, col + 2:col + 3], SCAL[:, col:col + 1], -1.0, AL.mult)

    cross_part_max(AMAX[:, 0:1], 0)
    allreduce_scales(0, 0, 1, [0])
    barrier()

    xq_bufs = [B3, B4, B5, B6]
    def xq_t(j):
        return xq_bufs[j // 2][:, (j % 2) * TOWN:(j % 2 + 1) * TOWN]

    own0 = NWP + 1
    warm0 = 1
    gp_r0f = B7[:, 0:TOWN]
    tz = B0[:, 0:TOWN]
    tg = B0[:, TOWN:2 * TOWN]
    ti16 = B1[:, 0:(TOWN + 1) // 2].bitcast(I16)[:, 0:TOWN]

    def quant4(out, x, scol, n, eng=V, z=None, g=None, i16=None):
        s = SCAL[:, scol:scol + 1]
        rs = SCAL[:, scol + 1:scol + 2]
        ns = SCAL[:, scol + 2:scol + 3]
        z = z if z is not None else tz
        g = g if g is not None else tg
        i16 = i16 if i16 is not None else ti16
        if eng == Gp:
            rs_b = rs.broadcast_to([128, n])
            s_b = s.broadcast_to([128, n])
            r0f = gp_r0f[:, 0:n]
            TT(z[:, 0:n], x, rs_b, AL.mult, eng=eng)
            TS(i16[:, 0:n], z[:, 0:n], 1.0, AL.mult, eng=eng)
            TS(r0f, i16[:, 0:n], 1.0, AL.mult, eng=eng)
            TT(g[:, 0:n], z[:, 0:n], r0f, AL.subtract, eng=eng)
            TS(z[:, 0:n], g[:, 0:n], 0.5, AL.is_gt, eng=eng)
            TT(r0f, z[:, 0:n], r0f, AL.add, eng=eng)
            TS(g[:, 0:n], g[:, 0:n], -0.5, AL.is_ge, eng=eng)
            TT(g[:, 0:n], g[:, 0:n], r0f, AL.add, eng=eng)
            TS(z[:, 0:n], g[:, 0:n], -1.0, AL.add, eng=eng)
            TT(out, z[:, 0:n], s_b, AL.mult, eng=eng)
            return
        TS(z[:, 0:n], x, rs, AL.mult, eng=eng)
        TS(i16[:, 0:n], x, rs, AL.mult, eng=eng)
        TT(g[:, 0:n], z[:, 0:n], i16[:, 0:n], AL.subtract, eng=eng)
        STT(z[:, 0:n], g[:, 0:n], 0.5, i16[:, 0:n], AL.is_gt, AL.add, eng=eng)
        STT(g[:, 0:n], g[:, 0:n], -0.5, z[:, 0:n], AL.is_ge, AL.add, eng=eng)
        TS(out, g[:, 0:n], s, AL.mult, ns, AL.add, eng=eng)

    for bidx, buf in enumerate(xq_bufs):
        dma(buf[:, 0:2 * TOWN].rearrange("p (j t) -> p j t", t=TOWN),
            sp_xn[bidx * 256:(bidx + 1) * 256, own0:own0 + TOWN]
            .rearrange("(j f) c -> f j c", f=128))
    em.wait_now(V, SD)
    em.wait_now(Gp, SD)
    for j in range(NFT):
        eng_ = V if j < 5 else Gp
        quant4(xq_t(j), xq_t(j), 0, TOWN, eng=eng_,
               z=(tz if eng_ == V else B1[:, TOWN:2 * TOWN]),
               g=(tg if eng_ == V else B2[:, 0:TOWN]),
               i16=(ti16 if eng_ == V else B2[:, TOWN:TOWN + TOWN // 2].bitcast(I16)[:, 0:TOWN]))
    barrier()

    xn_g = B0
    xmix = B1
    projg = B2

    def load_weightT(wt_ap):
        dma(B_W[:, 0:NFT * D].rearrange("p (j f) -> p j f", f=D),
            wt_ap.rearrange("(j r) f -> r j f", r=128))

    def proj_group(src_col0, n, mix_c, dst, dst0, sig):
        dma(xn_g[:, 0:NFT * (TB + 1)].rearrange("p (j t) -> p j t", t=TB + 1)[:, :, 0:n + 1],
            sp_xn[:, src_col0 - 1:src_col0 + n].rearrange("(j f) c -> f j c", f=128))
        em.wait_now(V, SD)
        for j in range(NFT):
            xg = ft_(xn_g, j, TB + 1)
            mx = ft_(xmix, j, TB)[:, 0:n]
            TT(mx, xg[:, 0:n], xg[:, 1:n + 1], AL.subtract)
            STT(mx, mx, mix_c[:, j:j + 1], xg[:, 1:n + 1], AL.mult, AL.add)
        em.wait_now(Tn, SV)
        for ot in range(NFT):
            bank = pb[ot % 2]
            if ot >= 2:
                em.wait_now(Tn, SS if sig else SV)
            for j in range(NFT):
                MM(bank[:, 0:n], ft_(B_W, j, D)[:, ot * 128:(ot + 1) * 128],
                   ft_(xmix, j, TB)[:, 0:n], j == 0, j == NFT - 1)
            po = ft_(projg, ot, TB)[:, 0:n]
            if sig:
                em.wait_now(Sc, ST)
                ACT(po, bank[:, 0:n], AF.Sigmoid)
            else:
                em.wait_now(V, ST)
                CP(po, bank[:, 0:n])
        em.wait_now(Sy, SS if sig else SV)
        dma(dst[:, dst0:dst0 + n].rearrange("(j f) c -> f j c", f=128),
            projg[:, 0:NFT * TB].rearrange("p (j t) -> p j t", t=TB)[:, :, 0:n])
        em.wait_now(Tn, SD)
        em.wait_now(V, SD) if not sig else em.wait_now(Sc, SD)

    for (wn, spill, mix_c, sig) in (("WkT", sp_k, c_tmk, False), ("WvT", sp_v, c_tmv, False),
                                    ("WrT", sp_r, c_tmr, False), ("WgT", sp_g, c_tmg, True)):
        load_weightT(WT[wn])
        em.wait_now(Tn, SD)
        kv = wn in ("WkT", "WvT")
        if kv and TWARM > 0:
            for gc in range(0, TWARM, TB):
                proj_group(warm0 + gc, min(TB, TWARM - gc), mix_c, spill, gc, sig)
        for gc in range(0, TOWN, TB):
            proj_group(own0 + gc, min(TB, TOWN - gc), mix_c, spill,
                       (TWARM + gc) if kv else gc, sig)
        barrier()

    assert NFT * TB >= 2 * TOWN, "buffer layout assumes TB*8 >= 2*TOWN"
    tu16 = B1[:, 0:(TOWN + 1) // 2].bitcast(U16)[:, 0:TOWN]



    wnat = B1
    wdqT = B7
    ustg4 = B2[:, 0:4 * FTB]
    wsc = B2[:, 4 * FTB:4 * FTB + 224]
    wz = B0[:, 0:FF]
    wgs = B8[:, 0:FF]
    wi16 = B9[:, 0:FF // 2].bitcast(I16)[:, 0:FF]

    def dequant_tile(wtile, nblk):
        bm = wsc[:, 0:nblk]
        sw = wsc[:, 96:96 + nblk]
        em.op(V, lambda E, o=bm.unsqueeze(2), x=wtile.rearrange("p (b s) -> p b s", s=32):
              E.tensor_reduce(o, x, axis=AX.X, op=AL.max, apply_absolute_value=True),
              [wtile], [wsc])
        TS(sw, bm, 1.0 / 7.0, AL.mult, 1e-12, AL.max)
        rw = bm
        RECIP(rw, sw)
        rwb = rw.unsqueeze(2).broadcast_to([128, nblk, 32])
        swb = sw.unsqueeze(2).broadcast_to([128, nblk, 32])
        nw = nblk * 32
        wv = wtile.rearrange("p (b s) -> p b s", s=32)
        zv = wz[:, 0:nw].rearrange("p (b s) -> p b s", s=32)
        i16v = wi16[:, 0:nw].rearrange("p (b s) -> p b s", s=32)
        em.op(V, lambda E, o=zv, a=wv, b=rwb: E.tensor_tensor(o, a, b, op=AL.mult),
              [wtile, wsc], [wz])
        em.op(V, lambda E, o=i16v, a=wv, b=rwb: E.tensor_tensor(o, a, b, op=AL.mult),
              [wtile, wsc], [wi16])
        TT(wgs[:, 0:nw], wz[:, 0:nw], wi16[:, 0:nw], AL.subtract)
        STT(wz[:, 0:nw], wgs[:, 0:nw], 0.5, wi16[:, 0:nw], AL.is_gt, AL.add)
        STT(wgs[:, 0:nw], wgs[:, 0:nw], -0.5, wz[:, 0:nw], AL.is_ge, AL.add)
        TS(wz[:, 0:nw], wgs[:, 0:nw], 1.0, AL.mult, -1.0, AL.add)
        ov = wtile.rearrange("p (b s) -> p b s", s=32)
        yv = wz[:, 0:nw].rearrange("p (b s) -> p b s", s=32)
        em.op(V, lambda E, o=ov, a=yv, b=swb: E.tensor_tensor(o, a, b, op=AL.mult),
              [wz, wsc], [wtile])

    NOS = NIT // 4

    def ffn_matmul_weight(w_in, spill, amax_col):
        for os_ in range(NOS):
            em.wait_now(Sy, ST)
            dma(wnat[:, 0:4 * D].rearrange("p (q f) -> p q f", f=D),
                w_in[os_ * 512:(os_ + 1) * 512, :].rearrange("(q r) f -> r q f", r=128))
            em.wait_now(V, SD)
            for q in range(4):
                dequant_tile(ft_(wnat, q, D), 32)
                em.wait_now(Tn, SV)
                for j in range(NFT):
                    TR(pb[2 + j // 4][:, (j % 4) * 128:(j % 4 + 1) * 128],
                       ft_(wnat, q, D)[:, j * 128:(j + 1) * 128])
                em.wait_now(V, ST)
                for half in range(2):
                    for j4 in range(4):
                        j = half * 4 + j4
                        CP(wdqT[:, j * 512 + q * 128:j * 512 + (q + 1) * 128],
                           pb[2 + half][:, j4 * 128:(j4 + 1) * 128])
                em.wait_now(Tn, SV)
            for tb in range(NFTB):
                em.wait_now(V, SD)
                for q in range(4):
                    bank = pb[q % 2]
                    if q >= 2 or tb > 0:
                        em.wait_now(Tn, SV)
                    for j in range(NFT):
                        MM(bank[:, 0:FTB], wdqT[:, j * 512 + q * 128:j * 512 + (q + 1) * 128],
                           xq_t(j)[:, tb * FTB:(tb + 1) * FTB], j == 0, j == NFT - 1)
                    em.wait_now(V, ST)
                    CP(ustg4[:, q * FTB:(q + 1) * FTB], bank[:, 0:FTB])
                    RED(stats[:, 6:7], ustg4[:, q * FTB:(q + 1) * FTB], AL.max, absval=True)
                    TT(AMAX[:, amax_col:amax_col + 1], AMAX[:, amax_col:amax_col + 1],
                       stats[:, 6:7], AL.max)
                em.wait_now(Sy, SV)
                dma(spill[os_ * 512:(os_ + 1) * 512, tb * FTB:(tb + 1) * FTB]
                    .rearrange("(q r) t -> r q t", r=128),
                    ustg4[:, 0:4 * FTB].rearrange("p (q t) -> p q t", t=FTB))
            barrier()

    ffn_matmul_weight(Wup, sp_upr, 1)
    ffn_matmul_weight(Wgate, sp_gater, 2)

    for mt in range(NFT):
        dma(wnat[:, 0:FF], Wdown[mt * 128:(mt + 1) * 128, :])
        em.wait_now(V, SD)
        dequant_tile(wnat[:, 0:FF], 96)
        em.wait_now(Tn, SV)
        for it in range(NIT):
            TR(pb[2 + (it % 8) // 4][:, (it % 4) * 128:(it % 4 + 1) * 128],
               wnat[:, it * 128:(it + 1) * 128])
            if it % 8 == 7:
                em.wait_now(V, ST)
                for b in range(2):
                    CP(stage[:, b * 512:(b + 1) * 512], pb[2 + b][:, 0:512])
                em.wait_now(Sy, SV)
                dma(sp_wd[(it - 7) * 128:(it + 1) * 128, mt * 128:(mt + 1) * 128]
                    .rearrange("(k r) c -> r k c", r=128),
                    stage[:, 0:1024].rearrange("p (k c) -> p k c", c=128))
                em.wait_now(Tn, SD)
                em.wait_now(V, SD)
        if mt % 3 == 2:
            barrier()
    barrier()

    cross_part_max(AMAX[:, 1:2], 0)
    cross_part_max(AMAX[:, 2:3], 1)
    allreduce_scales(1, 0, 2, [3, 6])
    barrier()

    load_weightT(WT["WoT"])
    MEMSET(St[:, :], 0.0)
    barrier()

    MW = 256
    k_g = B3[:, 0:NFT * MW]
    v_g = B3[:, NFT * MW:2 * NFT * MW]
    r_g = B4[:, 0:NFT * MW]
    gg_g = B4[:, NFT * MW:2 * NFT * MW]
    attn_g = B2[:, 0:NFT * MW]
    y_g = B2[:, NFT * MW:2 * NFT * MW]
    ym_g = B0[:, 0:NFT * MW]
    s_g = B0[:, NFT * MW:2 * NFT * MW]
    Atl = B1[:, 0:1024]
    kt_sub = B8[:, 3 * D:4 * D]
    ktm = B8[:, 0:D]
    vtm = B8[:, D:2 * D]
    k2tm = B8[:, 2 * D:3 * D]
    mf = lambda buf, j: buf[:, j * MW:(j + 1) * MW]

    def mixer_load(kvc0, n, own, ownc0):
        dma(k_g.rearrange("p (j t) -> p j t", t=MW)[:, :, 0:n],
            sp_k[:, kvc0:kvc0 + n].rearrange("(j f) c -> f j c", f=128))
        dma(v_g.rearrange("p (j t) -> p j t", t=MW)[:, :, 0:n],
            sp_v[:, kvc0:kvc0 + n].rearrange("(j f) c -> f j c", f=128))
        if own:
            dma(r_g.rearrange("p (j t) -> p j t", t=MW)[:, :, 0:n],
                sp_r[:, ownc0:ownc0 + n].rearrange("(j f) c -> f j c", f=128))
            dma(gg_g.rearrange("p (j t) -> p j t", t=MW)[:, :, 0:n],
                sp_g[:, ownc0:ownc0 + n].rearrange("(j f) c -> f j c", f=128))
        em.wait_now(V, SD)
        em.wait_now(Tn, SD)

    def mixer_subgroup(sg, own):
        t0g = sg * 128
        for j in range(NFT):
            kv_ = mf(k_g, j)[:, t0g:t0g + 128].rearrange("p (c s) -> p c s", s=L)
            mv_ = ft_(M_km, j, L).unsqueeze(1).broadcast_to([128, 2, L])
            ov_ = kt_sub[:, j * 128:(j + 1) * 128].rearrange("p (c s) -> p c s", s=L)
            em.op(V, lambda E, o=ov_, a=kv_, b=mv_: E.tensor_tensor(o, a, b, op=AL.mult),
                  [mf(k_g, j)], [kt_sub])
        em.wait_now(Tn, SV)
        for j in range(NFT):
            TR(pb[2 + j // 4][:, (j % 4) * 128:(j % 4 + 1) * 128],
               kt_sub[:, j * 128:(j + 1) * 128])
        em.wait_now(V, ST)
        CP(ktm[:, 0:512], pb[2][:, 0:512])
        CP(ktm[:, 512:1024], pb[3][:, 0:512])
        em.wait_now(Tn, SV)
        for j in range(NFT):
            TR(pb[2 + j // 4][:, (j % 4) * 128:(j % 4 + 1) * 128],
               mf(v_g, j)[:, t0g:t0g + 128])
        em.wait_now(V, ST)
        CP(vtm[:, 0:512], pb[2][:, 0:512])
        CP(vtm[:, 512:1024], pb[3][:, 0:512])
        TT(k2tm[:, :], ktm[:, :], M_k2L[:, :], AL.mult)
        em.wait_now(Tn, SV)
        for ci in range(2):
            cp_ = ci * 64
            tok0 = t0g + ci * 64
            if own:
                for j in range(NFT):
                    for hp in range(2):
                        MM(pb[4 + j // 4][cp_:cp_ + 64,
                                          (j % 4) * 128 + hp * 64:(j % 4) * 128 + hp * 64 + 64],
                           mf(v_g, j)[hp * 64:hp * 64 + 64, tok0:tok0 + 64],
                           mf(r_g, j)[hp * 64:hp * 64 + 64, tok0:tok0 + 64],
                           True, True)
                em.wait_now(V, ST)
                for b in range(2):
                    pv = pb[4 + b][cp_:cp_ + 64, 0:512].rearrange("p (k s) -> p k s", s=64)
                    mk = mask64[0:64, :].unsqueeze(1).broadcast_to([64, 8, 64])
                    ov = Atl[cp_:cp_ + 64, b * 512:(b + 1) * 512].rearrange("p (k s) -> p k s", s=64)
                    em.op(V, lambda E, o=ov, a=pv, m=mk: E.tensor_tensor(o, a, m, op=AL.mult),
                          [pb[4 + b]], [Atl])
                em.wait_now(Tn, SV)
                for j in range(NFT):
                    for hp in range(2):
                        h = 2 * j + hp
                        MM(pb[6][hp * 64:hp * 64 + 64, j * 64:j * 64 + 64],
                           ktm[cp_:cp_ + 64, h * 64:h * 64 + 64],
                           Atl[cp_:cp_ + 64, j * 128 + hp * 64:j * 128 + hp * 64 + 64],
                           True, False)
                        MM(pb[6][hp * 64:hp * 64 + 64, j * 64:j * 64 + 64],
                           St[hp * 64:hp * 64 + 64, j * 64:j * 64 + 64],
                           mf(r_g, j)[hp * 64:hp * 64 + 64, tok0:tok0 + 64],
                           False, True)
                em.wait_now(V, ST)
                for j in range(NFT):
                    att = mf(attn_g, j)
                    for hp in range(2):
                        TT(att[hp * 64:hp * 64 + 64, tok0:tok0 + 64],
                           pb[6][hp * 64:hp * 64 + 64, j * 64:j * 64 + 64],
                           ft_(M_out, j, L)[hp * 64:hp * 64 + 64, :], AL.mult)
            em.wait_now(Tn, SV)
            for j in range(NFT):
                for hp in range(2):
                    h = 2 * j + hp
                    MM(pb[7][hp * 64:hp * 64 + 64, j * 64:j * 64 + 64],
                       vtm[cp_:cp_ + 64, h * 64:h * 64 + 64],
                       k2tm[cp_:cp_ + 64, h * 64:h * 64 + 64],
                       True, True)
            em.wait_now(V, ST)
            TT(St[:, :], St[:, :], M_stL[:, :], AL.mult)
            TT(St[:, :], St[:, :], pb[7][:, :], AL.add)
            em.wait_now(Tn, SV)

    def p3c_tail(gc, n):
        for j in range(NFT):
            TT(mf(attn_g, j)[:, 0:n], mf(attn_g, j)[:, 0:n],
               mf(gg_g, j)[:, 0:n], AL.mult)
        em.wait_now(Tn, SV)
        for ot in range(NFT):
            bank = pb[ot % 2]
            if ot >= 2:
                em.wait_now(Tn, SV)
            for j in range(NFT):
                MM(bank[:, 0:n], ft_(B_W, j, D)[:, ot * 128:(ot + 1) * 128],
                   mf(attn_g, j)[:, 0:n], j == 0, j == NFT - 1)
            em.wait_now(V, ST)
            CP(mf(y_g, ot)[:, 0:n], bank[:, 0:n])
        em.wait_now(Tn, SV)
        for j in range(NFT):
            MM(pb[2][0:1, 0:n], meanw[:, 0:1], mf(y_g, j)[:, 0:n], j == 0, j == NFT - 1)
        em.wait_now(V, ST)
        CP(rowtmp[0:1, 500:500 + n], pb[2][0:1, 0:n])
        em.wait_now(Tn, SV)
        MM(pb[3][:, 0:n], ones_row[0:1, :], rowtmp[0:1, 500:500 + n], True, True)
        em.wait_now(V, ST)
        for j in range(NFT):
            TT(mf(ym_g, j)[:, 0:n], mf(y_g, j)[:, 0:n], pb[3][:, 0:n], AL.subtract)
        em.wait_now(Sc, SV)
        for j in range(NFT):
            ACT(mf(s_g, j)[:, 0:n], mf(ym_g, j)[:, 0:n], AF.Square)
        em.wait_now(Tn, SS)
        for j in range(NFT):
            MM(pb[2][0:1, 0:n], meanw[:, 0:1], mf(s_g, j)[:, 0:n], j == 0, j == NFT - 1)
        em.wait_now(Sc, ST)
        ACT(rowtmp[0:1, 300:300 + n], pb[2][0:1, 0:n], AF.Sqrt, bias=cEPS[0:1, 0:1])
        em.wait_now(V, SS)
        RECIP(rowtmp[0:1, 300:300 + n], rowtmp[0:1, 300:300 + n])
        em.wait_now(Tn, SV)
        MM(pb[3][:, 0:n], ones_row[0:1, :], rowtmp[0:1, 300:300 + n], True, True)
        em.wait_now(V, ST)
        for j in range(NFT):
            sj = mf(s_g, j)[:, 0:n]
            TT(sj, mf(ym_g, j)[:, 0:n], pb[3][:, 0:n], AL.mult)
            TS(sj, sj, c_gln[:, j:j + 1], AL.mult, c_bln[:, j:j + 1], AL.add)
        em.wait_now(Sy, SV)
        dma(sp_s[:, gc:gc + n].rearrange("(j f) c -> f j c", f=128),
            s_g.rearrange("p (j t) -> p j t", t=MW)[:, :, 0:n])

    f_upt = B5[:, 0:TOWN]
    f_gat = B5[:, TOWN:2 * TOWN]
    f_sig = B6[:, 0:TOWN]
    f_gst = B6[:, TOWN:2 * TOWN]
    f_z = B7[:, 0:TOWN]
    f_g = B7[:, TOWN:2 * TOWN]
    f_i16 = B9[:, 0:TOWN // 2].bitcast(I16)[:, 0:TOWN]
    f_u16 = B9[:, TOWN // 2:TOWN].bitcast(U16)[:, 0:TOWN]
    f_gi16 = B1[:, 3 * 1024:3 * 1024 + TOWN // 2].bitcast(I16)[:, 0:TOWN]
    f_r0f = B1[:, 1024:1024 + TOWN]

    def gp_round_to_int(out_i16, z, g, r0f, n):
        TS(out_i16, z, 1.0, AL.mult, eng=Gp)
        TS(r0f, out_i16, 1.0, AL.mult, eng=Gp)
        TT(g, z, r0f, AL.subtract, eng=Gp)
        TS(z, g, 0.5, AL.is_gt, eng=Gp)
        TT(r0f, z, r0f, AL.add, eng=Gp)
        TS(g, g, -0.5, AL.is_ge, eng=Gp)
        TT(g, g, r0f, AL.add, eng=Gp)
        TS(z, g, -1.0, AL.add, eng=Gp)
        TS(out_i16, z, 1.0, AL.mult, eng=Gp)

    def v_round_to_int(out_i16, z, g, n):
        TS(out_i16, z, 1.0, AL.mult, eng=V)
        TT(g, z, out_i16, AL.subtract, eng=V)
        STT(z, g, 0.5, out_i16, AL.is_gt, AL.add, eng=V)
        STT(g, g, -0.5, z, AL.is_ge, AL.add, eng=V)
        TS(out_i16, g, 1.0, AL.mult, -1.0, AL.add, eng=V)

    def p5_itile(it, eng):
        r0_, r1_ = it * 128, it * 128 + 128
        em.wait_now(Sy, SV)
        em.wait_now(Sy, SG)
        em.wait_now(Sy, SS)
        em.wait_now(Sy, SD)
        dma(f_upt, sp_upr[r0_:r1_, :])
        dma(f_gat, sp_gater[r0_:r1_, :])
        em.wait_now(eng, SD)
        rs2 = SCAL[:, 4:5]
        rs3 = SCAL[:, 7:8]
        s3 = SCAL[:, 6:7]
        if eng == Gp:
            TT(f_z, f_upt, rs2.broadcast_to([128, TOWN]), AL.mult, eng=Gp)
            gp_round_to_int(f_i16, f_z, f_g, f_r0f, TOWN)
        else:
            TS(f_z, f_upt, rs2, AL.mult, eng=V)
            v_round_to_int(f_i16, f_z, f_g, TOWN)
        em.wait_now(Sy, esem[eng])
        dma(sp_upq[r0_:r1_, :], f_i16)
        if eng == Gp:
            TT(f_z, f_gat, rs3.broadcast_to([128, TOWN]), AL.mult, eng=Gp)
            gp_round_to_int(f_gi16, f_z, f_g, f_r0f, TOWN)
            TS(f_z, f_gi16, 1.0, AL.mult, eng=Gp)
            TT(f_gat, f_z, s3.broadcast_to([128, TOWN]), AL.mult, eng=Gp)
        else:
            TS(f_z, f_gat, rs3, AL.mult, eng=V)
            v_round_to_int(f_gi16, f_z, f_g, TOWN)
            TS(f_gat, f_gi16, s3, AL.mult, eng=V)
        em.wait_now(Sc, esem[eng])
        ACT(f_sig, f_gat, AF.Sigmoid)
        em.wait_now(eng, SS)
        if eng == Gp:
            TS(f_z, f_sig, 65536.0, AL.mult, eng=Gp)
            TS(f_z, f_z, 65535.0, AL.min, eng=Gp)
            TS(f_u16, f_z, 1.0, AL.mult, eng=Gp)
            TS(f_r0f, f_u16, 1.0, AL.mult, eng=Gp)
            TT(f_g, f_z, f_r0f, AL.subtract, eng=Gp)
            TS(f_z, f_g, 0.5, AL.is_gt, eng=Gp)
            TT(f_r0f, f_z, f_r0f, AL.add, eng=Gp)
            TS(f_g, f_g, -0.5, AL.is_ge, eng=Gp)
            TT(f_g, f_g, f_r0f, AL.add, eng=Gp)
            TS(f_g, f_g, -1.0, AL.add, eng=Gp)
            TS(f_sig, f_g, 1.0 / 65536.0, AL.mult, eng=Gp)
        else:
            TS(f_z, f_sig, 65536.0, AL.mult, 65535.0, AL.min, eng=V)
            TS(f_u16, f_z, 1.0, AL.mult, eng=V)
            TT(f_g, f_z, f_u16, AL.subtract, eng=V)
            STT(f_z, f_g, 0.5, f_u16, AL.is_gt, AL.add, eng=V)
            STT(f_g, f_g, -0.5, f_z, AL.is_ge, AL.add, eng=V)
            TS(f_sig, f_g, 1.0 / 65536.0, AL.mult, -1.0 / 65536.0, AL.add, eng=V)
        TT(f_gst, f_gat, f_sig, AL.mult, eng=eng)
        if eng == Gp:
            em.wait_now(V, SG)
        RED(stats[:, 7:8], f_gst, AL.max, absval=True, eng=V)
        TT(AMAX[:, 3:4], AMAX[:, 3:4], stats[:, 7:8], AL.max, eng=V)
        em.wait_now(Sy, esem[eng])
        dma(sp_gs[r0_:r1_, :], f_gst)

    def p6_itile(it, eng):
        r0_, r1_ = it * 128, it * 128 + 128
        em.wait_now(Sy, SV)
        em.wait_now(Sy, SG)
        em.wait_now(Sy, SD)
        dma(f_gst, sp_gs[r0_:r1_, :])
        dma(f_i16, sp_upq[r0_:r1_, :])
        em.wait_now(eng, SD)
        rs4 = SCAL[:, 10:11]
        s4 = SCAL[:, 9:10]
        s2 = SCAL[:, 3:4]
        if eng == Gp:
            TT(f_z, f_gst, rs4.broadcast_to([128, TOWN]), AL.mult, eng=Gp)
            gp_round_to_int(f_gi16, f_z, f_g, f_r0f, TOWN)
            TS(f_z, f_gi16, 1.0, AL.mult, eng=Gp)
            TT(f_gat, f_z, s4.broadcast_to([128, TOWN]), AL.mult, eng=Gp)
            TS(f_z, f_i16, 1.0, AL.mult, eng=Gp)
            TT(f_upt, f_z, s2.broadcast_to([128, TOWN]), AL.mult, eng=Gp)
        else:
            TS(f_z, f_gst, rs4, AL.mult, eng=V)
            v_round_to_int(f_gi16, f_z, f_g, TOWN)
            TS(f_gat, f_gi16, s4, AL.mult, eng=V)
            TS(f_upt, f_i16, s2, AL.mult, eng=V)
        TT(f_gst, f_gat, f_upt, AL.mult, eng=eng)
        if eng == Gp:
            em.wait_now(V, SG)
        RED(stats[:, 8:9], f_gst, AL.max, absval=True, eng=V)
        TT(AMAX[:, 4:5], AMAX[:, 4:5], stats[:, 8:9], AL.max, eng=V)
        em.wait_now(Sy, esem[eng])
        dma(sp_o[r0_:r1_, :], f_gst)

    def ar3():
        em.wait_now(Tn, SV)
        em.wait_now(Tn, SG)
        cross_part_max(AMAX[:, 3:4], 0)
        allreduce_scales(2, 0, 1, [9])

    def ar4():
        em.wait_now(Tn, SV)
        em.wait_now(Tn, SG)
        cross_part_max(AMAX[:, 4:5], 0)
        allreduce_scales(3, 0, 1, [12])

    a_chunks = []
    if TWARM > 0:
        for gc in range(0, TWARM, MW):
            n = min(MW, TWARM - gc)
            a_chunks.append(lambda gc=gc, n=n: mixer_load(gc, n, False, 0))
            for sg in range(n // 128):
                a_chunks.append(lambda sg=sg: mixer_subgroup(sg, False))
    for gc in range(0, TOWN, MW):
        n = min(MW, TOWN - gc)
        a_chunks.append(lambda gc=gc, n=n: mixer_load(TWARM + gc, n, True, gc))
        for sg in range(n // 128):
            a_chunks.append(lambda sg=sg: mixer_subgroup(sg, True))
        a_chunks.append(lambda gc=gc, n=n: p3c_tail(gc, n))

    def eng_of(i):
        return Gp if i % 3 == 2 else V
    b_chunks = ([lambda it=it: p5_itile(it, eng_of(it)) for it in range(NIT)]
                + [ar3]
                + [lambda it=it: p6_itile(it, eng_of(it)) for it in range(NIT)]
                + [ar4])

    na, nb = len(a_chunks), len(b_chunks)
    ia = ib = 0
    while ia < na or ib < nb:
        for _ in range(2):
            if ia < na:
                a_chunks[ia]()
                ia += 1
        take = 2 if ia < na else min(3, nb - ib)
        for _ in range(take):
            if ib < nb:
                b_chunks[ib]()
                ib += 1
        barrier()

    oq4 = B3[:, 0:4 * FTB]
    oq = [oq4[:, q * FTB:(q + 1) * FTB] for q in range(4)]
    wd = B7
    g2t = B5
    st_ = B6
    xrow = B0[:, 0:D]
    orow = B0[:, D:2 * D]
    prodst = B1

    for tb in range(NFTB):
        c0 = tb * FTB
        for isl in range(NOS):
            em.wait_now(Sy, ST)
            em.wait_now(V, ST)
            dma(oq4.rearrange("p (q t) -> p q t", t=FTB),
                sp_o[isl * 512:(isl + 1) * 512, c0:c0 + FTB]
                .rearrange("(q r) t -> r q t", r=128))
            dma(wd[:, 0:4 * D].rearrange("p (q f) -> p q f", f=D),
                sp_wd[isl * 512:(isl + 1) * 512, :].rearrange("(q r) f -> r q f", r=128))
            em.wait_now(V, SD)
            for q in range(4):
                quant4(oq[q], oq[q], 12, FTB, z=B6[:, 0:FTB], g=B6[:, FTB:2 * FTB],
                       i16=B5[:, 0:FTB // 2].bitcast(I16)[:, 0:FTB])
            em.wait_now(Tn, SV)
            for m in range(NFT):
                for q in range(4):
                    MM(pb[m][:, 0:FTB], ft_(wd, q, D)[:, m * 128:(m + 1) * 128],
                       oq[q], isl == 0 and q == 0, isl == NOS - 1 and q == 3)
            em.wait_now(V, ST)
            em.wait_now(Sy, ST)
        barrier()
        em.wait_now(Sc, ST)
        for m in range(NFT):
            ACT(ft_(g2t, m, FTB), pb[m][:, 0:FTB], AF.Silu)
        dma(st_[:, 0:NFT * FTB].rearrange("p (j t) -> p j t", t=FTB),
            sp_s[:, c0:c0 + FTB].rearrange("(j f) c -> f j c", f=128))
        em.wait_now(V, SD)
        em.wait_now(V, SS)
        for j in range(NFT):
            TT(ft_(g2t, j, FTB), ft_(st_, j, FTB), ft_(g2t, j, FTB), AL.mult)
        em.wait_now(Tn, SV)
        for tt in range(FTB // 128):
            for j in range(NFT):
                TR(pb[j // 4][:, (j % 4) * 128:(j % 4 + 1) * 128],
                   ft_(g2t, j, FTB)[:, tt * 128:(tt + 1) * 128])
            em.wait_now(V, ST)
            CP(prodst[:, 0:512], pb[0][:, 0:512])
            CP(prodst[:, 512:1024], pb[1][:, 0:512])
            row0 = c0 + tt * 128
            dma(xrow, x_own[1 + row0:1 + row0 + 128, :])
            em.wait_now(V, SD)
            TT(orow, prodst[:, 0:D], xrow, AL.add)
            em.wait_now(Sy, SV)
            dma(y_out[row0:row0 + 128, :], orow)
            em.wait_now(V, SD)
            em.wait_now(Tn, SD)

    if debug:
        dbg_specs = [("d_scal", SCAL[:, 0:32], [128, 32], None),
                     ("d_amax", AMAX[:, 0:8], [128, 8], None)]
        for nm, ap_, shp, _ in dbg_specs:
            od = dt(nm, shp, F32, kind="ExternalOutput").ap()
            dma(od, ap_)
        for nm, sp, shp in [("d_xn", sp_xn, [D, TC]), ("d_s", sp_s, [D, TOWN]),
                            ("d_upr", sp_upr, [FF, TOWN]), ("d_gater", sp_gater, [FF, TOWN]),
                            ("d_gs", sp_gs, [FF, TOWN]), ("d_o", sp_o, [FF, TOWN]),
                            ("d_k", sp_k, [D, TWARM + TOWN]), ("d_r", sp_r, [D, TOWN]),
                            ("d_g", sp_g, [D, TOWN])]:
            od = dt(nm, shp, F32, kind="ExternalOutput").ap()
            dma(od, sp)
    em.wait_now(Sy, SD)
    barrier()
    return nc, em


def finalize(nc, em):
    ctxs = []
    sems = {}
    esems = {}
    for name in em.counts:
        h = nc.semaphore(name)
        sems[name] = h.__enter__()
        ctxs.append(h)
    for e in ENGINES:
        h = nc.semaphore("ep_" + e)
        esems[e] = h.__enter__()
        ctxs.append(h)
    blk = nc.Block()
    block = blk.__enter__()
    em.replay(block, sems, esems, SEM_OWNER)
    blk.__exit__(None, None, None)
    for h in reversed(ctxs):
        h.__exit__(None, None, None)
    return nc


def make_core_inputs(x_full, params, core, n_cores, cfg: Cfg):
    TOWN, TWARM = cfg.TOWN, cfg.TWARM
    t0 = core * TOWN
    xo = np.zeros((cfg.NOP, D), np.float32)
    xo[1:1 + TOWN] = x_full[t0:t0 + TOWN]
    xo[0] = x_full[1] if core == 0 else x_full[t0 - 1]
    xw = np.zeros((cfg.NWP, D), np.float32)
    if core > 0 and TWARM > 0:
        xw[1:1 + TWARM] = x_full[t0 - TWARM:t0]
        xw[0] = x_full[t0 - TWARM - 1] if t0 - TWARM - 1 >= 0 else 0.0
    vc = np.stack([params[k].reshape(NFT, 128).T for k in
                   ("td", "tmk", "tmv", "tmr", "tmg", "g_ln", "b_ln")], 1)
    vc = vc.reshape(128, 7 * NFT, order="F") if False else np.concatenate(
        [params[k].reshape(NFT, 128).T for k in
         ("td", "tmk", "tmv", "tmr", "tmg", "g_ln", "b_ln")], axis=1)
    return {
        "x_own": xo, "x_warm": xw,
        "WrT": np.ascontiguousarray(params["Wr"].T),
        "WkT": np.ascontiguousarray(params["Wk"].T),
        "WvT": np.ascontiguousarray(params["Wv"].T),
        "WgT": np.ascontiguousarray(params["Wg"].T),
        "WoT": np.ascontiguousarray(params["Wo"].T),
        "Wup": params["w_up"], "Wgate": params["w_gate"], "Wdown": params["w_down"],
        "vec_cols": np.ascontiguousarray(vc, np.float32),
        "td_row": params["td"].reshape(1, D),
        "ident_c": np.eye(128, dtype=np.float32),
        "mask64_c": np.triu(np.ones((64, 64), np.float32)),
        "iota64_c": np.arange(64, dtype=np.float32).reshape(1, 64),
        "ones_row_c": np.ones((1, 128), np.float32),
        "meanw_c": np.full((128, 1), 1.0 / D, np.float32),
    }


from concourse.bass_utils import run_bass_kernel_spmd

_N_CORES = 8
_CFG = Cfg(TOWN=2048, TWARM=1024, TB=512, n_cores=_N_CORES)
_NC_CACHE = {}


def _get_program():
    if "nc" not in _NC_CACHE:
        nc, em = build(_CFG)
        finalize(nc, em)
        _NC_CACHE["nc"] = nc
    return _NC_CACHE["nc"]


def kernel(L0_attn_x, L0_attn_time_decay, L0_attn_time_mix_k, L0_attn_time_mix_v,
           L0_attn_time_mix_r, L0_attn_time_mix_g, L0_attn_W_r, L0_attn_W_k,
           L0_attn_W_v, L0_attn_W_g, L0_attn_W_o, L0_attn_ln_gamma,
           L0_attn_ln_beta, L0_ffn_w_gate, L0_ffn_w_up, L0_ffn_w_down):
    x_full = np.ascontiguousarray(np.asarray(L0_attn_x, np.float32).reshape(-1, D))
    params = {
        "td": np.asarray(L0_attn_time_decay, np.float32),
        "tmk": np.asarray(L0_attn_time_mix_k, np.float32),
        "tmv": np.asarray(L0_attn_time_mix_v, np.float32),
        "tmr": np.asarray(L0_attn_time_mix_r, np.float32),
        "tmg": np.asarray(L0_attn_time_mix_g, np.float32),
        "Wr": np.asarray(L0_attn_W_r, np.float32),
        "Wk": np.asarray(L0_attn_W_k, np.float32),
        "Wv": np.asarray(L0_attn_W_v, np.float32),
        "Wg": np.asarray(L0_attn_W_g, np.float32),
        "Wo": np.asarray(L0_attn_W_o, np.float32),
        "g_ln": np.asarray(L0_attn_ln_gamma, np.float32),
        "b_ln": np.asarray(L0_attn_ln_beta, np.float32),
        "w_gate": np.asarray(L0_ffn_w_gate, np.float32),
        "w_up": np.asarray(L0_ffn_w_up, np.float32),
        "w_down": np.asarray(L0_ffn_w_down, np.float32),
    }
    nc = _get_program()
    in_maps = [make_core_inputs(x_full, params, c, _N_CORES, _CFG)
               for c in range(_N_CORES)]
    res = run_bass_kernel_spmd(nc, in_maps, core_ids=list(range(_N_CORES)))
    out = np.concatenate([np.asarray(r["y"], np.float32) for r in res.results], 0)
    return out.reshape(1, _N_CORES * _CFG.TOWN, D)


# version 10
# speedup vs baseline: 1.0786x; 1.0786x over previous
import numpy as np
import concourse.bass as bass
import concourse.mybir as mybir
from concourse.alu_op_type import AluOpType as AL

F32 = mybir.dt.float32
I16 = mybir.dt.int16
U16 = mybir.dt.uint16
AF = mybir.ActivationFunctionType
AX = mybir.AxisListType

ENGINES = ("sync", "tensor", "vector", "scalar", "gpsimd")
SEM_OWNER = {"sd0": "sync", "sd1": "sync", "sd2": "sync", "sd3": "sync",
             "st": "tensor", "sv": "vector", "ss": "scalar", "sg": "gpsimd"}
DSEMS = ("sd0", "sd1", "sd2", "sd3")

D = 1024
NFT = 8
H = 16
DH = 64
FF = 3072
NIT = FF // 128
L = 64
QMAX = 32767.0
EPS = 1e-5


class Cfg:
    def __init__(self, TOWN=2048, TWARM=1024, TB=512, n_cores=8):
        assert TB == 512, "buffer layout assumes TB == 512"
        assert TOWN % TB == 0 and (TWARM % TB == 0 or TWARM == 0)
        self.TOWN, self.TWARM, self.TB, self.n_cores = TOWN, TWARM, TB, n_cores
        self.NOP = TOWN + 128
        self.NWP = TWARM + 128 if TWARM > 0 else 128
        self.TC = self.NWP + self.NOP
        self.n_own_tiles = self.NOP // 128
        self.n_warm_tiles = self.NWP // 128
        self.NTB = TOWN // 512 if TOWN % 512 == 0 else None
        assert TOWN % 512 == 0 or TOWN < 512
        self.FTB = 512 if TOWN >= 512 else TOWN
        self.n_ftb = TOWN // self.FTB


class Emitter:

    def __init__(self, nc):
        self.nc = nc
        self.items = {e: [] for e in ENGINES}
        self.counts = {}
        self.touched = {e: [] for e in ENGINES}
        self.unfenced = {e: 0 for e in ENGINES}
        self.nbar = 0
        self.maxcount = {}

    def new_sem(self, name):
        self.counts[name] = 0
        self.maxcount[name] = 0
        return name

    @staticmethod
    def _rng(ap):
        try:
            t = ap.tensor.name
            lo = int(ap.offset)
            sz = 1
            for st, ct in ap.ap:
                sz += abs(int(st)) * (int(ct) - 1)
            return (t, lo, lo + sz)
        except Exception:
            return (str(ap), 0, 1 << 60)

    def _hazard(self, eng, reads, writes):
        tl = self.touched[eng]
        def ov(a, b):
            return a[0] == b[0] and a[1] < b[2] and b[1] < a[2]
        for ap in reads:
            r = self._rng(ap)
            for (k, lo, hi, w) in tl:
                if w and ov(r, (k, lo, hi)):
                    return True
        for ap in writes:
            r = self._rng(ap)
            for (k, lo, hi, w) in tl:
                if ov(r, (k, lo, hi)):
                    return True
        return False

    def _note(self, eng, reads, writes):
        tl = self.touched[eng]
        for ap in reads:
            k, lo, hi = self._rng(ap)
            tl.append((k, lo, hi, False))
        for ap in writes:
            k, lo, hi = self._rng(ap)
            tl.append((k, lo, hi, True))

    def op(self, eng, fn, reads=(), writes=(), inc=None, inc_by=1):
        reads, writes = list(reads), list(writes)
        if eng != "sync" and self._hazard(eng, reads, writes):
            self.items[eng].append(("drain",))
            self.touched[eng] = []
        self._note(eng, reads, writes)
        if inc is not None:
            self.counts[inc] += inc_by
            self.maxcount[inc] = max(self.maxcount[inc], self.counts[inc])
            assert self.counts[inc] <= 240, (
                f"sem {inc} exceeds 240 within an epoch ({self.counts[inc]}); "
                f"insert a barrier (epoch {self.nbar})")
            self.items[eng].append(("op", fn, inc, inc_by))
        else:
            self.items[eng].append(("op", fn, None, 0))
            self.unfenced[eng] += 1

    def fence(self, eng, sem):
        if self.unfenced[eng] == 0:
            return
        self.unfenced[eng] = 0
        self.counts[sem] += 1
        self.maxcount[sem] = max(self.maxcount[sem], self.counts[sem])
        assert self.counts[sem] <= 240, f"sem {sem} epoch overflow"
        self.items[eng].append(("fenceinc", sem))

    def wait(self, eng, sem, val):
        if val > 0:
            self.items[eng].append(("wait", sem, val))

    def wait_now(self, eng, sem):
        if sem == "sd":
            for s in DSEMS:
                self.wait(eng, s, self.counts[s])
            return
        own = SEM_OWNER[sem]
        if self.unfenced[own] > 0:
            self.fence(own, sem)
        self.wait(eng, sem, self.counts[sem])

    def barrier(self):
        tok = dict(self.counts)
        k = self.nbar
        self.nbar += 1
        assert self.nbar <= 125, "too many barriers for 8-bit epoch sems"
        for e in ENGINES:
            self.items[e].append(("barrier", k, tok))
            self.touched[e] = []
            self.unfenced[e] = 0
        for s in self.counts:
            self.counts[s] = 0

    def replay(self, block, sems, esems, owner):
        em = self

        def mk(eng_name):
            def body(proxy):
                last = [None]
                last_inc = [False]

                def emit_fence(s):
                    proxy.drain()
                    proxy.nop().then_inc(sems[s], 1)
                    last[0] = None
                    last_inc[0] = False

                for it in em.items[eng_name]:
                    kind = it[0]
                    if kind == "drain":
                        proxy.drain()
                        last[0] = None
                        last_inc[0] = False
                    elif kind == "wait":
                        proxy.wait_ge(sems[it[1]], it[2])
                        last[0] = None
                        last_inc[0] = False
                    elif kind == "fenceinc":
                        emit_fence(it[1])
                    elif kind == "ewait":
                        proxy.wait_ge(esems[eng_name], it[1])
                        last[0] = None
                        last_inc[0] = False
                    elif kind == "barrier":
                        k, tok = it[1], it[2]
                        for s, v in tok.items():
                            if v > 0:
                                proxy.wait_ge(sems[s], v)
                        proxy.nop().then_inc(esems[eng_name], 1)
                        for f in ENGINES:
                            proxy.wait_ge(esems[f], 2 * k + 1)
                        for s, own in owner.items():
                            if own == eng_name:
                                proxy.sem_clear(sems[s])
                        proxy.drain()
                        proxy.nop().then_inc(esems[eng_name], 1)
                        for f in ENGINES:
                            proxy.wait_ge(esems[f], 2 * k + 2)
                        last[0] = None
                        last_inc[0] = False
                    else:
                        _, fn, b, c = it
                        instr = fn(proxy)
                        last[0] = instr
                        last_inc[0] = False
                        if b is not None and instr is not None:
                            instr.then_inc(sems[b], c)
            return body

        for e in ENGINES:
            getattr(block, e)(mk(e))


def build(cfg: Cfg, debug=False):
    nc = bass.Bass("TRN2", target_bir_lowering=False)
    TOWN, TWARM, TB = cfg.TOWN, cfg.TWARM, cfg.TB
    NOP, NWP, TC = cfg.NOP, cfg.NWP, cfg.TC
    FTB, NFTB = cfg.FTB, cfg.n_ftb

    dt = nc.dram_tensor
    x_own = dt("x_own", [NOP, D], F32, kind="ExternalInput").ap()
    x_warm = dt("x_warm", [NWP, D], F32, kind="ExternalInput").ap()
    WT = {n: dt(n, [D, D], F32, kind="ExternalInput").ap()
          for n in ("WrT", "WkT", "WvT", "WgT", "WoT")}
    Wup = dt("Wup", [FF, D], F32, kind="ExternalInput").ap()
    Wgate = dt("Wgate", [FF, D], F32, kind="ExternalInput").ap()
    Wdown = dt("Wdown", [D, FF], F32, kind="ExternalInput").ap()
    vec_cols = dt("vec_cols", [128, 7 * NFT], F32, kind="ExternalInput").ap()
    td_row = dt("td_row", [1, D], F32, kind="ExternalInput").ap()
    ident_i = dt("ident_c", [128, 128], F32, kind="ExternalInput").ap()
    mask64_i = dt("mask64_c", [64, 64], F32, kind="ExternalInput").ap()
    iota64_i = dt("iota64_c", [1, 64], F32, kind="ExternalInput").ap()
    ones_row_i = dt("ones_row_c", [1, 128], F32, kind="ExternalInput").ap()
    meanw_i = dt("meanw_c", [128, 1], F32, kind="ExternalInput").ap()
    y_out = dt("y", [TOWN, D], F32, kind="ExternalOutput").ap()

    din = lambda n, s, dtp=F32: dt(n, s, dtp, kind="Internal").ap()
    sp_xn = din("sp_xn", [D, TC])
    sp_k = din("sp_k", [D, TWARM + TOWN])
    sp_v = din("sp_v", [D, TWARM + TOWN])
    sp_r = din("sp_r", [D, TOWN])
    sp_g = din("sp_g", [D, TOWN])
    sp_s = din("sp_s", [D, TOWN])
    sp_upr = din("sp_upr", [FF, TOWN])
    sp_gater = din("sp_gater", [FF, TOWN])
    sp_upq = din("sp_upq", [FF, TOWN], I16)
    sp_gs = din("sp_gs", [FF, TOWN])
    sp_o = din("sp_o", [FF, TOWN])
    sp_wd = din("sp_wd", [FF, D])
    cc_in = [din(f"cc_in{i}", [1, 2]) for i in range(4)]
    cc_out = [din(f"cc_out{i}", [1, 2]) for i in range(4)]

    def alloc(name, cols, dtp=F32, parts=128):
        return nc.alloc_sbuf_tensor(name, [parts, cols], dtp).ap()

    ident = alloc("c_ident", 128)
    mask64 = alloc("c_mask", 64, parts=64)
    ones_row = alloc("c_ones", 128, parts=1)
    meanw = alloc("c_meanw", 1)
    vecs = alloc("c_vecs", 7 * NFT)

    e_c = alloc("c_e", 2 * NFT)
    iota_bc = alloc("c_iotabc", 64)
    SE = alloc("c_SE", 512)
    M_km = alloc("c_Mkm", 512)
    M_out = alloc("c_Mout", 512)
    M_stL = alloc("c_MstL", 512)
    M_k2L = alloc("c_Mk2L", D)
    SCAL = alloc("c_scal", 32)
    cEPS = alloc("c_eps", 1)
    AMAX = alloc("c_amax", 8)
    rowtmp = alloc("c_rowtmp", 1024, parts=1)
    St = alloc("St", 512)
    stats = alloc("p_stats", 16)

    B_W = alloc("b_w", NFT * D)
    B0 = alloc("b_0", NFT * (TB + 1))
    B1 = alloc("b_1", NFT * TB)
    B2 = alloc("b_2", NFT * TB)
    B3 = alloc("b_3", NFT * TB)
    B4 = alloc("b_4", NFT * TB)
    B5 = alloc("b_5", NFT * TB)
    B6 = alloc("b_6", NFT * TB)
    B7 = alloc("b_7", NFT * TB)
    B8 = alloc("b_8", 4 * D)
    B9 = alloc("b_9", 2 * D)

    pb = [nc.alloc_psum_tensor(f"pb{i}", [128, 512], F32).ap() for i in range(8)]

    em = Emitter(nc)
    for _s in DSEMS:
        em.new_sem(_s)
    ST = em.new_sem("st")
    SV = em.new_sem("sv")
    SS = em.new_sem("ss")
    SG = em.new_sem("sg")
    SD = "sd"

    V, Tn, Sc, Gp, Sy = "vector", "tensor", "scalar", "gpsimd", "sync"
    esem = {V: SV, Tn: ST, Sc: SS, Gp: SG, Sy: SD}

    _dma_i = [0]

    def dma(out, in_):
        s = DSEMS[_dma_i[0] % 4]
        _dma_i[0] += 1
        em.op(Sy, lambda E, o=out, i=in_: E.dma_start(out=o, in_=i), inc=s, inc_by=16)
        em.unfenced["sync"] = 0

    def barrier():
        em.barrier()

    def TT(out, a, b, op, eng=V):
        em.op(eng, lambda E, o=out, x=a, y=b: E.tensor_tensor(o, x, y, op=op),
              [a, b], [out])

    def TS(out, a, s1, op0, s2=None, op1=None, eng=V):
        rd = [a] + [s for s in (s1, s2) if isinstance(s, bass.AP)]
        if s2 is None:
            em.op(eng, lambda E, o=out, x=a: E.tensor_scalar(o, x, s1, None, op0=op0),
                  rd, [out])
        else:
            em.op(eng, lambda E, o=out, x=a: E.tensor_scalar(o, x, s1, s2, op0=op0, op1=op1),
                  rd, [out])

    def STT(out, in0, scal, in1, op0, op1, eng=V):
        rd = [in0, in1] + ([scal] if isinstance(scal, bass.AP) else [])
        em.op(eng, lambda E, o=out, a=in0, b=in1: E.scalar_tensor_tensor(
            o, a, scal, b, op0=op0, op1=op1), rd, [out])

    def RED(out, a, op, absval=False, eng=V):
        em.op(eng, lambda E, o=out, x=a: E.tensor_reduce(
            o, x, axis=AX.X, op=op, apply_absolute_value=absval),
            [a], [out])

    def CP(out, a, eng=V):
        em.op(eng, lambda E, o=out, x=a: E.tensor_copy(o, x), [a], [out])

    def ACT(out, a, func, bias=0.0, scale=1.0):
        rd = [a] + ([bias] if isinstance(bias, bass.AP) else [])
        em.op(Sc, lambda E, o=out, x=a: E.activation(o, x, func, bias=bias, scale=scale),
              rd, [out])

    def RECIP(out, a, eng=V):
        em.op(eng, lambda E, o=out, x=a: E.reciprocal(o, x), [a], [out])

    def MM(out, lhsT, rhs, start, stop):
        em.op(Tn, lambda E, o=out, l=lhsT, r=rhs: E.matmul(o, l, r, start=start, stop=stop),
              [lhsT, rhs], [out])

    def TR(out, in_):
        em.op(Tn, lambda E, o=out, i=in_: E.transpose(o, i, ident[:, :]),
              [in_, ident], [out])

    def MEMSET(buf, val=0.0, eng=V):
        em.op(eng, lambda E, o=buf: E.memset(o, val), [], [buf])

    ft_ = lambda buf, j, w: buf[:, j * w:(j + 1) * w]

    dma(ident[:, :], ident_i)
    dma(mask64[0:64, :], mask64_i)
    dma(rowtmp[0:1, 0:64], iota64_i)
    dma(ones_row[0:1, :], ones_row_i)
    dma(meanw[:, 0:1], meanw_i)
    dma(vecs[:, :], vec_cols)
    dma(B8[0:1, 0:D], td_row)
    for e in (V, Sc, Tn):
        em.wait_now(e, SD)

    c_td = vecs[:, 0:NFT]
    c_tmk = vecs[:, NFT:2 * NFT]
    c_tmv = vecs[:, 2 * NFT:3 * NFT]
    c_tmr = vecs[:, 3 * NFT:4 * NFT]
    c_tmg = vecs[:, 4 * NFT:5 * NFT]
    c_gln = vecs[:, 5 * NFT:6 * NFT]
    c_bln = vecs[:, 6 * NFT:7 * NFT]
    tdrow = B8[0:1, 0:D]
    erow = B8[0:1, D:2 * D]
    wLrow = B8[0:1, 2 * D:3 * D]

    MEMSET(cEPS[:, 0:1], EPS)
    ACT(e_c[:, 0:NFT], c_td, AF.Exp)
    ACT(e_c[:, NFT:2 * NFT], e_c[:, 0:NFT], AF.Exp, scale=float(-L))
    ACT(erow, tdrow, AF.Exp)
    ACT(wLrow, erow, AF.Exp, scale=float(-L))
    em.wait_now(Tn, SS)
    MM(pb[0][:, 0:64], ones_row[0:1, :], rowtmp[0:1, 0:64], True, True)
    em.wait_now(V, ST)
    CP(iota_bc[:, :], pb[0][:, 0:64])
    em.wait_now(V, SS)
    for j in range(NFT):
        TS(ft_(SE, j, 64), iota_bc[:, :], e_c[:, j:j + 1], AL.mult)
        TS(ft_(M_stL, j, 64), iota_bc[:, :], 0.0, AL.mult, e_c[:, NFT + j:NFT + j + 1], AL.add)
    em.wait_now(Sc, SV)
    ACT(M_km[:, :], SE[:, :], AF.Exp)
    ACT(M_out[:, :], SE[:, :], AF.Exp, scale=-1.0)
    em.wait_now(Tn, SS)
    MM(pb[1][:, 0:512], ones_row[0:1, :], wLrow[0:1, 0:512], True, True)
    MM(pb[2][:, 0:512], ones_row[0:1, :], wLrow[0:1, 512:1024], True, True)
    em.wait_now(V, ST)
    CP(M_k2L[:, 0:512], pb[1][:, 0:512])
    CP(M_k2L[:, 512:1024], pb[2][:, 0:512])
    MEMSET(St[:, :], 0.0)
    MEMSET(AMAX[:, :], 0.0)
    barrier()

    xt = B_W[:, 0:D]
    xm = B_W[:, D:2 * D]
    sqt = B_W[:, 2 * D:3 * D]
    xnt = B_W[:, 3 * D:4 * D]
    stage = B9[:, 0:D]

    def ln_tiles(src_ap, n_tiles, col0):
        for ti in range(n_tiles):
            dma(xt, src_ap[ti * 128:(ti + 1) * 128, :])
            em.wait_now(V, SD)
            RED(stats[:, 0:1], xt, AL.add)
            TS(stats[:, 1:2], stats[:, 0:1], 1.0 / D, AL.mult)
            TS(xm, xt, stats[:, 1:2], AL.subtract)
            em.wait_now(Sc, SV)
            ACT(sqt, xm, AF.Square)
            em.wait_now(V, SS)
            RED(stats[:, 2:3], sqt, AL.add)
            em.wait_now(Sc, SV)
            ACT(stats[:, 3:4], stats[:, 2:3], AF.Sqrt, bias=cEPS[:, 0:1], scale=1.0 / D)
            em.wait_now(V, SS)
            RECIP(stats[:, 4:5], stats[:, 3:4])
            TS(xnt, xm, stats[:, 4:5], AL.mult)
            RED(stats[:, 5:6], xnt, AL.max, absval=True)
            TT(AMAX[:, 0:1], AMAX[:, 0:1], stats[:, 5:6], AL.max)
            em.wait_now(Tn, SV)
            for j in range(NFT):
                TR(pb[j // 4][:, (j % 4) * 128:(j % 4 + 1) * 128], xnt[:, j * 128:(j + 1) * 128])
            em.wait_now(V, ST)
            CP(stage[:, 0:512], pb[0][:, 0:512])
            CP(stage[:, 512:1024], pb[1][:, 0:512])
            em.wait_now(Sy, SV)
            bc = (col0 + ti) * 128
            dma(sp_xn[:, bc:bc + 128].rearrange("(j f) c -> f j c", f=128),
                stage[:, :].rearrange("p (j c) -> p j c", c=128))
            em.wait_now(V, SD)
            em.wait_now(Tn, SD)
            if ti % 6 == 5:
                barrier()

    ln_tiles(x_warm, cfg.n_warm_tiles, 0)
    ln_tiles(x_own, cfg.n_own_tiles, cfg.n_warm_tiles)
    barrier()

    def cross_part_max(src_col_ap, dst_row_off):
        TR(pb[0][0:1, 0:128], src_col_ap)
        em.wait_now(V, ST)
        CP(rowtmp[0:1, 64:192], pb[0][0:1, 0:128])
        RED(rowtmp[0:1, dst_row_off:dst_row_off + 1], rowtmp[0:1, 64:192], AL.max)
        em.wait_now(Tn, SV)

    def allreduce_scales(idx, src_off, nvals, sc_cols):
        em.wait_now(Sy, SV)
        dma(cc_in[idx][0:1, 0:nvals], rowtmp[0:1, src_off:src_off + nvals])
        em.wait_now(Gp, SD)
        em.op(Gp, lambda E: E.collective_compute(
            "AllReduce", AL.max,
            replica_groups=[list(range(cfg.n_cores))],
            ins=[cc_in[idx][0:1, 0:nvals]],
            outs=[cc_out[idx][0:1, 0:nvals]]), inc=SG)
        em.wait_now(Sy, SG)
        dma(rowtmp[0:1, 200:200 + nvals], cc_out[idx][0:1, 0:nvals])
        em.wait_now(V, SD)
        for c in range(nvals):
            TS(rowtmp[0:1, 210 + c:211 + c], rowtmp[0:1, 200 + c:201 + c],
               1.0 / QMAX, AL.mult, 1e-12, AL.max)
        em.wait_now(Tn, SV)
        for c in range(nvals):
            MM(pb[0][:, c:c + 1], ones_row[0:1, :], rowtmp[0:1, 210 + c:211 + c], True, True)
        em.wait_now(V, ST)
        for c in range(nvals):
            col = sc_cols[c]
            CP(SCAL[:, col:col + 1], pb[0][:, c:c + 1])
            RECIP(SCAL[:, col + 1:col + 2], SCAL[:, col:col + 1])
            TS(SCAL[:, col + 2:col + 3], SCAL[:, col:col + 1], -1.0, AL.mult)

    cross_part_max(AMAX[:, 0:1], 0)
    allreduce_scales(0, 0, 1, [0])
    barrier()

    xn_g = B0
    xmix = B1
    projg = B2
    own0 = NWP + 1
    warm0 = 1

    def load_weightT(wt_ap):
        dma(B_W[:, 0:NFT * D].rearrange("p (j f) -> p j f", f=D),
            wt_ap.rearrange("(j r) f -> r j f", r=128))

    def proj_group(src_col0, n, mix_c, dst, dst0, sig):
        dma(xn_g[:, 0:NFT * (TB + 1)].rearrange("p (j t) -> p j t", t=TB + 1)[:, :, 0:n + 1],
            sp_xn[:, src_col0 - 1:src_col0 + n].rearrange("(j f) c -> f j c", f=128))
        em.wait_now(V, SD)
        for j in range(NFT):
            xg = ft_(xn_g, j, TB + 1)
            mx = ft_(xmix, j, TB)[:, 0:n]
            TT(mx, xg[:, 0:n], xg[:, 1:n + 1], AL.subtract)
            STT(mx, mx, mix_c[:, j:j + 1], xg[:, 1:n + 1], AL.mult, AL.add)
        em.wait_now(Tn, SV)
        for ot in range(NFT):
            bank = pb[ot % 2]
            if ot >= 2:
                em.wait_now(Tn, SS if sig else SV)
            for j in range(NFT):
                MM(bank[:, 0:n], ft_(B_W, j, D)[:, ot * 128:(ot + 1) * 128],
                   ft_(xmix, j, TB)[:, 0:n], j == 0, j == NFT - 1)
            po = ft_(projg, ot, TB)[:, 0:n]
            if sig:
                em.wait_now(Sc, ST)
                ACT(po, bank[:, 0:n], AF.Sigmoid)
            else:
                em.wait_now(V, ST)
                CP(po, bank[:, 0:n])
        em.wait_now(Sy, SS if sig else SV)
        dma(dst[:, dst0:dst0 + n].rearrange("(j f) c -> f j c", f=128),
            projg[:, 0:NFT * TB].rearrange("p (j t) -> p j t", t=TB)[:, :, 0:n])
        em.wait_now(Tn, SD)
        em.wait_now(V, SD) if not sig else em.wait_now(Sc, SD)

    for (wn, spill, mix_c, sig) in (("WkT", sp_k, c_tmk, False), ("WvT", sp_v, c_tmv, False),
                                    ("WrT", sp_r, c_tmr, False), ("WgT", sp_g, c_tmg, True)):
        load_weightT(WT[wn])
        em.wait_now(Tn, SD)
        kv = wn in ("WkT", "WvT")
        if kv and TWARM > 0:
            for gc in range(0, TWARM, TB):
                proj_group(warm0 + gc, min(TB, TWARM - gc), mix_c, spill, gc, sig)
        for gc in range(0, TOWN, TB):
            proj_group(own0 + gc, min(TB, TOWN - gc), mix_c, spill,
                       (TWARM + gc) if kv else gc, sig)
        barrier()

    load_weightT(WT["WoT"])
    barrier()

    k_g, v_g, r_g, gg_g = B3, B4, B5, B6
    kt_sub = B8[:, 3 * D:4 * D]
    ktm = B8[:, 0:D]
    vtm = B8[:, D:2 * D]
    k2tm = B8[:, 2 * D:3 * D]
    Atl = B1
    attn_g = B2
    y_g = B7
    ym_g = B0
    s_g = B1

    def mixer_group(kvc0, n, own, ownc0=0):
        dma(k_g[:, 0:NFT * TB].rearrange("p (j t) -> p j t", t=TB)[:, :, 0:n],
            sp_k[:, kvc0:kvc0 + n].rearrange("(j f) c -> f j c", f=128))
        dma(v_g[:, 0:NFT * TB].rearrange("p (j t) -> p j t", t=TB)[:, :, 0:n],
            sp_v[:, kvc0:kvc0 + n].rearrange("(j f) c -> f j c", f=128))
        if own:
            dma(r_g[:, 0:NFT * TB].rearrange("p (j t) -> p j t", t=TB)[:, :, 0:n],
                sp_r[:, ownc0:ownc0 + n].rearrange("(j f) c -> f j c", f=128))
        em.wait_now(V, SD)
        em.wait_now(Tn, SD)
        for sg in range(n // 128):
            t0g = sg * 128
            for j in range(NFT):
                kv_ = ft_(k_g, j, TB)[:, t0g:t0g + 128].rearrange("p (c s) -> p c s", s=L)
                mv_ = ft_(M_km, j, L).unsqueeze(1).broadcast_to([128, 2, L])
                ov_ = kt_sub[:, j * 128:(j + 1) * 128].rearrange("p (c s) -> p c s", s=L)
                em.op(V, lambda E, o=ov_, a=kv_, b=mv_: E.tensor_tensor(o, a, b, op=AL.mult),
                      [ft_(k_g, j, TB)], [kt_sub])
            em.wait_now(Tn, SV)
            for j in range(NFT):
                TR(pb[2 + j // 4][:, (j % 4) * 128:(j % 4 + 1) * 128],
                   kt_sub[:, j * 128:(j + 1) * 128])
            em.wait_now(V, ST)
            CP(ktm[:, 0:512], pb[2][:, 0:512])
            CP(ktm[:, 512:1024], pb[3][:, 0:512])
            em.wait_now(Tn, SV)
            for j in range(NFT):
                TR(pb[2 + j // 4][:, (j % 4) * 128:(j % 4 + 1) * 128],
                   ft_(v_g, j, TB)[:, t0g:t0g + 128])
            em.wait_now(V, ST)
            CP(vtm[:, 0:512], pb[2][:, 0:512])
            CP(vtm[:, 512:1024], pb[3][:, 0:512])
            TT(k2tm[:, :], ktm[:, :], M_k2L[:, :], AL.mult)
            em.wait_now(Tn, SV)
            for ci in range(2):
                cp_ = ci * 64
                tok0 = t0g + ci * 64
                if own:
                    for j in range(NFT):
                        for hp in range(2):
                            MM(pb[4 + j // 4][cp_:cp_ + 64,
                                              (j % 4) * 128 + hp * 64:(j % 4) * 128 + hp * 64 + 64],
                               ft_(v_g, j, TB)[hp * 64:hp * 64 + 64, tok0:tok0 + 64],
                               ft_(r_g, j, TB)[hp * 64:hp * 64 + 64, tok0:tok0 + 64],
                               True, True)
                    em.wait_now(V, ST)
                    for b in range(2):
                        pv = pb[4 + b][cp_:cp_ + 64, 0:512].rearrange("p (k s) -> p k s", s=64)
                        mk = mask64[0:64, :].unsqueeze(1).broadcast_to([64, 8, 64])
                        ov = Atl[cp_:cp_ + 64, b * 512:(b + 1) * 512].rearrange("p (k s) -> p k s", s=64)
                        em.op(V, lambda E, o=ov, a=pv, m=mk: E.tensor_tensor(o, a, m, op=AL.mult),
                              [pb[4 + b]], [Atl])
                    em.wait_now(Tn, SV)
                    for j in range(NFT):
                        for hp in range(2):
                            h = 2 * j + hp
                            MM(pb[6][hp * 64:hp * 64 + 64, j * 64:j * 64 + 64],
                               ktm[cp_:cp_ + 64, h * 64:h * 64 + 64],
                               Atl[cp_:cp_ + 64, j * 128 + hp * 64:j * 128 + hp * 64 + 64],
                               True, False)
                            MM(pb[6][hp * 64:hp * 64 + 64, j * 64:j * 64 + 64],
                               St[hp * 64:hp * 64 + 64, j * 64:j * 64 + 64],
                               ft_(r_g, j, TB)[hp * 64:hp * 64 + 64, tok0:tok0 + 64],
                               False, True)
                    em.wait_now(V, ST)
                    for j in range(NFT):
                        att = ft_(attn_g, j, TB)
                        for hp in range(2):
                            TT(att[hp * 64:hp * 64 + 64, tok0:tok0 + 64],
                               pb[6][hp * 64:hp * 64 + 64, j * 64:j * 64 + 64],
                               ft_(M_out, j, L)[hp * 64:hp * 64 + 64, :], AL.mult)
                em.wait_now(Tn, SV)
                for j in range(NFT):
                    for hp in range(2):
                        h = 2 * j + hp
                        MM(pb[7][hp * 64:hp * 64 + 64, j * 64:j * 64 + 64],
                           vtm[cp_:cp_ + 64, h * 64:h * 64 + 64],
                           k2tm[cp_:cp_ + 64, h * 64:h * 64 + 64],
                           True, True)
                em.wait_now(V, ST)
                TT(St[:, :], St[:, :], M_stL[:, :], AL.mult)
                TT(St[:, :], St[:, :], pb[7][:, :], AL.add)
                em.wait_now(Tn, SV)

    if TWARM > 0:
        for gc in range(0, TWARM, TB):
            mixer_group(gc, min(TB, TWARM - gc), own=False)
            barrier()

    for gc in range(0, TOWN, TB):
        n = min(TB, TOWN - gc)
        mixer_group(TWARM + gc, n, own=True, ownc0=gc)
        dma(gg_g[:, 0:NFT * TB].rearrange("p (j t) -> p j t", t=TB)[:, :, 0:n],
            sp_g[:, gc:gc + n].rearrange("(j f) c -> f j c", f=128))
        em.wait_now(V, SD)
        for j in range(NFT):
            TT(ft_(attn_g, j, TB)[:, 0:n], ft_(attn_g, j, TB)[:, 0:n],
               ft_(gg_g, j, TB)[:, 0:n], AL.mult)
        em.wait_now(Tn, SV)
        for ot in range(NFT):
            bank = pb[ot % 2]
            if ot >= 2:
                em.wait_now(Tn, SV)
            for j in range(NFT):
                MM(bank[:, 0:n], ft_(B_W, j, D)[:, ot * 128:(ot + 1) * 128],
                   ft_(attn_g, j, TB)[:, 0:n], j == 0, j == NFT - 1)
            em.wait_now(V, ST)
            CP(ft_(y_g, ot, TB)[:, 0:n], bank[:, 0:n])
        em.wait_now(Tn, SV)
        for j in range(NFT):
            MM(pb[2][0:1, 0:n], meanw[:, 0:1], ft_(y_g, j, TB)[:, 0:n], j == 0, j == NFT - 1)
        em.wait_now(V, ST)
        CP(rowtmp[0:1, 500:500 + n], pb[2][0:1, 0:n])
        em.wait_now(Tn, SV)
        MM(pb[3][:, 0:n], ones_row[0:1, :], rowtmp[0:1, 500:500 + n], True, True)
        em.wait_now(V, ST)
        for j in range(NFT):
            TT(ft_(ym_g, j, TB)[:, 0:n], ft_(y_g, j, TB)[:, 0:n], pb[3][:, 0:n], AL.subtract)
        em.wait_now(Sc, SV)
        for j in range(NFT):
            ACT(ft_(s_g, j, TB)[:, 0:n], ft_(ym_g, j, TB)[:, 0:n], AF.Square)
        em.wait_now(Tn, SS)
        for j in range(NFT):
            MM(pb[2][0:1, 0:n], meanw[:, 0:1], ft_(s_g, j, TB)[:, 0:n], j == 0, j == NFT - 1)
        em.wait_now(Sc, ST)
        ACT(rowtmp[0:1, 300:300 + n], pb[2][0:1, 0:n], AF.Sqrt, bias=cEPS[0:1, 0:1])
        em.wait_now(V, SS)
        RECIP(rowtmp[0:1, 300:300 + n], rowtmp[0:1, 300:300 + n])
        em.wait_now(Tn, SV)
        MM(pb[3][:, 0:n], ones_row[0:1, :], rowtmp[0:1, 300:300 + n], True, True)
        em.wait_now(V, ST)
        for j in range(NFT):
            sj = ft_(s_g, j, TB)[:, 0:n]
            TT(sj, ft_(ym_g, j, TB)[:, 0:n], pb[3][:, 0:n], AL.mult)
            TS(sj, sj, c_gln[:, j:j + 1], AL.mult, c_bln[:, j:j + 1], AL.add)
        em.wait_now(Sy, SV)
        dma(sp_s[:, gc:gc + n].rearrange("(j f) c -> f j c", f=128),
            s_g[:, 0:NFT * TB].rearrange("p (j t) -> p j t", t=TB)[:, :, 0:n])
        barrier()

    assert NFT * TB >= 2 * TOWN, "buffer layout assumes TB*8 >= 2*TOWN"
    xq_bufs = [B3, B4, B5, B6]
    def xq_t(j):
        return xq_bufs[j // 2][:, (j % 2) * TOWN:(j % 2 + 1) * TOWN]

    tz = B0[:, 0:TOWN]
    tg = B0[:, TOWN:2 * TOWN]
    ti16 = B1[:, 0:(TOWN + 1) // 2].bitcast(I16)[:, 0:TOWN]
    tu16 = B1[:, 0:(TOWN + 1) // 2].bitcast(U16)[:, 0:TOWN]

    def quant4(out, x, scol, n, eng=V, z=None, g=None, i16=None):
        s = SCAL[:, scol:scol + 1]
        rs = SCAL[:, scol + 1:scol + 2]
        ns = SCAL[:, scol + 2:scol + 3]
        z = z if z is not None else tz
        g = g if g is not None else tg
        i16 = i16 if i16 is not None else ti16
        TS(z[:, 0:n], x, rs, AL.mult, eng=eng)
        TS(i16[:, 0:n], x, rs, AL.mult, eng=eng)
        TT(g[:, 0:n], z[:, 0:n], i16[:, 0:n], AL.subtract, eng=eng)
        STT(z[:, 0:n], g[:, 0:n], 0.5, i16[:, 0:n], AL.is_gt, AL.add, eng=eng)
        STT(g[:, 0:n], g[:, 0:n], -0.5, z[:, 0:n], AL.is_ge, AL.add, eng=eng)
        TS(out, g[:, 0:n], s, AL.mult, ns, AL.add, eng=eng)

    def gp_round_to_int(out_i16, z, g, r0f, n):
        TS(out_i16[:, 0:n], z[:, 0:n], 1.0, AL.mult, eng=Gp)
        TS(r0f[:, 0:n], out_i16[:, 0:n], 1.0, AL.mult, eng=Gp)
        TT(g[:, 0:n], z[:, 0:n], r0f[:, 0:n], AL.subtract, eng=Gp)
        TS(z[:, 0:n], g[:, 0:n], 0.5, AL.is_gt, eng=Gp)
        TT(r0f[:, 0:n], z[:, 0:n], r0f[:, 0:n], AL.add, eng=Gp)
        TS(g[:, 0:n], g[:, 0:n], -0.5, AL.is_ge, eng=Gp)
        TT(g[:, 0:n], g[:, 0:n], r0f[:, 0:n], AL.add, eng=Gp)
        TS(z[:, 0:n], g[:, 0:n], -1.0, AL.add, eng=Gp)

    def v_round_i16(out_i16, z, g, n):
        TS(out_i16[:, 0:n], z[:, 0:n], 1.0, AL.mult, eng=V)
        TT(g[:, 0:n], z[:, 0:n], out_i16[:, 0:n], AL.subtract, eng=V)
        STT(z[:, 0:n], g[:, 0:n], 0.5, out_i16[:, 0:n], AL.is_gt, AL.add, eng=V)
        STT(g[:, 0:n], g[:, 0:n], -0.5, z[:, 0:n], AL.is_ge, AL.add, eng=V)
        TS(out_i16[:, 0:n], g[:, 0:n], 1.0, AL.mult, -1.0, AL.add, eng=V)

    for bidx, buf in enumerate(xq_bufs):
        dma(buf[:, 0:2 * TOWN].rearrange("p (j t) -> p j t", t=TOWN),
            sp_xn[bidx * 256:(bidx + 1) * 256, own0:own0 + TOWN]
            .rearrange("(j f) c -> f j c", f=128))
    em.wait_now(V, SD)
    em.wait_now(Gp, SD)
    for j in range(NFT):
        if j < 5:
            quant4(xq_t(j), xq_t(j), 0, TOWN)
        else:
            gz = B1[:, 1024:1024 + TOWN]
            gg_ = B2[:, 0:TOWN]
            gr0f = B2[:, TOWN:2 * TOWN]
            gi16_ = B7[:, 0:TOWN // 2].bitcast(I16)[:, 0:TOWN]
            TT(gz, xq_t(j), SCAL[:, 1:2].broadcast_to([128, TOWN]), AL.mult, eng=Gp)
            gp_round_to_int(gi16_, gz, gg_, gr0f, TOWN)
            TT(xq_t(j), gz, SCAL[:, 0:1].broadcast_to([128, TOWN]), AL.mult, eng=Gp)
    barrier()

    wnat = B1
    wdqT = B7
    ustg4 = B2[:, 0:4 * FTB]
    wsc = B2[:, 4 * FTB:4 * FTB + 224]
    wz = B0[:, 0:FF]
    wgs = B8[:, 0:FF]
    wi16 = B9[:, 0:FF // 2].bitcast(I16)[:, 0:FF]

    def dequant_tile(wtile, nblk):
        bm = wsc[:, 0:nblk]
        sw = wsc[:, 96:96 + nblk]
        em.op(V, lambda E, o=bm.unsqueeze(2), x=wtile.rearrange("p (b s) -> p b s", s=32):
              E.tensor_reduce(o, x, axis=AX.X, op=AL.max, apply_absolute_value=True),
              [wtile], [wsc])
        TS(sw, bm, 1.0 / 7.0, AL.mult, 1e-12, AL.max)
        rw = bm
        RECIP(rw, sw)
        rwb = rw.unsqueeze(2).broadcast_to([128, nblk, 32])
        swb = sw.unsqueeze(2).broadcast_to([128, nblk, 32])
        nw = nblk * 32
        wv = wtile.rearrange("p (b s) -> p b s", s=32)
        zv = wz[:, 0:nw].rearrange("p (b s) -> p b s", s=32)
        i16v = wi16[:, 0:nw].rearrange("p (b s) -> p b s", s=32)
        em.op(V, lambda E, o=zv, a=wv, b=rwb: E.tensor_tensor(o, a, b, op=AL.mult),
              [wtile, wsc], [wz])
        em.op(V, lambda E, o=i16v, a=wv, b=rwb: E.tensor_tensor(o, a, b, op=AL.mult),
              [wtile, wsc], [wi16])
        TT(wgs[:, 0:nw], wz[:, 0:nw], wi16[:, 0:nw], AL.subtract)
        STT(wz[:, 0:nw], wgs[:, 0:nw], 0.5, wi16[:, 0:nw], AL.is_gt, AL.add)
        STT(wgs[:, 0:nw], wgs[:, 0:nw], -0.5, wz[:, 0:nw], AL.is_ge, AL.add)
        TS(wz[:, 0:nw], wgs[:, 0:nw], 1.0, AL.mult, -1.0, AL.add)
        ov = wtile.rearrange("p (b s) -> p b s", s=32)
        yv = wz[:, 0:nw].rearrange("p (b s) -> p b s", s=32)
        em.op(V, lambda E, o=ov, a=yv, b=swb: E.tensor_tensor(o, a, b, op=AL.mult),
              [wz, wsc], [wtile])

    NOS = NIT // 4

    def ffn_matmul_weight(w_in, spill, amax_col):
        for os_ in range(NOS):
            em.wait_now(Sy, ST)
            dma(wnat[:, 0:4 * D].rearrange("p (q f) -> p q f", f=D),
                w_in[os_ * 512:(os_ + 1) * 512, :].rearrange("(q r) f -> r q f", r=128))
            em.wait_now(V, SD)
            for q in range(4):
                dequant_tile(ft_(wnat, q, D), 32)
                em.wait_now(Tn, SV)
                for j in range(NFT):
                    TR(pb[2 + j // 4][:, (j % 4) * 128:(j % 4 + 1) * 128],
                       ft_(wnat, q, D)[:, j * 128:(j + 1) * 128])
                em.wait_now(V, ST)
                for half in range(2):
                    for j4 in range(4):
                        j = half * 4 + j4
                        CP(wdqT[:, j * 512 + q * 128:j * 512 + (q + 1) * 128],
                           pb[2 + half][:, j4 * 128:(j4 + 1) * 128])
                em.wait_now(Tn, SV)
            for tb in range(NFTB):
                em.wait_now(V, SD)
                for q in range(4):
                    bank = pb[q % 2]
                    if q >= 2 or tb > 0:
                        em.wait_now(Tn, SV)
                    for j in range(NFT):
                        MM(bank[:, 0:FTB], wdqT[:, j * 512 + q * 128:j * 512 + (q + 1) * 128],
                           xq_t(j)[:, tb * FTB:(tb + 1) * FTB], j == 0, j == NFT - 1)
                    em.wait_now(V, ST)
                    CP(ustg4[:, q * FTB:(q + 1) * FTB], bank[:, 0:FTB])
                    RED(stats[:, 6:7], ustg4[:, q * FTB:(q + 1) * FTB], AL.max, absval=True)
                    TT(AMAX[:, amax_col:amax_col + 1], AMAX[:, amax_col:amax_col + 1],
                       stats[:, 6:7], AL.max)
                em.wait_now(Sy, SV)
                dma(spill[os_ * 512:(os_ + 1) * 512, tb * FTB:(tb + 1) * FTB]
                    .rearrange("(q r) t -> r q t", r=128),
                    ustg4[:, 0:4 * FTB].rearrange("p (q t) -> p q t", t=FTB))
            barrier()

    ffn_matmul_weight(Wup, sp_upr, 1)
    ffn_matmul_weight(Wgate, sp_gater, 2)

    for mt in range(NFT):
        dma(wnat[:, 0:FF], Wdown[mt * 128:(mt + 1) * 128, :])
        em.wait_now(V, SD)
        dequant_tile(wnat[:, 0:FF], 96)
        em.wait_now(Tn, SV)
        for it in range(NIT):
            TR(pb[2 + (it % 8) // 4][:, (it % 4) * 128:(it % 4 + 1) * 128],
               wnat[:, it * 128:(it + 1) * 128])
            if it % 8 == 7:
                em.wait_now(V, ST)
                for b in range(2):
                    CP(stage[:, b * 512:(b + 1) * 512], pb[2 + b][:, 0:512])
                em.wait_now(Sy, SV)
                dma(sp_wd[(it - 7) * 128:(it + 1) * 128, mt * 128:(mt + 1) * 128]
                    .rearrange("(k r) c -> r k c", r=128),
                    stage[:, 0:1024].rearrange("p (k c) -> p k c", c=128))
                em.wait_now(Tn, SD)
                em.wait_now(V, SD)
        if mt % 3 == 2:
            barrier()
    barrier()

    cross_part_max(AMAX[:, 1:2], 0)
    cross_part_max(AMAX[:, 2:3], 1)
    allreduce_scales(1, 0, 2, [3, 6])
    barrier()

    class QSet:
        pass
    sV = QSet()
    sV.upt = B3[:, 0:TOWN]; sV.gat = B3[:, TOWN:2 * TOWN]
    sV.sig = B4[:, 0:TOWN]; sV.gst = B4[:, TOWN:2 * TOWN]
    sV.z = B6[:, 0:TOWN]; sV.g = B6[:, TOWN:2 * TOWN]
    sV.i16 = B5[:, 0:TOWN // 2].bitcast(I16)[:, 0:TOWN]
    sV.u16 = B5[:, TOWN // 2:TOWN].bitcast(U16)[:, 0:TOWN]
    sV.gi16 = B5[:, TOWN:TOWN + TOWN // 2].bitcast(I16)[:, 0:TOWN]
    sV.r0f = None
    sG = QSet()
    sG.upt = B0[:, 0:TOWN]; sG.gat = B0[:, TOWN:2 * TOWN]
    sG.sig = B1[:, 0:TOWN]; sG.gst = B1[:, TOWN:2 * TOWN]
    sG.z = B2[:, 0:TOWN]; sG.g = B2[:, TOWN:2 * TOWN]
    sG.i16 = B7[:, 0:TOWN // 2].bitcast(I16)[:, 0:TOWN]
    sG.u16 = B7[:, TOWN // 2:TOWN].bitcast(U16)[:, 0:TOWN]
    sG.gi16 = B7[:, TOWN:TOWN + TOWN // 2].bitcast(I16)[:, 0:TOWN]
    sG.r0f = B9[:, 0:TOWN]

    qlast = {id(sV): None, id(sG): None}

    def qs_wait_reuse(S, eng):
        snap = qlast[id(S)]
        if snap is not None:
            ep, cnts = snap
            if ep == em.nbar:
                for s_, v_ in cnts.items():
                    em.wait(Sy, s_, v_)

    def qs_record(S, eng):
        es = esem_name[eng]
        if em.unfenced[SEM_OWNER[es]] > 0:
            em.fence(SEM_OWNER[es], es)
        if em.unfenced["scalar"] > 0:
            em.fence("scalar", "ss")
        if em.unfenced["vector"] > 0:
            em.fence("vector", "sv")
        snap = {es: em.counts[es], "ss": em.counts["ss"], "sv": em.counts["sv"]}
        for ds in DSEMS:
            snap[ds] = em.counts[ds]
        qlast[id(S)] = (em.nbar, snap)

    esem_name = {V: "sv", Gp: "sg", Sc: "ss"}

    def p5_itile(it, eng, S):
        r0_, r1_ = it * 128, it * 128 + 128
        qs_wait_reuse(S, eng)
        dma(S.upt, sp_upr[r0_:r1_, :])
        dma(S.gat, sp_gater[r0_:r1_, :])
        em.wait_now(eng, SD)
        rs2 = SCAL[:, 4:5]; rs3 = SCAL[:, 7:8]; s3 = SCAL[:, 6:7]
        if eng == Gp:
            TT(S.z, S.upt, rs2.broadcast_to([128, TOWN]), AL.mult, eng=Gp)
            gp_round_to_int(S.i16, S.z, S.g, S.r0f, TOWN)
            TS(S.i16, S.z, 1.0, AL.mult, eng=Gp)
        else:
            TS(S.z, S.upt, rs2, AL.mult, eng=V)
            v_round_i16(S.i16, S.z, S.g, TOWN)
        em.wait_now(Sy, esem_name[eng] if False else ("sg" if eng == Gp else "sv"))
        dma(sp_upq[r0_:r1_, :], S.i16)
        if eng == Gp:
            TT(S.z, S.gat, rs3.broadcast_to([128, TOWN]), AL.mult, eng=Gp)
            gp_round_to_int(S.gi16, S.z, S.g, S.r0f, TOWN)
            TT(S.gat, S.z, s3.broadcast_to([128, TOWN]), AL.mult, eng=Gp)
        else:
            TS(S.z, S.gat, rs3, AL.mult, eng=V)
            v_round_i16(S.gi16, S.z, S.g, TOWN)
            TS(S.gat, S.gi16, s3, AL.mult, eng=V)
        em.wait_now(Sc, "sg" if eng == Gp else "sv")
        ACT(S.sig, S.gat, AF.Sigmoid)
        em.wait_now(eng, SS)
        if eng == Gp:
            TS(S.z, S.sig, 65536.0, AL.mult, eng=Gp)
            TS(S.z, S.z, 65535.0, AL.min, eng=Gp)
            TS(S.u16, S.z, 1.0, AL.mult, eng=Gp)
            TS(S.r0f, S.u16, 1.0, AL.mult, eng=Gp)
            TT(S.g, S.z, S.r0f, AL.subtract, eng=Gp)
            TS(S.z, S.g, 0.5, AL.is_gt, eng=Gp)
            TT(S.r0f, S.z, S.r0f, AL.add, eng=Gp)
            TS(S.g, S.g, -0.5, AL.is_ge, eng=Gp)
            TT(S.g, S.g, S.r0f, AL.add, eng=Gp)
            TS(S.g, S.g, -1.0, AL.add, eng=Gp)
            TS(S.sig, S.g, 1.0 / 65536.0, AL.mult, eng=Gp)
        else:
            TS(S.z, S.sig, 65536.0, AL.mult, 65535.0, AL.min, eng=V)
            TS(S.u16, S.z, 1.0, AL.mult, eng=V)
            TT(S.g, S.z, S.u16, AL.subtract, eng=V)
            STT(S.z, S.g, 0.5, S.u16, AL.is_gt, AL.add, eng=V)
            STT(S.g, S.g, -0.5, S.z, AL.is_ge, AL.add, eng=V)
            TS(S.sig, S.g, 1.0 / 65536.0, AL.mult, -1.0 / 65536.0, AL.add, eng=V)
        TT(S.gst, S.gat, S.sig, AL.mult, eng=eng)
        if eng == Gp:
            em.wait_now(V, "sg")
        RED(stats[:, 7:8], S.gst, AL.max, absval=True, eng=V)
        TT(AMAX[:, 3:4], AMAX[:, 3:4], stats[:, 7:8], AL.max, eng=V)
        em.wait_now(Sy, "sg" if eng == Gp else "sv")
        dma(sp_gs[r0_:r1_, :], S.gst)
        qs_record(S, eng)

    def p6_itile(it, eng, S):
        r0_, r1_ = it * 128, it * 128 + 128
        qs_wait_reuse(S, eng)
        dma(S.gst, sp_gs[r0_:r1_, :])
        dma(S.i16, sp_upq[r0_:r1_, :])
        em.wait_now(eng, SD)
        rs4 = SCAL[:, 10:11]; s4 = SCAL[:, 9:10]; s2 = SCAL[:, 3:4]
        if eng == Gp:
            TT(S.z, S.gst, rs4.broadcast_to([128, TOWN]), AL.mult, eng=Gp)
            gp_round_to_int(S.gi16, S.z, S.g, S.r0f, TOWN)
            TT(S.gat, S.z, s4.broadcast_to([128, TOWN]), AL.mult, eng=Gp)
            TS(S.z, S.i16, 1.0, AL.mult, eng=Gp)
            TT(S.upt, S.z, s2.broadcast_to([128, TOWN]), AL.mult, eng=Gp)
        else:
            TS(S.z, S.gst, rs4, AL.mult, eng=V)
            v_round_i16(S.gi16, S.z, S.g, TOWN)
            TS(S.gat, S.gi16, s4, AL.mult, eng=V)
            TS(S.upt, S.i16, s2, AL.mult, eng=V)
        TT(S.gst, S.gat, S.upt, AL.mult, eng=eng)
        if eng == Gp:
            em.wait_now(V, "sg")
        RED(stats[:, 8:9], S.gst, AL.max, absval=True, eng=V)
        TT(AMAX[:, 4:5], AMAX[:, 4:5], stats[:, 8:9], AL.max, eng=V)
        em.wait_now(Sy, "sg" if eng == Gp else "sv")
        dma(sp_o[r0_:r1_, :], S.gst)
        qs_record(S, eng)

    for it in range(NIT):
        if it % 3 == 2:
            p5_itile(it, Gp, sG)
        else:
            p5_itile(it, V, sV)
        if it % 3 == 2:
            barrier()

    barrier()
    cross_part_max(AMAX[:, 3:4], 0)
    allreduce_scales(2, 0, 1, [9])
    barrier()

    for it in range(NIT):
        if it % 3 == 2:
            p6_itile(it, Gp, sG)
        else:
            p6_itile(it, V, sV)
        if it % 3 == 2:
            barrier()

    barrier()
    cross_part_max(AMAX[:, 4:5], 0)
    allreduce_scales(3, 0, 1, [12])
    barrier()

    oq4 = B3[:, 0:4 * FTB]
    oq = [oq4[:, q * FTB:(q + 1) * FTB] for q in range(4)]
    wd = B7
    g2t = B5
    st_ = B6
    xrow = B0[:, 0:D]
    orow = B0[:, D:2 * D]
    prodst = B1

    for tb in range(NFTB):
        c0 = tb * FTB
        for isl in range(NOS):
            em.wait_now(Sy, ST)
            em.wait_now(V, ST)
            dma(oq4.rearrange("p (q t) -> p q t", t=FTB),
                sp_o[isl * 512:(isl + 1) * 512, c0:c0 + FTB]
                .rearrange("(q r) t -> r q t", r=128))
            dma(wd[:, 0:4 * D].rearrange("p (q f) -> p q f", f=D),
                sp_wd[isl * 512:(isl + 1) * 512, :].rearrange("(q r) f -> r q f", r=128))
            em.wait_now(V, SD)
            for q in range(4):
                quant4(oq[q], oq[q], 12, FTB, z=B6[:, 0:FTB], g=B6[:, FTB:2 * FTB],
                       i16=B5[:, 0:FTB // 2].bitcast(I16)[:, 0:FTB])
            em.wait_now(Tn, SV)
            for m in range(NFT):
                for q in range(4):
                    MM(pb[m][:, 0:FTB], ft_(wd, q, D)[:, m * 128:(m + 1) * 128],
                       oq[q], isl == 0 and q == 0, isl == NOS - 1 and q == 3)
            em.wait_now(V, ST)
            em.wait_now(Sy, ST)
        barrier()
        em.wait_now(Sc, ST)
        for m in range(NFT):
            ACT(ft_(g2t, m, FTB), pb[m][:, 0:FTB], AF.Silu)
        dma(st_[:, 0:NFT * FTB].rearrange("p (j t) -> p j t", t=FTB),
            sp_s[:, c0:c0 + FTB].rearrange("(j f) c -> f j c", f=128))
        em.wait_now(V, SD)
        em.wait_now(V, SS)
        for j in range(NFT):
            TT(ft_(g2t, j, FTB), ft_(st_, j, FTB), ft_(g2t, j, FTB), AL.mult)
        em.wait_now(Tn, SV)
        for tt in range(FTB // 128):
            for j in range(NFT):
                TR(pb[j // 4][:, (j % 4) * 128:(j % 4 + 1) * 128],
                   ft_(g2t, j, FTB)[:, tt * 128:(tt + 1) * 128])
            em.wait_now(V, ST)
            CP(prodst[:, 0:512], pb[0][:, 0:512])
            CP(prodst[:, 512:1024], pb[1][:, 0:512])
            row0 = c0 + tt * 128
            dma(xrow, x_own[1 + row0:1 + row0 + 128, :])
            em.wait_now(V, SD)
            TT(orow, prodst[:, 0:D], xrow, AL.add)
            em.wait_now(Sy, SV)
            dma(y_out[row0:row0 + 128, :], orow)
            em.wait_now(V, SD)
            em.wait_now(Tn, SD)

    if debug:
        dbg_specs = [("d_scal", SCAL[:, 0:32], [128, 32], None),
                     ("d_amax", AMAX[:, 0:8], [128, 8], None)]
        for nm, ap_, shp, _ in dbg_specs:
            od = dt(nm, shp, F32, kind="ExternalOutput").ap()
            dma(od, ap_)
        for nm, sp, shp in [("d_xn", sp_xn, [D, TC]), ("d_s", sp_s, [D, TOWN]),
                            ("d_upr", sp_upr, [FF, TOWN]), ("d_gater", sp_gater, [FF, TOWN]),
                            ("d_gs", sp_gs, [FF, TOWN]), ("d_o", sp_o, [FF, TOWN]),
                            ("d_k", sp_k, [D, TWARM + TOWN]), ("d_r", sp_r, [D, TOWN]),
                            ("d_g", sp_g, [D, TOWN])]:
            od = dt(nm, shp, F32, kind="ExternalOutput").ap()
            dma(od, sp)
    em.wait_now(Sy, SD)
    barrier()
    return nc, em


def finalize(nc, em):
    ctxs = []
    sems = {}
    esems = {}
    for name in em.counts:
        h = nc.semaphore(name)
        sems[name] = h.__enter__()
        ctxs.append(h)
    for e in ENGINES:
        h = nc.semaphore("ep_" + e)
        esems[e] = h.__enter__()
        ctxs.append(h)
    blk = nc.Block()
    block = blk.__enter__()
    em.replay(block, sems, esems, SEM_OWNER)
    blk.__exit__(None, None, None)
    for h in reversed(ctxs):
        h.__exit__(None, None, None)
    return nc


def make_core_inputs(x_full, params, core, n_cores, cfg: Cfg):
    TOWN, TWARM = cfg.TOWN, cfg.TWARM
    t0 = core * TOWN
    xo = np.zeros((cfg.NOP, D), np.float32)
    xo[1:1 + TOWN] = x_full[t0:t0 + TOWN]
    xo[0] = x_full[1] if core == 0 else x_full[t0 - 1]
    xw = np.zeros((cfg.NWP, D), np.float32)
    if core > 0 and TWARM > 0:
        xw[1:1 + TWARM] = x_full[t0 - TWARM:t0]
        xw[0] = x_full[t0 - TWARM - 1] if t0 - TWARM - 1 >= 0 else 0.0
    vc = np.stack([params[k].reshape(NFT, 128).T for k in
                   ("td", "tmk", "tmv", "tmr", "tmg", "g_ln", "b_ln")], 1)
    vc = vc.reshape(128, 7 * NFT, order="F") if False else np.concatenate(
        [params[k].reshape(NFT, 128).T for k in
         ("td", "tmk", "tmv", "tmr", "tmg", "g_ln", "b_ln")], axis=1)
    return {
        "x_own": xo, "x_warm": xw,
        "WrT": np.ascontiguousarray(params["Wr"].T),
        "WkT": np.ascontiguousarray(params["Wk"].T),
        "WvT": np.ascontiguousarray(params["Wv"].T),
        "WgT": np.ascontiguousarray(params["Wg"].T),
        "WoT": np.ascontiguousarray(params["Wo"].T),
        "Wup": params["w_up"], "Wgate": params["w_gate"], "Wdown": params["w_down"],
        "vec_cols": np.ascontiguousarray(vc, np.float32),
        "td_row": params["td"].reshape(1, D),
        "ident_c": np.eye(128, dtype=np.float32),
        "mask64_c": np.triu(np.ones((64, 64), np.float32)),
        "iota64_c": np.arange(64, dtype=np.float32).reshape(1, 64),
        "ones_row_c": np.ones((1, 128), np.float32),
        "meanw_c": np.full((128, 1), 1.0 / D, np.float32),
    }


from concourse.bass_utils import run_bass_kernel_spmd

_N_CORES = 8
_CFG = Cfg(TOWN=2048, TWARM=1024, TB=512, n_cores=_N_CORES)
_NC_CACHE = {}


def _get_program():
    if "nc" not in _NC_CACHE:
        nc, em = build(_CFG)
        finalize(nc, em)
        _NC_CACHE["nc"] = nc
    return _NC_CACHE["nc"]


def kernel(L0_attn_x, L0_attn_time_decay, L0_attn_time_mix_k, L0_attn_time_mix_v,
           L0_attn_time_mix_r, L0_attn_time_mix_g, L0_attn_W_r, L0_attn_W_k,
           L0_attn_W_v, L0_attn_W_g, L0_attn_W_o, L0_attn_ln_gamma,
           L0_attn_ln_beta, L0_ffn_w_gate, L0_ffn_w_up, L0_ffn_w_down):
    x_full = np.ascontiguousarray(np.asarray(L0_attn_x, np.float32).reshape(-1, D))
    params = {
        "td": np.asarray(L0_attn_time_decay, np.float32),
        "tmk": np.asarray(L0_attn_time_mix_k, np.float32),
        "tmv": np.asarray(L0_attn_time_mix_v, np.float32),
        "tmr": np.asarray(L0_attn_time_mix_r, np.float32),
        "tmg": np.asarray(L0_attn_time_mix_g, np.float32),
        "Wr": np.asarray(L0_attn_W_r, np.float32),
        "Wk": np.asarray(L0_attn_W_k, np.float32),
        "Wv": np.asarray(L0_attn_W_v, np.float32),
        "Wg": np.asarray(L0_attn_W_g, np.float32),
        "Wo": np.asarray(L0_attn_W_o, np.float32),
        "g_ln": np.asarray(L0_attn_ln_gamma, np.float32),
        "b_ln": np.asarray(L0_attn_ln_beta, np.float32),
        "w_gate": np.asarray(L0_ffn_w_gate, np.float32),
        "w_up": np.asarray(L0_ffn_w_up, np.float32),
        "w_down": np.asarray(L0_ffn_w_down, np.float32),
    }
    nc = _get_program()
    in_maps = [make_core_inputs(x_full, params, c, _N_CORES, _CFG)
               for c in range(_N_CORES)]
    res = run_bass_kernel_spmd(nc, in_maps, core_ids=list(range(_N_CORES)))
    out = np.concatenate([np.asarray(r["y"], np.float32) for r in res.results], 0)
    return out.reshape(1, _N_CORES * _CFG.TOWN, D)


# version 12
# speedup vs baseline: 1.5977x; 1.4813x over previous
import numpy as np
import concourse.bass as bass
import concourse.mybir as mybir
from concourse.alu_op_type import AluOpType as AL

F32 = mybir.dt.float32
I16 = mybir.dt.int16
U16 = mybir.dt.uint16
AF = mybir.ActivationFunctionType
AX = mybir.AxisListType

ENGINES = ("sync", "tensor", "vector", "scalar", "gpsimd")
SEM_OWNER = {"sd0": "sync", "sd1": "sync", "sd2": "sync", "sd3": "sync",
             "st": "tensor", "sv": "vector", "ss": "scalar", "sg": "gpsimd"}
DSEMS = ("sd0", "sd1", "sd2", "sd3")

D = 1024
NFT = 8
H = 16
DH = 64
FF = 3072
NIT = FF // 128
L = 64
QMAX = 32767.0
EPS = 1e-5


class Cfg:
    def __init__(self, TOWN=2048, TWARM=1024, TB=512, n_cores=8):
        assert TB == 512, "buffer layout assumes TB == 512"
        assert TOWN % TB == 0 and (TWARM % TB == 0 or TWARM == 0)
        self.TOWN, self.TWARM, self.TB, self.n_cores = TOWN, TWARM, TB, n_cores
        self.NOP = TOWN + 128
        self.NWP = TWARM + 128 if TWARM > 0 else 128
        self.TC = self.NWP + self.NOP
        self.n_own_tiles = self.NOP // 128
        self.n_warm_tiles = self.NWP // 128
        self.NTB = TOWN // 512 if TOWN % 512 == 0 else None
        assert TOWN % 512 == 0 or TOWN < 512
        self.FTB = 512 if TOWN >= 512 else TOWN
        self.n_ftb = TOWN // self.FTB


class Emitter:

    def __init__(self, nc):
        self.nc = nc
        self.items = {e: [] for e in ENGINES}
        self.counts = {}
        self.touched = {e: [] for e in ENGINES}
        self.unfenced = {e: 0 for e in ENGINES}
        self.nbar = 0
        self.maxcount = {}

    def new_sem(self, name):
        self.counts[name] = 0
        self.maxcount[name] = 0
        return name

    @staticmethod
    def _rng(ap):
        try:
            t = ap.tensor.name
            lo = int(ap.offset)
            sz = 1
            for st, ct in ap.ap:
                sz += abs(int(st)) * (int(ct) - 1)
            return (t, lo, lo + sz)
        except Exception:
            return (str(ap), 0, 1 << 60)

    def _hazard(self, eng, reads, writes):
        tl = self.touched[eng]
        def ov(a, b):
            return a[0] == b[0] and a[1] < b[2] and b[1] < a[2]
        for ap in reads:
            r = self._rng(ap)
            for (k, lo, hi, w) in tl:
                if w and ov(r, (k, lo, hi)):
                    return True
        for ap in writes:
            r = self._rng(ap)
            for (k, lo, hi, w) in tl:
                if ov(r, (k, lo, hi)):
                    return True
        return False

    def _note(self, eng, reads, writes):
        tl = self.touched[eng]
        for ap in reads:
            k, lo, hi = self._rng(ap)
            tl.append((k, lo, hi, False))
        for ap in writes:
            k, lo, hi = self._rng(ap)
            tl.append((k, lo, hi, True))

    def op(self, eng, fn, reads=(), writes=(), inc=None, inc_by=1):
        reads, writes = list(reads), list(writes)
        if eng != "sync" and self._hazard(eng, reads, writes):
            self.items[eng].append(("drain",))
            self.touched[eng] = []
        self._note(eng, reads, writes)
        if inc is not None:
            self.counts[inc] += inc_by
            self.maxcount[inc] = max(self.maxcount[inc], self.counts[inc])
            assert self.counts[inc] <= 240, (
                f"sem {inc} exceeds 240 within an epoch ({self.counts[inc]}); "
                f"insert a barrier (epoch {self.nbar})")
            self.items[eng].append(("op", fn, inc, inc_by))
        else:
            self.items[eng].append(("op", fn, None, 0))
            self.unfenced[eng] += 1

    def fence(self, eng, sem):
        if self.unfenced[eng] == 0:
            return
        self.unfenced[eng] = 0
        self.counts[sem] += 1
        self.maxcount[sem] = max(self.maxcount[sem], self.counts[sem])
        assert self.counts[sem] <= 240, f"sem {sem} epoch overflow"
        self.items[eng].append(("fenceinc", sem))

    def wait(self, eng, sem, val):
        if val > 0:
            self.items[eng].append(("wait", sem, val))

    def wait_now(self, eng, sem):
        if sem == "sd":
            for s in DSEMS:
                self.wait(eng, s, self.counts[s])
            return
        own = SEM_OWNER[sem]
        if self.unfenced[own] > 0:
            self.fence(own, sem)
        self.wait(eng, sem, self.counts[sem])

    def barrier(self):
        tok = dict(self.counts)
        k = self.nbar
        self.nbar += 1
        assert self.nbar <= 125, "too many barriers for 8-bit epoch sems"
        for e in ENGINES:
            self.items[e].append(("barrier", k, tok))
            self.touched[e] = []
            self.unfenced[e] = 0
        for s in self.counts:
            self.counts[s] = 0

    def replay(self, block, sems, esems, owner):
        em = self

        def mk(eng_name):
            def body(proxy):
                last = [None]
                last_inc = [False]

                def emit_fence(s):
                    if last[0] is not None and not last_inc[0]:
                        last[0].then_inc(sems[s], 1)
                        last_inc[0] = True
                    else:
                        proxy.drain()
                        proxy.nop().then_inc(sems[s], 1)
                        last[0] = None
                        last_inc[0] = False

                for it in em.items[eng_name]:
                    kind = it[0]
                    if kind == "drain":
                        proxy.drain()
                        last[0] = None
                        last_inc[0] = False
                    elif kind == "wait":
                        proxy.wait_ge(sems[it[1]], it[2])
                        last[0] = None
                        last_inc[0] = False
                    elif kind == "fenceinc":
                        emit_fence(it[1])
                    elif kind == "ewait":
                        proxy.wait_ge(esems[eng_name], it[1])
                        last[0] = None
                        last_inc[0] = False
                    elif kind == "barrier":
                        k, tok = it[1], it[2]
                        for s, v in tok.items():
                            if v > 0:
                                proxy.wait_ge(sems[s], v)
                        proxy.nop().then_inc(esems[eng_name], 1)
                        for f in ENGINES:
                            proxy.wait_ge(esems[f], 2 * k + 1)
                        for s, own in owner.items():
                            if own == eng_name:
                                proxy.sem_clear(sems[s])
                        proxy.drain()
                        proxy.nop().then_inc(esems[eng_name], 1)
                        for f in ENGINES:
                            proxy.wait_ge(esems[f], 2 * k + 2)
                        last[0] = None
                        last_inc[0] = False
                    else:
                        _, fn, b, c = it
                        instr = fn(proxy)
                        last[0] = instr
                        last_inc[0] = False
                        if b is not None and instr is not None:
                            instr.then_inc(sems[b], c)
            return body

        for e in ENGINES:
            getattr(block, e)(mk(e))


def build(cfg: Cfg, debug=False):
    nc = bass.Bass("TRN2", target_bir_lowering=False)
    TOWN, TWARM, TB = cfg.TOWN, cfg.TWARM, cfg.TB
    NOP, NWP, TC = cfg.NOP, cfg.NWP, cfg.TC
    FTB, NFTB = cfg.FTB, cfg.n_ftb

    dt = nc.dram_tensor
    x_own = dt("x_own", [NOP, D], F32, kind="ExternalInput").ap()
    x_warm = dt("x_warm", [NWP, D], F32, kind="ExternalInput").ap()
    WT = {n: dt(n, [D, D], F32, kind="ExternalInput").ap()
          for n in ("WrT", "WkT", "WvT", "WgT", "WoT")}
    Wup = dt("Wup", [FF, D], F32, kind="ExternalInput").ap()
    Wgate = dt("Wgate", [FF, D], F32, kind="ExternalInput").ap()
    Wdown = dt("Wdown", [D, FF], F32, kind="ExternalInput").ap()
    vec_cols = dt("vec_cols", [128, 7 * NFT], F32, kind="ExternalInput").ap()
    td_row = dt("td_row", [1, D], F32, kind="ExternalInput").ap()
    ident_i = dt("ident_c", [128, 128], F32, kind="ExternalInput").ap()
    mask64_i = dt("mask64_c", [64, 64], F32, kind="ExternalInput").ap()
    iota64_i = dt("iota64_c", [1, 64], F32, kind="ExternalInput").ap()
    ones_row_i = dt("ones_row_c", [1, 128], F32, kind="ExternalInput").ap()
    meanw_i = dt("meanw_c", [128, 1], F32, kind="ExternalInput").ap()
    y_out = dt("y", [TOWN, D], F32, kind="ExternalOutput").ap()

    din = lambda n, s, dtp=F32: dt(n, s, dtp, kind="Internal").ap()
    sp_xn = din("sp_xn", [D, TC])
    sp_k = din("sp_k", [D, TWARM + TOWN])
    sp_v = din("sp_v", [D, TWARM + TOWN])
    sp_r = din("sp_r", [D, TOWN])
    sp_g = din("sp_g", [D, TOWN])
    sp_s = din("sp_s", [D, TOWN])
    sp_upr = din("sp_upr", [FF, TOWN])
    sp_gater = din("sp_gater", [FF, TOWN])
    sp_upq = din("sp_upq", [FF, TOWN], I16)
    sp_gs = din("sp_gs", [FF, TOWN])
    sp_o = din("sp_o", [FF, TOWN])
    sp_wd = din("sp_wd", [FF, D])
    cc_in = [din(f"cc_in{i}", [1, 2]) for i in range(4)]
    cc_out = [din(f"cc_out{i}", [1, 2]) for i in range(4)]

    def alloc(name, cols, dtp=F32, parts=128):
        return nc.alloc_sbuf_tensor(name, [parts, cols], dtp).ap()

    ident = alloc("c_ident", 128)
    mask64 = alloc("c_mask", 64, parts=64)
    ones_row = alloc("c_ones", 128, parts=1)
    meanw = alloc("c_meanw", 1)
    vecs = alloc("c_vecs", 7 * NFT)

    e_c = alloc("c_e", 2 * NFT)
    iota_bc = alloc("c_iotabc", 64)
    SE = alloc("c_SE", 512)
    M_km = alloc("c_Mkm", 512)
    M_out = alloc("c_Mout", 512)
    M_stL = alloc("c_MstL", 512)
    M_k2L = alloc("c_Mk2L", D)
    SCAL = alloc("c_scal", 32)
    cEPS = alloc("c_eps", 1)
    AMAX = alloc("c_amax", 8)
    rowtmp = alloc("c_rowtmp", 1024, parts=1)
    St = alloc("St", 512)
    stats = alloc("p_stats", 16)

    B_W = alloc("b_w", NFT * D)
    B0 = alloc("b_0", NFT * (TB + 1))
    B1 = alloc("b_1", NFT * TB)
    B2 = alloc("b_2", NFT * TB)
    B3 = alloc("b_3", NFT * TB)
    B4 = alloc("b_4", NFT * TB)
    B5 = alloc("b_5", NFT * TB)
    B6 = alloc("b_6", NFT * TB)
    B7 = alloc("b_7", NFT * TB)
    B8 = alloc("b_8", 4 * D)
    B9 = alloc("b_9", 2 * D)

    pb = [nc.alloc_psum_tensor(f"pb{i}", [128, 512], F32).ap() for i in range(8)]

    em = Emitter(nc)
    for _s in DSEMS:
        em.new_sem(_s)
    ST = em.new_sem("st")
    SV = em.new_sem("sv")
    SS = em.new_sem("ss")
    SG = em.new_sem("sg")
    SD = "sd"

    V, Tn, Sc, Gp, Sy = "vector", "tensor", "scalar", "gpsimd", "sync"
    esem = {V: SV, Tn: ST, Sc: SS, Gp: SG, Sy: SD}

    _dma_i = [0]

    def dma(out, in_):
        s = DSEMS[_dma_i[0] % 4]
        _dma_i[0] += 1
        em.op(Sy, lambda E, o=out, i=in_: E.dma_start(out=o, in_=i), inc=s, inc_by=16)
        em.unfenced["sync"] = 0

    def barrier():
        em.barrier()

    def TT(out, a, b, op, eng=V):
        em.op(eng, lambda E, o=out, x=a, y=b: E.tensor_tensor(o, x, y, op=op),
              [a, b], [out])

    def TS(out, a, s1, op0, s2=None, op1=None, eng=V):
        rd = [a] + [s for s in (s1, s2) if isinstance(s, bass.AP)]
        if s2 is None:
            em.op(eng, lambda E, o=out, x=a: E.tensor_scalar(o, x, s1, None, op0=op0),
                  rd, [out])
        else:
            em.op(eng, lambda E, o=out, x=a: E.tensor_scalar(o, x, s1, s2, op0=op0, op1=op1),
                  rd, [out])

    def STT(out, in0, scal, in1, op0, op1, eng=V):
        rd = [in0, in1] + ([scal] if isinstance(scal, bass.AP) else [])
        em.op(eng, lambda E, o=out, a=in0, b=in1: E.scalar_tensor_tensor(
            o, a, scal, b, op0=op0, op1=op1), rd, [out])

    def RED(out, a, op, absval=False, eng=V):
        em.op(eng, lambda E, o=out, x=a: E.tensor_reduce(
            o, x, axis=AX.X, op=op, apply_absolute_value=absval),
            [a], [out])

    def CP(out, a, eng=V):
        em.op(eng, lambda E, o=out, x=a: E.tensor_copy(o, x), [a], [out])

    def ACT(out, a, func, bias=0.0, scale=1.0):
        rd = [a] + ([bias] if isinstance(bias, bass.AP) else [])
        em.op(Sc, lambda E, o=out, x=a: E.activation(o, x, func, bias=bias, scale=scale),
              rd, [out])

    def RECIP(out, a, eng=V):
        em.op(eng, lambda E, o=out, x=a: E.reciprocal(o, x), [a], [out])

    def MM(out, lhsT, rhs, start, stop):
        em.op(Tn, lambda E, o=out, l=lhsT, r=rhs: E.matmul(o, l, r, start=start, stop=stop),
              [lhsT, rhs], [out])

    def TR(out, in_):
        em.op(Tn, lambda E, o=out, i=in_: E.transpose(o, i, ident[:, :]),
              [in_, ident], [out])

    def MEMSET(buf, val=0.0, eng=V):
        em.op(eng, lambda E, o=buf: E.memset(o, val), [], [buf])

    ft_ = lambda buf, j, w: buf[:, j * w:(j + 1) * w]

    dma(ident[:, :], ident_i)
    dma(mask64[0:64, :], mask64_i)
    dma(rowtmp[0:1, 0:64], iota64_i)
    dma(ones_row[0:1, :], ones_row_i)
    dma(meanw[:, 0:1], meanw_i)
    dma(vecs[:, :], vec_cols)
    dma(B8[0:1, 0:D], td_row)
    for e in (V, Sc, Tn):
        em.wait_now(e, SD)

    c_td = vecs[:, 0:NFT]
    c_tmk = vecs[:, NFT:2 * NFT]
    c_tmv = vecs[:, 2 * NFT:3 * NFT]
    c_tmr = vecs[:, 3 * NFT:4 * NFT]
    c_tmg = vecs[:, 4 * NFT:5 * NFT]
    c_gln = vecs[:, 5 * NFT:6 * NFT]
    c_bln = vecs[:, 6 * NFT:7 * NFT]
    tdrow = B8[0:1, 0:D]
    erow = B8[0:1, D:2 * D]
    wLrow = B8[0:1, 2 * D:3 * D]

    MEMSET(cEPS[:, 0:1], EPS)
    ACT(e_c[:, 0:NFT], c_td, AF.Exp)
    ACT(e_c[:, NFT:2 * NFT], e_c[:, 0:NFT], AF.Exp, scale=float(-L))
    ACT(erow, tdrow, AF.Exp)
    ACT(wLrow, erow, AF.Exp, scale=float(-L))
    em.wait_now(Tn, SS)
    MM(pb[0][:, 0:64], ones_row[0:1, :], rowtmp[0:1, 0:64], True, True)
    em.wait_now(V, ST)
    CP(iota_bc[:, :], pb[0][:, 0:64])
    em.wait_now(V, SS)
    for j in range(NFT):
        TS(ft_(SE, j, 64), iota_bc[:, :], e_c[:, j:j + 1], AL.mult)
        TS(ft_(M_stL, j, 64), iota_bc[:, :], 0.0, AL.mult, e_c[:, NFT + j:NFT + j + 1], AL.add)
    em.wait_now(Sc, SV)
    ACT(M_km[:, :], SE[:, :], AF.Exp)
    ACT(M_out[:, :], SE[:, :], AF.Exp, scale=-1.0)
    em.wait_now(Tn, SS)
    MM(pb[1][:, 0:512], ones_row[0:1, :], wLrow[0:1, 0:512], True, True)
    MM(pb[2][:, 0:512], ones_row[0:1, :], wLrow[0:1, 512:1024], True, True)
    em.wait_now(V, ST)
    CP(M_k2L[:, 0:512], pb[1][:, 0:512])
    CP(M_k2L[:, 512:1024], pb[2][:, 0:512])
    MEMSET(St[:, :], 0.0)
    MEMSET(AMAX[:, :], 0.0)
    barrier()

    xt = B_W[:, 0:D]
    xm = B_W[:, D:2 * D]
    sqt = B_W[:, 2 * D:3 * D]
    xnt = B_W[:, 3 * D:4 * D]
    stage = B9[:, 0:D]

    def ln_tiles(src_ap, n_tiles, col0):
        for ti in range(n_tiles):
            dma(xt, src_ap[ti * 128:(ti + 1) * 128, :])
            em.wait_now(V, SD)
            RED(stats[:, 0:1], xt, AL.add)
            TS(stats[:, 1:2], stats[:, 0:1], 1.0 / D, AL.mult)
            TS(xm, xt, stats[:, 1:2], AL.subtract)
            em.wait_now(Sc, SV)
            ACT(sqt, xm, AF.Square)
            em.wait_now(V, SS)
            RED(stats[:, 2:3], sqt, AL.add)
            em.wait_now(Sc, SV)
            ACT(stats[:, 3:4], stats[:, 2:3], AF.Sqrt, bias=cEPS[:, 0:1], scale=1.0 / D)
            em.wait_now(V, SS)
            RECIP(stats[:, 4:5], stats[:, 3:4])
            TS(xnt, xm, stats[:, 4:5], AL.mult)
            RED(stats[:, 5:6], xnt, AL.max, absval=True)
            TT(AMAX[:, 0:1], AMAX[:, 0:1], stats[:, 5:6], AL.max)
            em.wait_now(Tn, SV)
            for j in range(NFT):
                TR(pb[j // 4][:, (j % 4) * 128:(j % 4 + 1) * 128], xnt[:, j * 128:(j + 1) * 128])
            em.wait_now(V, ST)
            CP(stage[:, 0:512], pb[0][:, 0:512])
            CP(stage[:, 512:1024], pb[1][:, 0:512])
            em.wait_now(Sy, SV)
            bc = (col0 + ti) * 128
            dma(sp_xn[:, bc:bc + 128].rearrange("(j f) c -> f j c", f=128),
                stage[:, :].rearrange("p (j c) -> p j c", c=128))
            em.wait_now(V, SD)
            em.wait_now(Tn, SD)
            if ti % 6 == 5:
                barrier()

    ln_tiles(x_warm, cfg.n_warm_tiles, 0)
    ln_tiles(x_own, cfg.n_own_tiles, cfg.n_warm_tiles)
    barrier()

    def cross_part_max(src_col_ap, dst_row_off):
        TR(pb[0][0:1, 0:128], src_col_ap)
        em.wait_now(V, ST)
        CP(rowtmp[0:1, 64:192], pb[0][0:1, 0:128])
        RED(rowtmp[0:1, dst_row_off:dst_row_off + 1], rowtmp[0:1, 64:192], AL.max)
        em.wait_now(Tn, SV)

    def allreduce_scales(idx, src_off, nvals, sc_cols):
        em.wait_now(Sy, SV)
        dma(cc_in[idx][0:1, 0:nvals], rowtmp[0:1, src_off:src_off + nvals])
        em.wait_now(Gp, SD)
        em.op(Gp, lambda E: E.collective_compute(
            "AllReduce", AL.max,
            replica_groups=[list(range(cfg.n_cores))],
            ins=[cc_in[idx][0:1, 0:nvals]],
            outs=[cc_out[idx][0:1, 0:nvals]]), inc=SG)
        em.wait_now(Sy, SG)
        dma(rowtmp[0:1, 200:200 + nvals], cc_out[idx][0:1, 0:nvals])
        em.wait_now(V, SD)
        for c in range(nvals):
            TS(rowtmp[0:1, 210 + c:211 + c], rowtmp[0:1, 200 + c:201 + c],
               1.0 / QMAX, AL.mult, 1e-12, AL.max)
        em.wait_now(Tn, SV)
        for c in range(nvals):
            MM(pb[0][:, c:c + 1], ones_row[0:1, :], rowtmp[0:1, 210 + c:211 + c], True, True)
        em.wait_now(V, ST)
        for c in range(nvals):
            col = sc_cols[c]
            CP(SCAL[:, col:col + 1], pb[0][:, c:c + 1])
            RECIP(SCAL[:, col + 1:col + 2], SCAL[:, col:col + 1])
            TS(SCAL[:, col + 2:col + 3], SCAL[:, col:col + 1], -1.0, AL.mult)

    cross_part_max(AMAX[:, 0:1], 0)
    allreduce_scales(0, 0, 1, [0])
    barrier()

    xn_g = B0
    xmix = B1
    projg = B2
    own0 = NWP + 1
    warm0 = 1

    def load_weightT(wt_ap):
        dma(B_W[:, 0:NFT * D].rearrange("p (j f) -> p j f", f=D),
            wt_ap.rearrange("(j r) f -> r j f", r=128))

    def proj_group(src_col0, n, mix_c, dst, dst0, sig):
        dma(xn_g[:, 0:NFT * (TB + 1)].rearrange("p (j t) -> p j t", t=TB + 1)[:, :, 0:n + 1],
            sp_xn[:, src_col0 - 1:src_col0 + n].rearrange("(j f) c -> f j c", f=128))
        em.wait_now(V, SD)
        for j in range(NFT):
            xg = ft_(xn_g, j, TB + 1)
            mx = ft_(xmix, j, TB)[:, 0:n]
            TT(mx, xg[:, 0:n], xg[:, 1:n + 1], AL.subtract)
            STT(mx, mx, mix_c[:, j:j + 1], xg[:, 1:n + 1], AL.mult, AL.add)
        em.wait_now(Tn, SV)
        for ot in range(NFT):
            bank = pb[ot % 2]
            if ot >= 2:
                em.wait_now(Tn, SS if sig else SV)
            for j in range(NFT):
                MM(bank[:, 0:n], ft_(B_W, j, D)[:, ot * 128:(ot + 1) * 128],
                   ft_(xmix, j, TB)[:, 0:n], j == 0, j == NFT - 1)
            po = ft_(projg, ot, TB)[:, 0:n]
            if sig:
                em.wait_now(Sc, ST)
                ACT(po, bank[:, 0:n], AF.Sigmoid)
            else:
                em.wait_now(V, ST)
                CP(po, bank[:, 0:n])
        em.wait_now(Sy, SS if sig else SV)
        dma(dst[:, dst0:dst0 + n].rearrange("(j f) c -> f j c", f=128),
            projg[:, 0:NFT * TB].rearrange("p (j t) -> p j t", t=TB)[:, :, 0:n])
        em.wait_now(Tn, SD)
        em.wait_now(V, SD) if not sig else em.wait_now(Sc, SD)

    for (wn, spill, mix_c, sig) in (("WkT", sp_k, c_tmk, False), ("WvT", sp_v, c_tmv, False),
                                    ("WrT", sp_r, c_tmr, False), ("WgT", sp_g, c_tmg, True)):
        load_weightT(WT[wn])
        em.wait_now(Tn, SD)
        kv = wn in ("WkT", "WvT")
        if kv and TWARM > 0:
            for gc in range(0, TWARM, TB):
                proj_group(warm0 + gc, min(TB, TWARM - gc), mix_c, spill, gc, sig)
        for gc in range(0, TOWN, TB):
            proj_group(own0 + gc, min(TB, TOWN - gc), mix_c, spill,
                       (TWARM + gc) if kv else gc, sig)
        barrier()

    load_weightT(WT["WoT"])
    barrier()

    k_g, v_g, r_g, gg_g = B3, B4, B5, B6
    kt_sub = B8[:, 3 * D:4 * D]
    ktm = B8[:, 0:D]
    vtm = B8[:, D:2 * D]
    k2tm = B8[:, 2 * D:3 * D]
    Atl = B1
    attn_g = B2
    y_g = B7
    ym_g = B0
    s_g = B1

    def mixer_group(kvc0, n, own, ownc0=0):
        dma(k_g[:, 0:NFT * TB].rearrange("p (j t) -> p j t", t=TB)[:, :, 0:n],
            sp_k[:, kvc0:kvc0 + n].rearrange("(j f) c -> f j c", f=128))
        dma(v_g[:, 0:NFT * TB].rearrange("p (j t) -> p j t", t=TB)[:, :, 0:n],
            sp_v[:, kvc0:kvc0 + n].rearrange("(j f) c -> f j c", f=128))
        if own:
            dma(r_g[:, 0:NFT * TB].rearrange("p (j t) -> p j t", t=TB)[:, :, 0:n],
                sp_r[:, ownc0:ownc0 + n].rearrange("(j f) c -> f j c", f=128))
        em.wait_now(V, SD)
        em.wait_now(Tn, SD)
        for sg in range(n // 128):
            t0g = sg * 128
            for j in range(NFT):
                kv_ = ft_(k_g, j, TB)[:, t0g:t0g + 128].rearrange("p (c s) -> p c s", s=L)
                mv_ = ft_(M_km, j, L).unsqueeze(1).broadcast_to([128, 2, L])
                ov_ = kt_sub[:, j * 128:(j + 1) * 128].rearrange("p (c s) -> p c s", s=L)
                em.op(V, lambda E, o=ov_, a=kv_, b=mv_: E.tensor_tensor(o, a, b, op=AL.mult),
                      [ft_(k_g, j, TB)], [kt_sub])
            em.wait_now(Tn, SV)
            for j in range(NFT):
                TR(pb[2 + j // 4][:, (j % 4) * 128:(j % 4 + 1) * 128],
                   kt_sub[:, j * 128:(j + 1) * 128])
            em.wait_now(V, ST)
            CP(ktm[:, 0:512], pb[2][:, 0:512])
            CP(ktm[:, 512:1024], pb[3][:, 0:512])
            em.wait_now(Tn, SV)
            for j in range(NFT):
                TR(pb[2 + j // 4][:, (j % 4) * 128:(j % 4 + 1) * 128],
                   ft_(v_g, j, TB)[:, t0g:t0g + 128])
            em.wait_now(V, ST)
            CP(vtm[:, 0:512], pb[2][:, 0:512])
            CP(vtm[:, 512:1024], pb[3][:, 0:512])
            TT(k2tm[:, :], ktm[:, :], M_k2L[:, :], AL.mult)
            em.wait_now(Tn, SV)
            for ci in range(2):
                cp_ = ci * 64
                tok0 = t0g + ci * 64
                if own:
                    for j in range(NFT):
                        for hp in range(2):
                            MM(pb[4 + j // 4][cp_:cp_ + 64,
                                              (j % 4) * 128 + hp * 64:(j % 4) * 128 + hp * 64 + 64],
                               ft_(v_g, j, TB)[hp * 64:hp * 64 + 64, tok0:tok0 + 64],
                               ft_(r_g, j, TB)[hp * 64:hp * 64 + 64, tok0:tok0 + 64],
                               True, True)
                    em.wait_now(V, ST)
                    for b in range(2):
                        pv = pb[4 + b][cp_:cp_ + 64, 0:512].rearrange("p (k s) -> p k s", s=64)
                        mk = mask64[0:64, :].unsqueeze(1).broadcast_to([64, 8, 64])
                        ov = Atl[cp_:cp_ + 64, b * 512:(b + 1) * 512].rearrange("p (k s) -> p k s", s=64)
                        em.op(V, lambda E, o=ov, a=pv, m=mk: E.tensor_tensor(o, a, m, op=AL.mult),
                              [pb[4 + b]], [Atl])
                    em.wait_now(Tn, SV)
                    for j in range(NFT):
                        for hp in range(2):
                            h = 2 * j + hp
                            MM(pb[6][hp * 64:hp * 64 + 64, j * 64:j * 64 + 64],
                               ktm[cp_:cp_ + 64, h * 64:h * 64 + 64],
                               Atl[cp_:cp_ + 64, j * 128 + hp * 64:j * 128 + hp * 64 + 64],
                               True, False)
                            MM(pb[6][hp * 64:hp * 64 + 64, j * 64:j * 64 + 64],
                               St[hp * 64:hp * 64 + 64, j * 64:j * 64 + 64],
                               ft_(r_g, j, TB)[hp * 64:hp * 64 + 64, tok0:tok0 + 64],
                               False, True)
                    em.wait_now(V, ST)
                    for j in range(NFT):
                        att = ft_(attn_g, j, TB)
                        for hp in range(2):
                            TT(att[hp * 64:hp * 64 + 64, tok0:tok0 + 64],
                               pb[6][hp * 64:hp * 64 + 64, j * 64:j * 64 + 64],
                               ft_(M_out, j, L)[hp * 64:hp * 64 + 64, :], AL.mult)
                em.wait_now(Tn, SV)
                for j in range(NFT):
                    for hp in range(2):
                        h = 2 * j + hp
                        MM(pb[7][hp * 64:hp * 64 + 64, j * 64:j * 64 + 64],
                           vtm[cp_:cp_ + 64, h * 64:h * 64 + 64],
                           k2tm[cp_:cp_ + 64, h * 64:h * 64 + 64],
                           True, True)
                em.wait_now(V, ST)
                TT(St[:, :], St[:, :], M_stL[:, :], AL.mult)
                TT(St[:, :], St[:, :], pb[7][:, :], AL.add)
                em.wait_now(Tn, SV)

    if TWARM > 0:
        for gc in range(0, TWARM, TB):
            mixer_group(gc, min(TB, TWARM - gc), own=False)
            barrier()

    for gc in range(0, TOWN, TB):
        n = min(TB, TOWN - gc)
        mixer_group(TWARM + gc, n, own=True, ownc0=gc)
        dma(gg_g[:, 0:NFT * TB].rearrange("p (j t) -> p j t", t=TB)[:, :, 0:n],
            sp_g[:, gc:gc + n].rearrange("(j f) c -> f j c", f=128))
        em.wait_now(V, SD)
        for j in range(NFT):
            TT(ft_(attn_g, j, TB)[:, 0:n], ft_(attn_g, j, TB)[:, 0:n],
               ft_(gg_g, j, TB)[:, 0:n], AL.mult)
        em.wait_now(Tn, SV)
        for ot in range(NFT):
            bank = pb[ot % 2]
            if ot >= 2:
                em.wait_now(Tn, SV)
            for j in range(NFT):
                MM(bank[:, 0:n], ft_(B_W, j, D)[:, ot * 128:(ot + 1) * 128],
                   ft_(attn_g, j, TB)[:, 0:n], j == 0, j == NFT - 1)
            em.wait_now(V, ST)
            CP(ft_(y_g, ot, TB)[:, 0:n], bank[:, 0:n])
        em.wait_now(Tn, SV)
        for j in range(NFT):
            MM(pb[2][0:1, 0:n], meanw[:, 0:1], ft_(y_g, j, TB)[:, 0:n], j == 0, j == NFT - 1)
        em.wait_now(V, ST)
        CP(rowtmp[0:1, 500:500 + n], pb[2][0:1, 0:n])
        em.wait_now(Tn, SV)
        MM(pb[3][:, 0:n], ones_row[0:1, :], rowtmp[0:1, 500:500 + n], True, True)
        em.wait_now(V, ST)
        for j in range(NFT):
            TT(ft_(ym_g, j, TB)[:, 0:n], ft_(y_g, j, TB)[:, 0:n], pb[3][:, 0:n], AL.subtract)
        em.wait_now(Sc, SV)
        for j in range(NFT):
            ACT(ft_(s_g, j, TB)[:, 0:n], ft_(ym_g, j, TB)[:, 0:n], AF.Square)
        em.wait_now(Tn, SS)
        for j in range(NFT):
            MM(pb[2][0:1, 0:n], meanw[:, 0:1], ft_(s_g, j, TB)[:, 0:n], j == 0, j == NFT - 1)
        em.wait_now(Sc, ST)
        ACT(rowtmp[0:1, 300:300 + n], pb[2][0:1, 0:n], AF.Sqrt, bias=cEPS[0:1, 0:1])
        em.wait_now(V, SS)
        RECIP(rowtmp[0:1, 300:300 + n], rowtmp[0:1, 300:300 + n])
        em.wait_now(Tn, SV)
        MM(pb[3][:, 0:n], ones_row[0:1, :], rowtmp[0:1, 300:300 + n], True, True)
        em.wait_now(V, ST)
        for j in range(NFT):
            sj = ft_(s_g, j, TB)[:, 0:n]
            TT(sj, ft_(ym_g, j, TB)[:, 0:n], pb[3][:, 0:n], AL.mult)
            TS(sj, sj, c_gln[:, j:j + 1], AL.mult, c_bln[:, j:j + 1], AL.add)
        em.wait_now(Sy, SV)
        dma(sp_s[:, gc:gc + n].rearrange("(j f) c -> f j c", f=128),
            s_g[:, 0:NFT * TB].rearrange("p (j t) -> p j t", t=TB)[:, :, 0:n])
        barrier()

    assert NFT * TB >= 2 * TOWN, "buffer layout assumes TB*8 >= 2*TOWN"
    xq_bufs = [B3, B4, B5, B6]
    def xq_t(j):
        return xq_bufs[j // 2][:, (j % 2) * TOWN:(j % 2 + 1) * TOWN]

    tz = B0[:, 0:TOWN]
    tg = B0[:, TOWN:2 * TOWN]
    ti16 = B1[:, 0:(TOWN + 1) // 2].bitcast(I16)[:, 0:TOWN]
    tu16 = B1[:, 0:(TOWN + 1) // 2].bitcast(U16)[:, 0:TOWN]

    def quant4(out, x, scol, n, eng=V, z=None, g=None, i16=None):
        s = SCAL[:, scol:scol + 1]
        rs = SCAL[:, scol + 1:scol + 2]
        ns = SCAL[:, scol + 2:scol + 3]
        z = z if z is not None else tz
        g = g if g is not None else tg
        i16 = i16 if i16 is not None else ti16
        TS(z[:, 0:n], x, rs, AL.mult, eng=eng)
        TS(i16[:, 0:n], x, rs, AL.mult, eng=eng)
        TT(g[:, 0:n], z[:, 0:n], i16[:, 0:n], AL.subtract, eng=eng)
        STT(z[:, 0:n], g[:, 0:n], 0.5, i16[:, 0:n], AL.is_gt, AL.add, eng=eng)
        STT(g[:, 0:n], g[:, 0:n], -0.5, z[:, 0:n], AL.is_ge, AL.add, eng=eng)
        TS(out, g[:, 0:n], s, AL.mult, ns, AL.add, eng=eng)

    def gp_round_to_int(out_i16, z, g, r0f, n):
        TS(out_i16[:, 0:n], z[:, 0:n], 1.0, AL.mult, eng=Gp)
        TS(r0f[:, 0:n], out_i16[:, 0:n], 1.0, AL.mult, eng=Gp)
        TT(g[:, 0:n], z[:, 0:n], r0f[:, 0:n], AL.subtract, eng=Gp)
        TS(z[:, 0:n], g[:, 0:n], 0.5, AL.is_gt, eng=Gp)
        TT(r0f[:, 0:n], z[:, 0:n], r0f[:, 0:n], AL.add, eng=Gp)
        TS(g[:, 0:n], g[:, 0:n], -0.5, AL.is_ge, eng=Gp)
        TT(g[:, 0:n], g[:, 0:n], r0f[:, 0:n], AL.add, eng=Gp)
        TS(z[:, 0:n], g[:, 0:n], -1.0, AL.add, eng=Gp)

    def v_round_i16(out_i16, z, g, n):
        TS(out_i16[:, 0:n], z[:, 0:n], 1.0, AL.mult, eng=V)
        TT(g[:, 0:n], z[:, 0:n], out_i16[:, 0:n], AL.subtract, eng=V)
        STT(z[:, 0:n], g[:, 0:n], 0.5, out_i16[:, 0:n], AL.is_gt, AL.add, eng=V)
        STT(g[:, 0:n], g[:, 0:n], -0.5, z[:, 0:n], AL.is_ge, AL.add, eng=V)
        TS(out_i16[:, 0:n], g[:, 0:n], 1.0, AL.mult, -1.0, AL.add, eng=V)

    for bidx, buf in enumerate(xq_bufs):
        dma(buf[:, 0:2 * TOWN].rearrange("p (j t) -> p j t", t=TOWN),
            sp_xn[bidx * 256:(bidx + 1) * 256, own0:own0 + TOWN]
            .rearrange("(j f) c -> f j c", f=128))
    em.wait_now(V, SD)
    for j in range(NFT):
        quant4(xq_t(j), xq_t(j), 0, TOWN)
    barrier()

    wnat = B1
    wdqT = B7
    ustg4 = B2[:, 0:4 * FTB]
    wsc = B2[:, 4 * FTB:4 * FTB + 224]
    wz = B0[:, 0:FF]
    wgs = B8[:, 0:FF]
    wi16 = B9[:, 0:FF // 2].bitcast(I16)[:, 0:FF]

    def dequant_tile(wtile, nblk):
        bm = wsc[:, 0:nblk]
        sw = wsc[:, 96:96 + nblk]
        em.op(V, lambda E, o=bm.unsqueeze(2), x=wtile.rearrange("p (b s) -> p b s", s=32):
              E.tensor_reduce(o, x, axis=AX.X, op=AL.max, apply_absolute_value=True),
              [wtile], [wsc])
        TS(sw, bm, 1.0 / 7.0, AL.mult, 1e-12, AL.max)
        rw = bm
        RECIP(rw, sw)
        rwb = rw.unsqueeze(2).broadcast_to([128, nblk, 32])
        swb = sw.unsqueeze(2).broadcast_to([128, nblk, 32])
        nw = nblk * 32
        wv = wtile.rearrange("p (b s) -> p b s", s=32)
        zv = wz[:, 0:nw].rearrange("p (b s) -> p b s", s=32)
        i16v = wi16[:, 0:nw].rearrange("p (b s) -> p b s", s=32)
        em.op(V, lambda E, o=zv, a=wv, b=rwb: E.tensor_tensor(o, a, b, op=AL.mult),
              [wtile, wsc], [wz])
        em.op(V, lambda E, o=i16v, a=wv, b=rwb: E.tensor_tensor(o, a, b, op=AL.mult),
              [wtile, wsc], [wi16])
        TT(wgs[:, 0:nw], wz[:, 0:nw], wi16[:, 0:nw], AL.subtract)
        STT(wz[:, 0:nw], wgs[:, 0:nw], 0.5, wi16[:, 0:nw], AL.is_gt, AL.add)
        STT(wgs[:, 0:nw], wgs[:, 0:nw], -0.5, wz[:, 0:nw], AL.is_ge, AL.add)
        TS(wz[:, 0:nw], wgs[:, 0:nw], 1.0, AL.mult, -1.0, AL.add)
        ov = wtile.rearrange("p (b s) -> p b s", s=32)
        yv = wz[:, 0:nw].rearrange("p (b s) -> p b s", s=32)
        em.op(V, lambda E, o=ov, a=yv, b=swb: E.tensor_tensor(o, a, b, op=AL.mult),
              [wz, wsc], [wtile])

    NOS = NIT // 4

    def ffn_matmul_weight(w_in, spill, amax_col):
        for os_ in range(NOS):
            em.wait_now(Sy, ST)
            dma(wnat[:, 0:4 * D].rearrange("p (q f) -> p q f", f=D),
                w_in[os_ * 512:(os_ + 1) * 512, :].rearrange("(q r) f -> r q f", r=128))
            em.wait_now(V, SD)
            for q in range(4):
                dequant_tile(ft_(wnat, q, D), 32)
                em.wait_now(Tn, SV)
                for j in range(NFT):
                    TR(pb[2 + j // 4][:, (j % 4) * 128:(j % 4 + 1) * 128],
                       ft_(wnat, q, D)[:, j * 128:(j + 1) * 128])
                em.wait_now(V, ST)
                for half in range(2):
                    for j4 in range(4):
                        j = half * 4 + j4
                        CP(wdqT[:, j * 512 + q * 128:j * 512 + (q + 1) * 128],
                           pb[2 + half][:, j4 * 128:(j4 + 1) * 128])
                em.wait_now(Tn, SV)
            for tb in range(NFTB):
                em.wait_now(V, SD)
                for q in range(4):
                    bank = pb[q % 2]
                    if q >= 2 or tb > 0:
                        em.wait_now(Tn, SV)
                    for j in range(NFT):
                        MM(bank[:, 0:FTB], wdqT[:, j * 512 + q * 128:j * 512 + (q + 1) * 128],
                           xq_t(j)[:, tb * FTB:(tb + 1) * FTB], j == 0, j == NFT - 1)
                    em.wait_now(V, ST)
                    CP(ustg4[:, q * FTB:(q + 1) * FTB], bank[:, 0:FTB])
                    RED(stats[:, 6:7], ustg4[:, q * FTB:(q + 1) * FTB], AL.max, absval=True)
                    TT(AMAX[:, amax_col:amax_col + 1], AMAX[:, amax_col:amax_col + 1],
                       stats[:, 6:7], AL.max)
                em.wait_now(Sy, SV)
                dma(spill[os_ * 512:(os_ + 1) * 512, tb * FTB:(tb + 1) * FTB]
                    .rearrange("(q r) t -> r q t", r=128),
                    ustg4[:, 0:4 * FTB].rearrange("p (q t) -> p q t", t=FTB))
            barrier()

    ffn_matmul_weight(Wup, sp_upr, 1)
    ffn_matmul_weight(Wgate, sp_gater, 2)

    for mt in range(NFT):
        dma(wnat[:, 0:FF], Wdown[mt * 128:(mt + 1) * 128, :])
        em.wait_now(V, SD)
        dequant_tile(wnat[:, 0:FF], 96)
        em.wait_now(Tn, SV)
        for it in range(NIT):
            TR(pb[2 + (it % 8) // 4][:, (it % 4) * 128:(it % 4 + 1) * 128],
               wnat[:, it * 128:(it + 1) * 128])
            if it % 8 == 7:
                em.wait_now(V, ST)
                for b in range(2):
                    CP(stage[:, b * 512:(b + 1) * 512], pb[2 + b][:, 0:512])
                em.wait_now(Sy, SV)
                dma(sp_wd[(it - 7) * 128:(it + 1) * 128, mt * 128:(mt + 1) * 128]
                    .rearrange("(k r) c -> r k c", r=128),
                    stage[:, 0:1024].rearrange("p (k c) -> p k c", c=128))
                em.wait_now(Tn, SD)
                em.wait_now(V, SD)
        if mt % 3 == 2:
            barrier()
    barrier()

    cross_part_max(AMAX[:, 1:2], 0)
    cross_part_max(AMAX[:, 2:3], 1)
    allreduce_scales(1, 0, 2, [3, 6])
    barrier()

    class QSet:
        pass
    sV = QSet()
    sV.upt = B3[:, 0:TOWN]; sV.gat = B3[:, TOWN:2 * TOWN]
    sV.sig = B4[:, 0:TOWN]; sV.gst = B4[:, TOWN:2 * TOWN]
    sV.z = B6[:, 0:TOWN]; sV.g = B6[:, TOWN:2 * TOWN]
    sV.i16 = B5[:, 0:TOWN // 2].bitcast(I16)[:, 0:TOWN]
    sV.u16 = B5[:, TOWN // 2:TOWN].bitcast(U16)[:, 0:TOWN]
    sV.gi16 = B5[:, TOWN:TOWN + TOWN // 2].bitcast(I16)[:, 0:TOWN]
    sV.r0f = None
    sG = QSet()
    sG.upt = B0[:, 0:TOWN]; sG.gat = B0[:, TOWN:2 * TOWN]
    sG.sig = B1[:, 0:TOWN]; sG.gst = B1[:, TOWN:2 * TOWN]
    sG.z = B2[:, 0:TOWN]; sG.g = B2[:, TOWN:2 * TOWN]
    sG.i16 = B7[:, 0:TOWN // 2].bitcast(I16)[:, 0:TOWN]
    sG.u16 = B7[:, TOWN // 2:TOWN].bitcast(U16)[:, 0:TOWN]
    sG.gi16 = B7[:, TOWN:TOWN + TOWN // 2].bitcast(I16)[:, 0:TOWN]
    sG.r0f = B9[:, 0:TOWN]

    qlast = {id(sV): None, id(sG): None}

    def qs_wait_reuse(S, eng):
        snap = qlast[id(S)]
        if snap is not None:
            ep, cnts = snap
            if ep == em.nbar:
                for s_, v_ in cnts.items():
                    em.wait(Sy, s_, v_)

    def qs_record(S, eng):
        es = esem_name[eng]
        if em.unfenced[SEM_OWNER[es]] > 0:
            em.fence(SEM_OWNER[es], es)
        if em.unfenced["scalar"] > 0:
            em.fence("scalar", "ss")
        if em.unfenced["vector"] > 0:
            em.fence("vector", "sv")
        snap = {es: em.counts[es], "ss": em.counts["ss"], "sv": em.counts["sv"]}
        for ds in DSEMS:
            snap[ds] = em.counts[ds]
        qlast[id(S)] = (em.nbar, snap)

    esem_name = {V: "sv", Gp: "sg", Sc: "ss"}

    def p5_itile(it, eng, S):
        r0_, r1_ = it * 128, it * 128 + 128
        qs_wait_reuse(S, eng)
        dma(S.upt, sp_upr[r0_:r1_, :])
        dma(S.gat, sp_gater[r0_:r1_, :])
        em.wait_now(eng, SD)
        rs2 = SCAL[:, 4:5]; rs3 = SCAL[:, 7:8]; s3 = SCAL[:, 6:7]
        if eng == Gp:
            TT(S.z, S.upt, rs2.broadcast_to([128, TOWN]), AL.mult, eng=Gp)
            gp_round_to_int(S.i16, S.z, S.g, S.r0f, TOWN)
            TS(S.i16, S.z, 1.0, AL.mult, eng=Gp)
        else:
            TS(S.z, S.upt, rs2, AL.mult, eng=V)
            v_round_i16(S.i16, S.z, S.g, TOWN)
        em.wait_now(Sy, esem_name[eng] if False else ("sg" if eng == Gp else "sv"))
        dma(sp_upq[r0_:r1_, :], S.i16)
        if eng == Gp:
            TT(S.z, S.gat, rs3.broadcast_to([128, TOWN]), AL.mult, eng=Gp)
            gp_round_to_int(S.gi16, S.z, S.g, S.r0f, TOWN)
            TT(S.gat, S.z, s3.broadcast_to([128, TOWN]), AL.mult, eng=Gp)
        else:
            TS(S.z, S.gat, rs3, AL.mult, eng=V)
            v_round_i16(S.gi16, S.z, S.g, TOWN)
            TS(S.gat, S.gi16, s3, AL.mult, eng=V)
        em.wait_now(Sc, "sg" if eng == Gp else "sv")
        ACT(S.sig, S.gat, AF.Sigmoid)
        em.wait_now(eng, SS)
        if eng == Gp:
            TS(S.z, S.sig, 65536.0, AL.mult, eng=Gp)
            TS(S.z, S.z, 65535.0, AL.min, eng=Gp)
            TS(S.u16, S.z, 1.0, AL.mult, eng=Gp)
            TS(S.r0f, S.u16, 1.0, AL.mult, eng=Gp)
            TT(S.g, S.z, S.r0f, AL.subtract, eng=Gp)
            TS(S.z, S.g, 0.5, AL.is_gt, eng=Gp)
            TT(S.r0f, S.z, S.r0f, AL.add, eng=Gp)
            TS(S.g, S.g, -0.5, AL.is_ge, eng=Gp)
            TT(S.g, S.g, S.r0f, AL.add, eng=Gp)
            TS(S.g, S.g, -1.0, AL.add, eng=Gp)
            TS(S.sig, S.g, 1.0 / 65536.0, AL.mult, eng=Gp)
        else:
            TS(S.z, S.sig, 65536.0, AL.mult, 65535.0, AL.min, eng=V)
            TS(S.u16, S.z, 1.0, AL.mult, eng=V)
            TT(S.g, S.z, S.u16, AL.subtract, eng=V)
            STT(S.z, S.g, 0.5, S.u16, AL.is_gt, AL.add, eng=V)
            STT(S.g, S.g, -0.5, S.z, AL.is_ge, AL.add, eng=V)
            TS(S.sig, S.g, 1.0 / 65536.0, AL.mult, -1.0 / 65536.0, AL.add, eng=V)
        TT(S.gst, S.gat, S.sig, AL.mult, eng=eng)
        if eng == Gp:
            em.wait_now(V, "sg")
        RED(stats[:, 7:8], S.gst, AL.max, absval=True, eng=V)
        TT(AMAX[:, 3:4], AMAX[:, 3:4], stats[:, 7:8], AL.max, eng=V)
        em.wait_now(Sy, "sg" if eng == Gp else "sv")
        dma(sp_gs[r0_:r1_, :], S.gst)
        qs_record(S, eng)

    def p6_itile(it, eng, S):
        r0_, r1_ = it * 128, it * 128 + 128
        qs_wait_reuse(S, eng)
        dma(S.gst, sp_gs[r0_:r1_, :])
        dma(S.i16, sp_upq[r0_:r1_, :])
        em.wait_now(eng, SD)
        rs4 = SCAL[:, 10:11]; s4 = SCAL[:, 9:10]; s2 = SCAL[:, 3:4]
        if eng == Gp:
            TT(S.z, S.gst, rs4.broadcast_to([128, TOWN]), AL.mult, eng=Gp)
            gp_round_to_int(S.gi16, S.z, S.g, S.r0f, TOWN)
            TT(S.gat, S.z, s4.broadcast_to([128, TOWN]), AL.mult, eng=Gp)
            TS(S.z, S.i16, 1.0, AL.mult, eng=Gp)
            TT(S.upt, S.z, s2.broadcast_to([128, TOWN]), AL.mult, eng=Gp)
        else:
            TS(S.z, S.gst, rs4, AL.mult, eng=V)
            v_round_i16(S.gi16, S.z, S.g, TOWN)
            TS(S.gat, S.gi16, s4, AL.mult, eng=V)
            TS(S.upt, S.i16, s2, AL.mult, eng=V)
        TT(S.gst, S.gat, S.upt, AL.mult, eng=eng)
        if eng == Gp:
            em.wait_now(V, "sg")
        RED(stats[:, 8:9], S.gst, AL.max, absval=True, eng=V)
        TT(AMAX[:, 4:5], AMAX[:, 4:5], stats[:, 8:9], AL.max, eng=V)
        em.wait_now(Sy, "sg" if eng == Gp else "sv")
        dma(sp_o[r0_:r1_, :], S.gst)
        qs_record(S, eng)

    for it in range(NIT):
        p5_itile(it, V, sV if it % 2 == 0 else sG)
        if it % 3 == 2:
            barrier()

    barrier()
    cross_part_max(AMAX[:, 3:4], 0)
    allreduce_scales(2, 0, 1, [9])
    barrier()

    for it in range(NIT):
        p6_itile(it, V, sV if it % 2 == 0 else sG)
        if it % 3 == 2:
            barrier()

    barrier()
    cross_part_max(AMAX[:, 4:5], 0)
    allreduce_scales(3, 0, 1, [12])
    barrier()

    oq4 = B3[:, 0:4 * FTB]
    oq = [oq4[:, q * FTB:(q + 1) * FTB] for q in range(4)]
    wd = B7
    g2t = B5
    st_ = B6
    xrow = B0[:, 0:D]
    orow = B0[:, D:2 * D]
    prodst = B1

    for tb in range(NFTB):
        c0 = tb * FTB
        for isl in range(NOS):
            em.wait_now(Sy, ST)
            em.wait_now(V, ST)
            dma(oq4.rearrange("p (q t) -> p q t", t=FTB),
                sp_o[isl * 512:(isl + 1) * 512, c0:c0 + FTB]
                .rearrange("(q r) t -> r q t", r=128))
            dma(wd[:, 0:4 * D].rearrange("p (q f) -> p q f", f=D),
                sp_wd[isl * 512:(isl + 1) * 512, :].rearrange("(q r) f -> r q f", r=128))
            em.wait_now(V, SD)
            for q in range(4):
                quant4(oq[q], oq[q], 12, FTB, z=B6[:, 0:FTB], g=B6[:, FTB:2 * FTB],
                       i16=B5[:, 0:FTB // 2].bitcast(I16)[:, 0:FTB])
            em.wait_now(Tn, SV)
            for m in range(NFT):
                for q in range(4):
                    MM(pb[m][:, 0:FTB], ft_(wd, q, D)[:, m * 128:(m + 1) * 128],
                       oq[q], isl == 0 and q == 0, isl == NOS - 1 and q == 3)
            em.wait_now(V, ST)
            em.wait_now(Sy, ST)
        barrier()
        em.wait_now(Sc, ST)
        for m in range(NFT):
            ACT(ft_(g2t, m, FTB), pb[m][:, 0:FTB], AF.Silu)
        dma(st_[:, 0:NFT * FTB].rearrange("p (j t) -> p j t", t=FTB),
            sp_s[:, c0:c0 + FTB].rearrange("(j f) c -> f j c", f=128))
        em.wait_now(V, SD)
        em.wait_now(V, SS)
        for j in range(NFT):
            TT(ft_(g2t, j, FTB), ft_(st_, j, FTB), ft_(g2t, j, FTB), AL.mult)
        em.wait_now(Tn, SV)
        for tt in range(FTB // 128):
            for j in range(NFT):
                TR(pb[j // 4][:, (j % 4) * 128:(j % 4 + 1) * 128],
                   ft_(g2t, j, FTB)[:, tt * 128:(tt + 1) * 128])
            em.wait_now(V, ST)
            CP(prodst[:, 0:512], pb[0][:, 0:512])
            CP(prodst[:, 512:1024], pb[1][:, 0:512])
            row0 = c0 + tt * 128
            dma(xrow, x_own[1 + row0:1 + row0 + 128, :])
            em.wait_now(V, SD)
            TT(orow, prodst[:, 0:D], xrow, AL.add)
            em.wait_now(Sy, SV)
            dma(y_out[row0:row0 + 128, :], orow)
            em.wait_now(V, SD)
            em.wait_now(Tn, SD)

    if debug:
        dbg_specs = [("d_scal", SCAL[:, 0:32], [128, 32], None),
                     ("d_amax", AMAX[:, 0:8], [128, 8], None)]
        for nm, ap_, shp, _ in dbg_specs:
            od = dt(nm, shp, F32, kind="ExternalOutput").ap()
            dma(od, ap_)
        for nm, sp, shp in [("d_xn", sp_xn, [D, TC]), ("d_s", sp_s, [D, TOWN]),
                            ("d_upr", sp_upr, [FF, TOWN]), ("d_gater", sp_gater, [FF, TOWN]),
                            ("d_gs", sp_gs, [FF, TOWN]), ("d_o", sp_o, [FF, TOWN]),
                            ("d_k", sp_k, [D, TWARM + TOWN]), ("d_r", sp_r, [D, TOWN]),
                            ("d_g", sp_g, [D, TOWN])]:
            od = dt(nm, shp, F32, kind="ExternalOutput").ap()
            dma(od, sp)
    em.wait_now(Sy, SD)
    barrier()
    return nc, em


def finalize(nc, em):
    ctxs = []
    sems = {}
    esems = {}
    for name in em.counts:
        h = nc.semaphore(name)
        sems[name] = h.__enter__()
        ctxs.append(h)
    for e in ENGINES:
        h = nc.semaphore("ep_" + e)
        esems[e] = h.__enter__()
        ctxs.append(h)
    blk = nc.Block()
    block = blk.__enter__()
    em.replay(block, sems, esems, SEM_OWNER)
    blk.__exit__(None, None, None)
    for h in reversed(ctxs):
        h.__exit__(None, None, None)
    return nc


def make_core_inputs(x_full, params, core, n_cores, cfg: Cfg):
    TOWN, TWARM = cfg.TOWN, cfg.TWARM
    t0 = core * TOWN
    xo = np.zeros((cfg.NOP, D), np.float32)
    xo[1:1 + TOWN] = x_full[t0:t0 + TOWN]
    xo[0] = x_full[1] if core == 0 else x_full[t0 - 1]
    xw = np.zeros((cfg.NWP, D), np.float32)
    if core > 0 and TWARM > 0:
        xw[1:1 + TWARM] = x_full[t0 - TWARM:t0]
        xw[0] = x_full[t0 - TWARM - 1] if t0 - TWARM - 1 >= 0 else 0.0
    vc = np.stack([params[k].reshape(NFT, 128).T for k in
                   ("td", "tmk", "tmv", "tmr", "tmg", "g_ln", "b_ln")], 1)
    vc = vc.reshape(128, 7 * NFT, order="F") if False else np.concatenate(
        [params[k].reshape(NFT, 128).T for k in
         ("td", "tmk", "tmv", "tmr", "tmg", "g_ln", "b_ln")], axis=1)
    return {
        "x_own": xo, "x_warm": xw,
        "WrT": np.ascontiguousarray(params["Wr"].T),
        "WkT": np.ascontiguousarray(params["Wk"].T),
        "WvT": np.ascontiguousarray(params["Wv"].T),
        "WgT": np.ascontiguousarray(params["Wg"].T),
        "WoT": np.ascontiguousarray(params["Wo"].T),
        "Wup": params["w_up"], "Wgate": params["w_gate"], "Wdown": params["w_down"],
        "vec_cols": np.ascontiguousarray(vc, np.float32),
        "td_row": params["td"].reshape(1, D),
        "ident_c": np.eye(128, dtype=np.float32),
        "mask64_c": np.triu(np.ones((64, 64), np.float32)),
        "iota64_c": np.arange(64, dtype=np.float32).reshape(1, 64),
        "ones_row_c": np.ones((1, 128), np.float32),
        "meanw_c": np.full((128, 1), 1.0 / D, np.float32),
    }


from concourse.bass_utils import run_bass_kernel_spmd

_N_CORES = 8
_CFG = Cfg(TOWN=2048, TWARM=1024, TB=512, n_cores=_N_CORES)
_NC_CACHE = {}


def _get_program():
    if "nc" not in _NC_CACHE:
        nc, em = build(_CFG)
        finalize(nc, em)
        _NC_CACHE["nc"] = nc
    return _NC_CACHE["nc"]


def kernel(L0_attn_x, L0_attn_time_decay, L0_attn_time_mix_k, L0_attn_time_mix_v,
           L0_attn_time_mix_r, L0_attn_time_mix_g, L0_attn_W_r, L0_attn_W_k,
           L0_attn_W_v, L0_attn_W_g, L0_attn_W_o, L0_attn_ln_gamma,
           L0_attn_ln_beta, L0_ffn_w_gate, L0_ffn_w_up, L0_ffn_w_down):
    x_full = np.ascontiguousarray(np.asarray(L0_attn_x, np.float32).reshape(-1, D))
    params = {
        "td": np.asarray(L0_attn_time_decay, np.float32),
        "tmk": np.asarray(L0_attn_time_mix_k, np.float32),
        "tmv": np.asarray(L0_attn_time_mix_v, np.float32),
        "tmr": np.asarray(L0_attn_time_mix_r, np.float32),
        "tmg": np.asarray(L0_attn_time_mix_g, np.float32),
        "Wr": np.asarray(L0_attn_W_r, np.float32),
        "Wk": np.asarray(L0_attn_W_k, np.float32),
        "Wv": np.asarray(L0_attn_W_v, np.float32),
        "Wg": np.asarray(L0_attn_W_g, np.float32),
        "Wo": np.asarray(L0_attn_W_o, np.float32),
        "g_ln": np.asarray(L0_attn_ln_gamma, np.float32),
        "b_ln": np.asarray(L0_attn_ln_beta, np.float32),
        "w_gate": np.asarray(L0_ffn_w_gate, np.float32),
        "w_up": np.asarray(L0_ffn_w_up, np.float32),
        "w_down": np.asarray(L0_ffn_w_down, np.float32),
    }
    nc = _get_program()
    in_maps = [make_core_inputs(x_full, params, c, _N_CORES, _CFG)
               for c in range(_N_CORES)]
    res = run_bass_kernel_spmd(nc, in_maps, core_ids=list(range(_N_CORES)))
    out = np.concatenate([np.asarray(r["y"], np.float32) for r in res.results], 0)
    return out.reshape(1, _N_CORES * _CFG.TOWN, D)


# version 13
# speedup vs baseline: 1.6212x; 1.0147x over previous
import numpy as np
import concourse.bass as bass
import concourse.mybir as mybir
from concourse.alu_op_type import AluOpType as AL

F32 = mybir.dt.float32
I16 = mybir.dt.int16
U16 = mybir.dt.uint16
AF = mybir.ActivationFunctionType
AX = mybir.AxisListType

ENGINES = ("sync", "tensor", "vector", "scalar", "gpsimd")
SEM_OWNER = {"sd0": "sync", "sd1": "sync", "sd2": "sync", "sd3": "sync",
             "st": "tensor", "sv": "vector", "ss": "scalar", "sg": "gpsimd"}
DSEMS = ("sd0", "sd1", "sd2", "sd3")

D = 1024
NFT = 8
H = 16
DH = 64
FF = 3072
NIT = FF // 128
L = 64
QMAX = 32767.0
EPS = 1e-5


class Cfg:
    def __init__(self, TOWN=2048, TWARM=1024, TB=512, n_cores=8):
        assert TB == 512, "buffer layout assumes TB == 512"
        assert TOWN % TB == 0 and (TWARM % TB == 0 or TWARM == 0)
        self.TOWN, self.TWARM, self.TB, self.n_cores = TOWN, TWARM, TB, n_cores
        self.NOP = TOWN + 128
        self.NWP = TWARM + 128 if TWARM > 0 else 128
        self.TC = self.NWP + self.NOP
        self.n_own_tiles = self.NOP // 128
        self.n_warm_tiles = self.NWP // 128
        self.NTB = TOWN // 512 if TOWN % 512 == 0 else None
        assert TOWN % 512 == 0 or TOWN < 512
        self.FTB = 512 if TOWN >= 512 else TOWN
        self.n_ftb = TOWN // self.FTB


class Emitter:

    def __init__(self, nc):
        self.nc = nc
        self.items = {e: [] for e in ENGINES}
        self.counts = {}
        self.touched = {e: [] for e in ENGINES}
        self.unfenced = {e: 0 for e in ENGINES}
        self.nbar = 0
        self.maxcount = {}

    def new_sem(self, name):
        self.counts[name] = 0
        self.maxcount[name] = 0
        return name

    @staticmethod
    def _rng(ap):
        try:
            t = ap.tensor.name
            lo = int(ap.offset)
            sz = 1
            for st, ct in ap.ap:
                sz += abs(int(st)) * (int(ct) - 1)
            return (t, lo, lo + sz)
        except Exception:
            return (str(ap), 0, 1 << 60)

    def _hazard(self, eng, reads, writes):
        tl = self.touched[eng]
        def ov(a, b):
            return a[0] == b[0] and a[1] < b[2] and b[1] < a[2]
        for ap in reads:
            r = self._rng(ap)
            for (k, lo, hi, w) in tl:
                if w and ov(r, (k, lo, hi)):
                    return True
        for ap in writes:
            r = self._rng(ap)
            for (k, lo, hi, w) in tl:
                if ov(r, (k, lo, hi)):
                    return True
        return False

    def _note(self, eng, reads, writes):
        tl = self.touched[eng]
        for ap in reads:
            k, lo, hi = self._rng(ap)
            tl.append((k, lo, hi, False))
        for ap in writes:
            k, lo, hi = self._rng(ap)
            tl.append((k, lo, hi, True))

    def op(self, eng, fn, reads=(), writes=(), inc=None, inc_by=1):
        reads, writes = list(reads), list(writes)
        if eng != "sync" and self._hazard(eng, reads, writes):
            self.items[eng].append(("drain",))
            self.touched[eng] = []
        self._note(eng, reads, writes)
        if inc is not None:
            self.counts[inc] += inc_by
            self.maxcount[inc] = max(self.maxcount[inc], self.counts[inc])
            assert self.counts[inc] <= 240, (
                f"sem {inc} exceeds 240 within an epoch ({self.counts[inc]}); "
                f"insert a barrier (epoch {self.nbar})")
            self.items[eng].append(("op", fn, inc, inc_by))
        else:
            self.items[eng].append(("op", fn, None, 0))
            self.unfenced[eng] += 1

    def fence(self, eng, sem):
        if self.unfenced[eng] == 0:
            return
        self.unfenced[eng] = 0
        self.counts[sem] += 1
        self.maxcount[sem] = max(self.maxcount[sem], self.counts[sem])
        assert self.counts[sem] <= 240, f"sem {sem} epoch overflow"
        self.items[eng].append(("fenceinc", sem))

    def wait(self, eng, sem, val):
        if val > 0:
            self.items[eng].append(("wait", sem, val))

    def wait_now(self, eng, sem):
        if sem == "sd":
            for s in DSEMS:
                self.wait(eng, s, self.counts[s])
            return
        own = SEM_OWNER[sem]
        if self.unfenced[own] > 0:
            self.fence(own, sem)
        self.wait(eng, sem, self.counts[sem])

    def barrier(self):
        tok = dict(self.counts)
        k = self.nbar
        self.nbar += 1
        assert self.nbar <= 125, "too many barriers for 8-bit epoch sems"
        for e in ENGINES:
            self.items[e].append(("barrier", k, tok))
            self.touched[e] = []
            self.unfenced[e] = 0
        for s in self.counts:
            self.counts[s] = 0

    def replay(self, block, sems, esems, owner):
        em = self

        def mk(eng_name):
            def body(proxy):
                last = [None]
                last_inc = [False]

                def emit_fence(s):
                    if last[0] is not None and not last_inc[0]:
                        last[0].then_inc(sems[s], 1)
                        last_inc[0] = True
                    else:
                        proxy.drain()
                        proxy.nop().then_inc(sems[s], 1)
                        last[0] = None
                        last_inc[0] = False

                for it in em.items[eng_name]:
                    kind = it[0]
                    if kind == "drain":
                        proxy.drain()
                        last[0] = None
                        last_inc[0] = False
                    elif kind == "wait":
                        proxy.wait_ge(sems[it[1]], it[2])
                        last[0] = None
                        last_inc[0] = False
                    elif kind == "fenceinc":
                        emit_fence(it[1])
                    elif kind == "ewait":
                        proxy.wait_ge(esems[eng_name], it[1])
                        last[0] = None
                        last_inc[0] = False
                    elif kind == "barrier":
                        k, tok = it[1], it[2]
                        for s, v in tok.items():
                            if v > 0:
                                proxy.wait_ge(sems[s], v)
                        proxy.nop().then_inc(esems[eng_name], 1)
                        for f in ENGINES:
                            proxy.wait_ge(esems[f], 2 * k + 1)
                        for s, own in owner.items():
                            if own == eng_name:
                                proxy.sem_clear(sems[s])
                        proxy.drain()
                        proxy.nop().then_inc(esems[eng_name], 1)
                        for f in ENGINES:
                            proxy.wait_ge(esems[f], 2 * k + 2)
                        last[0] = None
                        last_inc[0] = False
                    else:
                        _, fn, b, c = it
                        instr = fn(proxy)
                        last[0] = instr
                        last_inc[0] = False
                        if b is not None and instr is not None:
                            instr.then_inc(sems[b], c)
            return body

        for e in ENGINES:
            getattr(block, e)(mk(e))


def build(cfg: Cfg, debug=False):
    nc = bass.Bass("TRN2", target_bir_lowering=False)
    TOWN, TWARM, TB = cfg.TOWN, cfg.TWARM, cfg.TB
    NOP, NWP, TC = cfg.NOP, cfg.NWP, cfg.TC
    FTB, NFTB = cfg.FTB, cfg.n_ftb

    dt = nc.dram_tensor
    x_own = dt("x_own", [NOP, D], F32, kind="ExternalInput").ap()
    x_warm = dt("x_warm", [NWP, D], F32, kind="ExternalInput").ap()
    WT = {n: dt(n, [D, D], F32, kind="ExternalInput").ap()
          for n in ("WrT", "WkT", "WvT", "WgT", "WoT")}
    Wup = dt("Wup", [FF, D], F32, kind="ExternalInput").ap()
    Wgate = dt("Wgate", [FF, D], F32, kind="ExternalInput").ap()
    Wdown = dt("Wdown", [D, FF], F32, kind="ExternalInput").ap()
    vec_cols = dt("vec_cols", [128, 7 * NFT], F32, kind="ExternalInput").ap()
    td_row = dt("td_row", [1, D], F32, kind="ExternalInput").ap()
    ident_i = dt("ident_c", [128, 128], F32, kind="ExternalInput").ap()
    mask64_i = dt("mask64_c", [64, 64], F32, kind="ExternalInput").ap()
    iota64_i = dt("iota64_c", [1, 64], F32, kind="ExternalInput").ap()
    ones_row_i = dt("ones_row_c", [1, 128], F32, kind="ExternalInput").ap()
    meanw_i = dt("meanw_c", [128, 1], F32, kind="ExternalInput").ap()
    y_out = dt("y", [TOWN, D], F32, kind="ExternalOutput").ap()

    din = lambda n, s, dtp=F32: dt(n, s, dtp, kind="Internal").ap()
    sp_xn = din("sp_xn", [D, TC])
    sp_k = din("sp_k", [D, TWARM + TOWN])
    sp_v = din("sp_v", [D, TWARM + TOWN])
    sp_r = din("sp_r", [D, TOWN])
    sp_g = din("sp_g", [D, TOWN])
    sp_s = din("sp_s", [D, TOWN])
    sp_upr = din("sp_upr", [FF, TOWN])
    sp_gater = din("sp_gater", [FF, TOWN])
    sp_upq = din("sp_upq", [FF, TOWN], I16)
    sp_gs = din("sp_gs", [FF, TOWN])
    sp_o = din("sp_o", [FF, TOWN])
    sp_wd = din("sp_wd", [FF, D])
    cc_in = [din(f"cc_in{i}", [1, 2]) for i in range(4)]
    cc_out = [din(f"cc_out{i}", [1, 2]) for i in range(4)]

    def alloc(name, cols, dtp=F32, parts=128):
        return nc.alloc_sbuf_tensor(name, [parts, cols], dtp).ap()

    ident = alloc("c_ident", 128)
    mask64 = alloc("c_mask", 64, parts=64)
    ones_row = alloc("c_ones", 128, parts=1)
    meanw = alloc("c_meanw", 1)
    vecs = alloc("c_vecs", 7 * NFT)

    e_c = alloc("c_e", 2 * NFT)
    iota_bc = alloc("c_iotabc", 64)
    SE = alloc("c_SE", 512)
    M_km = alloc("c_Mkm", 512)
    M_out = alloc("c_Mout", 512)
    M_stL = alloc("c_MstL", 512)
    M_k2L = alloc("c_Mk2L", D)
    SCAL = alloc("c_scal", 32)
    cEPS = alloc("c_eps", 1)
    AMAX = alloc("c_amax", 8)
    rowtmp = alloc("c_rowtmp", 1024, parts=1)
    St = alloc("St", 512)
    stats = alloc("p_stats", 16)

    B_W = alloc("b_w", NFT * D)
    B0 = alloc("b_0", NFT * (TB + 1))
    B1 = alloc("b_1", NFT * TB)
    B2 = alloc("b_2", NFT * TB)
    B3 = alloc("b_3", NFT * TB)
    B4 = alloc("b_4", NFT * TB)
    B5 = alloc("b_5", NFT * TB)
    B6 = alloc("b_6", NFT * TB)
    B7 = alloc("b_7", NFT * TB)
    B8 = alloc("b_8", 4 * D)
    B9 = alloc("b_9", 2 * D)

    pb = [nc.alloc_psum_tensor(f"pb{i}", [128, 512], F32).ap() for i in range(8)]

    em = Emitter(nc)
    for _s in DSEMS:
        em.new_sem(_s)
    ST = em.new_sem("st")
    SV = em.new_sem("sv")
    SS = em.new_sem("ss")
    SG = em.new_sem("sg")
    SD = "sd"

    V, Tn, Sc, Gp, Sy = "vector", "tensor", "scalar", "gpsimd", "sync"
    esem = {V: SV, Tn: ST, Sc: SS, Gp: SG, Sy: SD}

    _dma_i = [0]

    def dma(out, in_):
        s = DSEMS[_dma_i[0] % 4]
        _dma_i[0] += 1
        em.op(Sy, lambda E, o=out, i=in_: E.dma_start(out=o, in_=i), inc=s, inc_by=16)
        em.unfenced["sync"] = 0

    def barrier():
        em.barrier()

    def TT(out, a, b, op, eng=V):
        em.op(eng, lambda E, o=out, x=a, y=b: E.tensor_tensor(o, x, y, op=op),
              [a, b], [out])

    def TS(out, a, s1, op0, s2=None, op1=None, eng=V):
        rd = [a] + [s for s in (s1, s2) if isinstance(s, bass.AP)]
        if s2 is None:
            em.op(eng, lambda E, o=out, x=a: E.tensor_scalar(o, x, s1, None, op0=op0),
                  rd, [out])
        else:
            em.op(eng, lambda E, o=out, x=a: E.tensor_scalar(o, x, s1, s2, op0=op0, op1=op1),
                  rd, [out])

    def STT(out, in0, scal, in1, op0, op1, eng=V):
        rd = [in0, in1] + ([scal] if isinstance(scal, bass.AP) else [])
        em.op(eng, lambda E, o=out, a=in0, b=in1: E.scalar_tensor_tensor(
            o, a, scal, b, op0=op0, op1=op1), rd, [out])

    def RED(out, a, op, absval=False, eng=V):
        em.op(eng, lambda E, o=out, x=a: E.tensor_reduce(
            o, x, axis=AX.X, op=op, apply_absolute_value=absval),
            [a], [out])

    def CP(out, a, eng=V):
        em.op(eng, lambda E, o=out, x=a: E.tensor_copy(o, x), [a], [out])

    def ACT(out, a, func, bias=0.0, scale=1.0):
        rd = [a] + ([bias] if isinstance(bias, bass.AP) else [])
        em.op(Sc, lambda E, o=out, x=a: E.activation(o, x, func, bias=bias, scale=scale),
              rd, [out])

    def RECIP(out, a, eng=V):
        em.op(eng, lambda E, o=out, x=a: E.reciprocal(o, x), [a], [out])

    def MM(out, lhsT, rhs, start, stop):
        em.op(Tn, lambda E, o=out, l=lhsT, r=rhs: E.matmul(o, l, r, start=start, stop=stop),
              [lhsT, rhs], [out])

    def TR(out, in_):
        em.op(Tn, lambda E, o=out, i=in_: E.transpose(o, i, ident[:, :]),
              [in_, ident], [out])

    def MEMSET(buf, val=0.0, eng=V):
        em.op(eng, lambda E, o=buf: E.memset(o, val), [], [buf])

    ft_ = lambda buf, j, w: buf[:, j * w:(j + 1) * w]

    dma(ident[:, :], ident_i)
    dma(mask64[0:64, :], mask64_i)
    dma(rowtmp[0:1, 0:64], iota64_i)
    dma(ones_row[0:1, :], ones_row_i)
    dma(meanw[:, 0:1], meanw_i)
    dma(vecs[:, :], vec_cols)
    dma(B8[0:1, 0:D], td_row)
    for e in (V, Sc, Tn):
        em.wait_now(e, SD)

    c_td = vecs[:, 0:NFT]
    c_tmk = vecs[:, NFT:2 * NFT]
    c_tmv = vecs[:, 2 * NFT:3 * NFT]
    c_tmr = vecs[:, 3 * NFT:4 * NFT]
    c_tmg = vecs[:, 4 * NFT:5 * NFT]
    c_gln = vecs[:, 5 * NFT:6 * NFT]
    c_bln = vecs[:, 6 * NFT:7 * NFT]
    tdrow = B8[0:1, 0:D]
    erow = B8[0:1, D:2 * D]
    wLrow = B8[0:1, 2 * D:3 * D]

    MEMSET(cEPS[:, 0:1], EPS)
    ACT(e_c[:, 0:NFT], c_td, AF.Exp)
    ACT(e_c[:, NFT:2 * NFT], e_c[:, 0:NFT], AF.Exp, scale=float(-L))
    ACT(erow, tdrow, AF.Exp)
    ACT(wLrow, erow, AF.Exp, scale=float(-L))
    em.wait_now(Tn, SS)
    MM(pb[0][:, 0:64], ones_row[0:1, :], rowtmp[0:1, 0:64], True, True)
    em.wait_now(V, ST)
    CP(iota_bc[:, :], pb[0][:, 0:64])
    em.wait_now(V, SS)
    for j in range(NFT):
        TS(ft_(SE, j, 64), iota_bc[:, :], e_c[:, j:j + 1], AL.mult)
        TS(ft_(M_stL, j, 64), iota_bc[:, :], 0.0, AL.mult, e_c[:, NFT + j:NFT + j + 1], AL.add)
    em.wait_now(Sc, SV)
    ACT(M_km[:, :], SE[:, :], AF.Exp)
    ACT(M_out[:, :], SE[:, :], AF.Exp, scale=-1.0)
    em.wait_now(Tn, SS)
    MM(pb[1][:, 0:512], ones_row[0:1, :], wLrow[0:1, 0:512], True, True)
    MM(pb[2][:, 0:512], ones_row[0:1, :], wLrow[0:1, 512:1024], True, True)
    em.wait_now(V, ST)
    CP(M_k2L[:, 0:512], pb[1][:, 0:512])
    CP(M_k2L[:, 512:1024], pb[2][:, 0:512])
    MEMSET(St[:, :], 0.0)
    MEMSET(AMAX[:, :], 0.0)
    barrier()

    xt = B_W[:, 0:D]
    xm = B_W[:, D:2 * D]
    sqt = B_W[:, 2 * D:3 * D]
    xnt = B_W[:, 3 * D:4 * D]
    stage = B9[:, 0:D]

    def ln_tiles(src_ap, n_tiles, col0):
        for ti in range(n_tiles):
            dma(xt, src_ap[ti * 128:(ti + 1) * 128, :])
            em.wait_now(V, SD)
            RED(stats[:, 0:1], xt, AL.add)
            TS(stats[:, 1:2], stats[:, 0:1], 1.0 / D, AL.mult)
            TS(xm, xt, stats[:, 1:2], AL.subtract)
            em.wait_now(Sc, SV)
            ACT(sqt, xm, AF.Square)
            em.wait_now(V, SS)
            RED(stats[:, 2:3], sqt, AL.add)
            em.wait_now(Sc, SV)
            ACT(stats[:, 3:4], stats[:, 2:3], AF.Sqrt, bias=cEPS[:, 0:1], scale=1.0 / D)
            em.wait_now(V, SS)
            RECIP(stats[:, 4:5], stats[:, 3:4])
            TS(xnt, xm, stats[:, 4:5], AL.mult)
            RED(stats[:, 5:6], xnt, AL.max, absval=True)
            TT(AMAX[:, 0:1], AMAX[:, 0:1], stats[:, 5:6], AL.max)
            em.wait_now(Tn, SV)
            for j in range(NFT):
                TR(pb[j // 4][:, (j % 4) * 128:(j % 4 + 1) * 128], xnt[:, j * 128:(j + 1) * 128])
            em.wait_now(V, ST)
            CP(stage[:, 0:512], pb[0][:, 0:512])
            CP(stage[:, 512:1024], pb[1][:, 0:512])
            em.wait_now(Sy, SV)
            bc = (col0 + ti) * 128
            dma(sp_xn[:, bc:bc + 128].rearrange("(j f) c -> f j c", f=128),
                stage[:, :].rearrange("p (j c) -> p j c", c=128))
            em.wait_now(V, SD)
            em.wait_now(Tn, SD)
            if ti % 6 == 5:
                barrier()

    ln_tiles(x_warm, cfg.n_warm_tiles, 0)
    ln_tiles(x_own, cfg.n_own_tiles, cfg.n_warm_tiles)
    barrier()

    def cross_part_max(src_col_ap, dst_row_off):
        TR(pb[0][0:1, 0:128], src_col_ap)
        em.wait_now(V, ST)
        CP(rowtmp[0:1, 64:192], pb[0][0:1, 0:128])
        RED(rowtmp[0:1, dst_row_off:dst_row_off + 1], rowtmp[0:1, 64:192], AL.max)
        em.wait_now(Tn, SV)

    def allreduce_scales(idx, src_off, nvals, sc_cols):
        em.wait_now(Sy, SV)
        dma(cc_in[idx][0:1, 0:nvals], rowtmp[0:1, src_off:src_off + nvals])
        em.wait_now(Gp, SD)
        em.op(Gp, lambda E: E.collective_compute(
            "AllReduce", AL.max,
            replica_groups=[list(range(cfg.n_cores))],
            ins=[cc_in[idx][0:1, 0:nvals]],
            outs=[cc_out[idx][0:1, 0:nvals]]), inc=SG)
        em.wait_now(Sy, SG)
        dma(rowtmp[0:1, 200:200 + nvals], cc_out[idx][0:1, 0:nvals])
        em.wait_now(V, SD)
        for c in range(nvals):
            TS(rowtmp[0:1, 210 + c:211 + c], rowtmp[0:1, 200 + c:201 + c],
               1.0 / QMAX, AL.mult, 1e-12, AL.max)
        em.wait_now(Tn, SV)
        for c in range(nvals):
            MM(pb[0][:, c:c + 1], ones_row[0:1, :], rowtmp[0:1, 210 + c:211 + c], True, True)
        em.wait_now(V, ST)
        for c in range(nvals):
            col = sc_cols[c]
            CP(SCAL[:, col:col + 1], pb[0][:, c:c + 1])
            RECIP(SCAL[:, col + 1:col + 2], SCAL[:, col:col + 1])
            TS(SCAL[:, col + 2:col + 3], SCAL[:, col:col + 1], -1.0, AL.mult)

    cross_part_max(AMAX[:, 0:1], 0)
    allreduce_scales(0, 0, 1, [0])
    barrier()

    xn_g = B0
    xmix = B1
    projg = B2
    own0 = NWP + 1
    warm0 = 1

    def load_weightT(wt_ap):
        dma(B_W[:, 0:NFT * D].rearrange("p (j f) -> p j f", f=D),
            wt_ap.rearrange("(j r) f -> r j f", r=128))

    def proj_group(src_col0, n, mix_c, dst, dst0, sig):
        dma(xn_g[:, 0:NFT * (TB + 1)].rearrange("p (j t) -> p j t", t=TB + 1)[:, :, 0:n + 1],
            sp_xn[:, src_col0 - 1:src_col0 + n].rearrange("(j f) c -> f j c", f=128))
        em.wait_now(V, SD)
        for j in range(NFT):
            xg = ft_(xn_g, j, TB + 1)
            mx = ft_(xmix, j, TB)[:, 0:n]
            TT(mx, xg[:, 0:n], xg[:, 1:n + 1], AL.subtract)
            STT(mx, mx, mix_c[:, j:j + 1], xg[:, 1:n + 1], AL.mult, AL.add)
        em.wait_now(Tn, SV)
        for ot in range(NFT):
            bank = pb[ot % 2]
            if ot >= 2:
                em.wait_now(Tn, SS if sig else SV)
            for j in range(NFT):
                MM(bank[:, 0:n], ft_(B_W, j, D)[:, ot * 128:(ot + 1) * 128],
                   ft_(xmix, j, TB)[:, 0:n], j == 0, j == NFT - 1)
            po = ft_(projg, ot, TB)[:, 0:n]
            if sig:
                em.wait_now(Sc, ST)
                ACT(po, bank[:, 0:n], AF.Sigmoid)
            else:
                em.wait_now(V, ST)
                CP(po, bank[:, 0:n])
        em.wait_now(Sy, SS if sig else SV)
        dma(dst[:, dst0:dst0 + n].rearrange("(j f) c -> f j c", f=128),
            projg[:, 0:NFT * TB].rearrange("p (j t) -> p j t", t=TB)[:, :, 0:n])
        em.wait_now(Tn, SD)
        em.wait_now(V, SD) if not sig else em.wait_now(Sc, SD)

    for (wn, spill, mix_c, sig) in (("WkT", sp_k, c_tmk, False), ("WvT", sp_v, c_tmv, False),
                                    ("WrT", sp_r, c_tmr, False), ("WgT", sp_g, c_tmg, True)):
        load_weightT(WT[wn])
        em.wait_now(Tn, SD)
        kv = wn in ("WkT", "WvT")
        if kv and TWARM > 0:
            for gc in range(0, TWARM, TB):
                proj_group(warm0 + gc, min(TB, TWARM - gc), mix_c, spill, gc, sig)
        for gc in range(0, TOWN, TB):
            proj_group(own0 + gc, min(TB, TOWN - gc), mix_c, spill,
                       (TWARM + gc) if kv else gc, sig)
        barrier()

    load_weightT(WT["WoT"])
    barrier()

    k_g, v_g, r_g, gg_g = B3, B4, B5, B6
    ktm_g = B8
    vtm_sub = B0[:, 0:1024]
    k2tm_sub = B0[:, 1024:2048]
    kt_sub = B0[:, 2048:3072]
    scr512 = B0[:, 3072:3584]
    StEntry = B0[:, 3584:4096]
    Atl_g = B1
    Ug = B7
    attn_g = B2
    y_g = B7
    ym_g = B0
    s_g = B1

    def mixer_ph1(sg, own):
        t0g = sg * 128
        ktm = ktm_g[:, sg * 1024:(sg + 1) * 1024]
        for j in range(NFT):
            kv_ = ft_(k_g, j, TB)[:, t0g:t0g + 128].rearrange("p (c s) -> p c s", s=L)
            mv_ = ft_(M_km, j, L).unsqueeze(1).broadcast_to([128, 2, L])
            ov_ = kt_sub[:, j * 128:(j + 1) * 128].rearrange("p (c s) -> p c s", s=L)
            em.op(V, lambda E, o=ov_, a=kv_, b=mv_: E.tensor_tensor(o, a, b, op=AL.mult),
                  [ft_(k_g, j, TB)], [kt_sub])
        em.wait_now(Tn, SV)
        for j in range(NFT):
            TR(pb[2 + j // 4][:, (j % 4) * 128:(j % 4 + 1) * 128],
               kt_sub[:, j * 128:(j + 1) * 128])
        em.wait_now(V, ST)
        CP(ktm[:, 0:512], pb[2][:, 0:512])
        CP(ktm[:, 512:1024], pb[3][:, 0:512])
        em.wait_now(Tn, SV)
        for j in range(NFT):
            TR(pb[2 + j // 4][:, (j % 4) * 128:(j % 4 + 1) * 128],
               ft_(v_g, j, TB)[:, t0g:t0g + 128])
        em.wait_now(V, ST)
        CP(vtm_sub[:, 0:512], pb[2][:, 0:512])
        CP(vtm_sub[:, 512:1024], pb[3][:, 0:512])
        TT(k2tm_sub[:, :], ktm[:, :], M_k2L[:, :], AL.mult)
        em.wait_now(Tn, SV)
        for ci in range(2):
            cp_ = ci * 64
            c = sg * 2 + ci
            for j in range(NFT):
                for hp in range(2):
                    h = 2 * j + hp
                    MM(pb[7][hp * 64:hp * 64 + 64, j * 64:j * 64 + 64],
                       vtm_sub[cp_:cp_ + 64, h * 64:h * 64 + 64],
                       k2tm_sub[cp_:cp_ + 64, h * 64:h * 64 + 64],
                       True, True)
            em.wait_now(V, ST)
            CP(Ug[:, c * 512:(c + 1) * 512], pb[7][:, :])
            em.wait_now(Tn, SV)
        if own:
            for ci in range(2):
                cp_ = ci * 64
                tok0 = t0g + ci * 64
                for j in range(NFT):
                    for hp in range(2):
                        MM(pb[4 + j // 4][cp_:cp_ + 64,
                                          (j % 4) * 128 + hp * 64:(j % 4) * 128 + hp * 64 + 64],
                           ft_(v_g, j, TB)[hp * 64:hp * 64 + 64, tok0:tok0 + 64],
                           ft_(r_g, j, TB)[hp * 64:hp * 64 + 64, tok0:tok0 + 64],
                           True, True)
            em.wait_now(V, ST)
            for b in range(2):
                pv = pb[4 + b][:, 0:512].rearrange("p (k s) -> p k s", s=64)
                mk = mask64[0:64, :].unsqueeze(0).unsqueeze(2) if False else mask64[0:64, :]
                mkb = mk.unsqueeze(1).broadcast_to([64, 8, 64])
                ov = Atl_g[:, sg * 1024 + b * 512:sg * 1024 + (b + 1) * 512] \
                    .rearrange("p (k s) -> p k s", s=64)
                pv2 = pb[4 + b][0:64, 0:512].rearrange("p (k s) -> p k s", s=64)
                ov2 = Atl_g[0:64, sg * 1024 + b * 512:sg * 1024 + (b + 1) * 512] \
                    .rearrange("p (k s) -> p k s", s=64)
                pv3 = pb[4 + b][64:128, 0:512].rearrange("p (k s) -> p k s", s=64)
                ov3 = Atl_g[64:128, sg * 1024 + b * 512:sg * 1024 + (b + 1) * 512] \
                    .rearrange("p (k s) -> p k s", s=64)
                em.op(V, lambda E, o=ov2, a=pv2, m=mkb: E.tensor_tensor(o, a, m, op=AL.mult),
                      [pb[4 + b]], [Atl_g])
                em.op(V, lambda E, o=ov3, a=pv3, m=mkb: E.tensor_tensor(o, a, m, op=AL.mult),
                      [pb[4 + b]], [Atl_g])
            em.wait_now(Tn, SV)

    def mixer_scan(nch):
        CP(StEntry, St[:, :])
        TT(scr512, St[:, :], M_stL[:, :], AL.mult)
        TT(Ug[:, 0:512], scr512, Ug[:, 0:512], AL.add)
        for c in range(1, nch):
            TT(scr512, Ug[:, (c - 1) * 512:c * 512], M_stL[:, :], AL.mult)
            TT(Ug[:, c * 512:(c + 1) * 512], scr512, Ug[:, c * 512:(c + 1) * 512], AL.add)
        CP(St[:, :], Ug[:, (nch - 1) * 512:nch * 512])

    def mixer_ph3(nch):
        em.wait_now(Tn, SV)
        for c in range(nch):
            bank = pb[c % 2]
            if c >= 2:
                em.wait_now(Tn, SV)
            sg = c // 2
            cp_ = (c % 2) * 64
            st_src = StEntry if c == 0 else Ug[:, (c - 1) * 512:c * 512]
            for j in range(NFT):
                for hp in range(2):
                    h = 2 * j + hp
                    MM(bank[hp * 64:hp * 64 + 64, j * 64:j * 64 + 64],
                       ktm_g[:, sg * 1024:(sg + 1) * 1024][cp_:cp_ + 64, h * 64:h * 64 + 64],
                       Atl_g[cp_:cp_ + 64, sg * 1024 + j * 128 + hp * 64:
                             sg * 1024 + j * 128 + hp * 64 + 64],
                       True, False)
                    MM(bank[hp * 64:hp * 64 + 64, j * 64:j * 64 + 64],
                       st_src[hp * 64:hp * 64 + 64, j * 64:j * 64 + 64],
                       ft_(r_g, j, TB)[hp * 64:hp * 64 + 64, c * 64:c * 64 + 64],
                       False, True)
            em.wait_now(V, ST)
            pv = bank[:, 0:512].rearrange("p (j s) -> p j s", s=64)
            mo = M_out[:, :].rearrange("p (j s) -> p j s", s=64)
            ov = attn_g[:, 0:NFT * TB].rearrange("p (j t) -> p j t", t=TB)[:, :, c * 64:c * 64 + 64]
            em.op(V, lambda E, o=ov, a=pv, m=mo: E.tensor_tensor(o, a, m, op=AL.mult),
                  [bank, M_out], [attn_g])

    def mixer_group(kvc0, n, own, ownc0=0):
        dma(k_g[:, 0:NFT * TB].rearrange("p (j t) -> p j t", t=TB)[:, :, 0:n],
            sp_k[:, kvc0:kvc0 + n].rearrange("(j f) c -> f j c", f=128))
        dma(v_g[:, 0:NFT * TB].rearrange("p (j t) -> p j t", t=TB)[:, :, 0:n],
            sp_v[:, kvc0:kvc0 + n].rearrange("(j f) c -> f j c", f=128))
        if own:
            dma(r_g[:, 0:NFT * TB].rearrange("p (j t) -> p j t", t=TB)[:, :, 0:n],
                sp_r[:, ownc0:ownc0 + n].rearrange("(j f) c -> f j c", f=128))
        em.wait_now(V, SD)
        em.wait_now(Tn, SD)
        for sg in range(n // 128):
            mixer_ph1(sg, own)
        mixer_scan(n // L)
        if own:
            mixer_ph3(n // L)

    if TWARM > 0:
        for gc in range(0, TWARM, TB):
            mixer_group(gc, min(TB, TWARM - gc), own=False)
            barrier()

    for gc in range(0, TOWN, TB):
        n = min(TB, TOWN - gc)
        mixer_group(TWARM + gc, n, own=True, ownc0=gc)
        dma(gg_g[:, 0:NFT * TB].rearrange("p (j t) -> p j t", t=TB)[:, :, 0:n],
            sp_g[:, gc:gc + n].rearrange("(j f) c -> f j c", f=128))
        em.wait_now(V, SD)
        for j in range(NFT):
            TT(ft_(attn_g, j, TB)[:, 0:n], ft_(attn_g, j, TB)[:, 0:n],
               ft_(gg_g, j, TB)[:, 0:n], AL.mult)
        em.wait_now(Tn, SV)
        for ot in range(NFT):
            bank = pb[ot % 2]
            if ot >= 2:
                em.wait_now(Tn, SV)
            for j in range(NFT):
                MM(bank[:, 0:n], ft_(B_W, j, D)[:, ot * 128:(ot + 1) * 128],
                   ft_(attn_g, j, TB)[:, 0:n], j == 0, j == NFT - 1)
            em.wait_now(V, ST)
            CP(ft_(y_g, ot, TB)[:, 0:n], bank[:, 0:n])
        em.wait_now(Tn, SV)
        for j in range(NFT):
            MM(pb[2][0:1, 0:n], meanw[:, 0:1], ft_(y_g, j, TB)[:, 0:n], j == 0, j == NFT - 1)
        em.wait_now(V, ST)
        CP(rowtmp[0:1, 500:500 + n], pb[2][0:1, 0:n])
        em.wait_now(Tn, SV)
        MM(pb[3][:, 0:n], ones_row[0:1, :], rowtmp[0:1, 500:500 + n], True, True)
        em.wait_now(V, ST)
        for j in range(NFT):
            TT(ft_(ym_g, j, TB)[:, 0:n], ft_(y_g, j, TB)[:, 0:n], pb[3][:, 0:n], AL.subtract)
        em.wait_now(Sc, SV)
        for j in range(NFT):
            ACT(ft_(s_g, j, TB)[:, 0:n], ft_(ym_g, j, TB)[:, 0:n], AF.Square)
        em.wait_now(Tn, SS)
        for j in range(NFT):
            MM(pb[2][0:1, 0:n], meanw[:, 0:1], ft_(s_g, j, TB)[:, 0:n], j == 0, j == NFT - 1)
        em.wait_now(Sc, ST)
        ACT(rowtmp[0:1, 300:300 + n], pb[2][0:1, 0:n], AF.Sqrt, bias=cEPS[0:1, 0:1])
        em.wait_now(V, SS)
        RECIP(rowtmp[0:1, 300:300 + n], rowtmp[0:1, 300:300 + n])
        em.wait_now(Tn, SV)
        MM(pb[3][:, 0:n], ones_row[0:1, :], rowtmp[0:1, 300:300 + n], True, True)
        em.wait_now(V, ST)
        for j in range(NFT):
            sj = ft_(s_g, j, TB)[:, 0:n]
            TT(sj, ft_(ym_g, j, TB)[:, 0:n], pb[3][:, 0:n], AL.mult)
            TS(sj, sj, c_gln[:, j:j + 1], AL.mult, c_bln[:, j:j + 1], AL.add)
        em.wait_now(Sy, SV)
        dma(sp_s[:, gc:gc + n].rearrange("(j f) c -> f j c", f=128),
            s_g[:, 0:NFT * TB].rearrange("p (j t) -> p j t", t=TB)[:, :, 0:n])
        barrier()

    assert NFT * TB >= 2 * TOWN, "buffer layout assumes TB*8 >= 2*TOWN"
    xq_bufs = [B3, B4, B5, B6]
    def xq_t(j):
        return xq_bufs[j // 2][:, (j % 2) * TOWN:(j % 2 + 1) * TOWN]

    tz = B0[:, 0:TOWN]
    tg = B0[:, TOWN:2 * TOWN]
    ti16 = B1[:, 0:(TOWN + 1) // 2].bitcast(I16)[:, 0:TOWN]
    tu16 = B1[:, 0:(TOWN + 1) // 2].bitcast(U16)[:, 0:TOWN]

    def quant4(out, x, scol, n, eng=V, z=None, g=None, i16=None):
        s = SCAL[:, scol:scol + 1]
        rs = SCAL[:, scol + 1:scol + 2]
        ns = SCAL[:, scol + 2:scol + 3]
        z = z if z is not None else tz
        g = g if g is not None else tg
        i16 = i16 if i16 is not None else ti16
        TS(z[:, 0:n], x, rs, AL.mult, eng=eng)
        TS(i16[:, 0:n], x, rs, AL.mult, eng=eng)
        TT(g[:, 0:n], z[:, 0:n], i16[:, 0:n], AL.subtract, eng=eng)
        STT(z[:, 0:n], g[:, 0:n], 0.5, i16[:, 0:n], AL.is_gt, AL.add, eng=eng)
        STT(g[:, 0:n], g[:, 0:n], -0.5, z[:, 0:n], AL.is_ge, AL.add, eng=eng)
        TS(out, g[:, 0:n], s, AL.mult, ns, AL.add, eng=eng)

    def gp_round_to_int(out_i16, z, g, r0f, n):
        TS(out_i16[:, 0:n], z[:, 0:n], 1.0, AL.mult, eng=Gp)
        TS(r0f[:, 0:n], out_i16[:, 0:n], 1.0, AL.mult, eng=Gp)
        TT(g[:, 0:n], z[:, 0:n], r0f[:, 0:n], AL.subtract, eng=Gp)
        TS(z[:, 0:n], g[:, 0:n], 0.5, AL.is_gt, eng=Gp)
        TT(r0f[:, 0:n], z[:, 0:n], r0f[:, 0:n], AL.add, eng=Gp)
        TS(g[:, 0:n], g[:, 0:n], -0.5, AL.is_ge, eng=Gp)
        TT(g[:, 0:n], g[:, 0:n], r0f[:, 0:n], AL.add, eng=Gp)
        TS(z[:, 0:n], g[:, 0:n], -1.0, AL.add, eng=Gp)

    def v_round_i16(out_i16, z, g, n):
        TS(out_i16[:, 0:n], z[:, 0:n], 1.0, AL.mult, eng=V)
        TT(g[:, 0:n], z[:, 0:n], out_i16[:, 0:n], AL.subtract, eng=V)
        STT(z[:, 0:n], g[:, 0:n], 0.5, out_i16[:, 0:n], AL.is_gt, AL.add, eng=V)
        STT(g[:, 0:n], g[:, 0:n], -0.5, z[:, 0:n], AL.is_ge, AL.add, eng=V)
        TS(out_i16[:, 0:n], g[:, 0:n], 1.0, AL.mult, -1.0, AL.add, eng=V)

    for bidx, buf in enumerate(xq_bufs):
        dma(buf[:, 0:2 * TOWN].rearrange("p (j t) -> p j t", t=TOWN),
            sp_xn[bidx * 256:(bidx + 1) * 256, own0:own0 + TOWN]
            .rearrange("(j f) c -> f j c", f=128))
    em.wait_now(V, SD)
    for j in range(NFT):
        quant4(xq_t(j), xq_t(j), 0, TOWN)
    barrier()

    wnat = B1
    wdqT = B7
    ustg4 = B2[:, 0:4 * FTB]
    wsc = B2[:, 4 * FTB:4 * FTB + 224]
    wz = B0[:, 0:FF]
    wgs = B8[:, 0:FF]
    wi16 = B9[:, 0:FF // 2].bitcast(I16)[:, 0:FF]

    def dequant_tile(wtile, nblk):
        bm = wsc[:, 0:nblk]
        sw = wsc[:, 96:96 + nblk]
        em.op(V, lambda E, o=bm.unsqueeze(2), x=wtile.rearrange("p (b s) -> p b s", s=32):
              E.tensor_reduce(o, x, axis=AX.X, op=AL.max, apply_absolute_value=True),
              [wtile], [wsc])
        TS(sw, bm, 1.0 / 7.0, AL.mult, 1e-12, AL.max)
        rw = bm
        RECIP(rw, sw)
        rwb = rw.unsqueeze(2).broadcast_to([128, nblk, 32])
        swb = sw.unsqueeze(2).broadcast_to([128, nblk, 32])
        nw = nblk * 32
        wv = wtile.rearrange("p (b s) -> p b s", s=32)
        zv = wz[:, 0:nw].rearrange("p (b s) -> p b s", s=32)
        i16v = wi16[:, 0:nw].rearrange("p (b s) -> p b s", s=32)
        em.op(V, lambda E, o=zv, a=wv, b=rwb: E.tensor_tensor(o, a, b, op=AL.mult),
              [wtile, wsc], [wz])
        em.op(V, lambda E, o=i16v, a=wv, b=rwb: E.tensor_tensor(o, a, b, op=AL.mult),
              [wtile, wsc], [wi16])
        TT(wgs[:, 0:nw], wz[:, 0:nw], wi16[:, 0:nw], AL.subtract)
        STT(wz[:, 0:nw], wgs[:, 0:nw], 0.5, wi16[:, 0:nw], AL.is_gt, AL.add)
        STT(wgs[:, 0:nw], wgs[:, 0:nw], -0.5, wz[:, 0:nw], AL.is_ge, AL.add)
        TS(wz[:, 0:nw], wgs[:, 0:nw], 1.0, AL.mult, -1.0, AL.add)
        ov = wtile.rearrange("p (b s) -> p b s", s=32)
        yv = wz[:, 0:nw].rearrange("p (b s) -> p b s", s=32)
        em.op(V, lambda E, o=ov, a=yv, b=swb: E.tensor_tensor(o, a, b, op=AL.mult),
              [wz, wsc], [wtile])

    NOS = NIT // 4

    def ffn_matmul_weight(w_in, spill, amax_col):
        for os_ in range(NOS):
            em.wait_now(Sy, ST)
            dma(wnat[:, 0:4 * D].rearrange("p (q f) -> p q f", f=D),
                w_in[os_ * 512:(os_ + 1) * 512, :].rearrange("(q r) f -> r q f", r=128))
            em.wait_now(V, SD)
            for q in range(4):
                dequant_tile(ft_(wnat, q, D), 32)
                em.wait_now(Tn, SV)
                for j in range(NFT):
                    TR(pb[2 + j // 4][:, (j % 4) * 128:(j % 4 + 1) * 128],
                       ft_(wnat, q, D)[:, j * 128:(j + 1) * 128])
                em.wait_now(V, ST)
                for half in range(2):
                    for j4 in range(4):
                        j = half * 4 + j4
                        CP(wdqT[:, j * 512 + q * 128:j * 512 + (q + 1) * 128],
                           pb[2 + half][:, j4 * 128:(j4 + 1) * 128])
                em.wait_now(Tn, SV)
            for tb in range(NFTB):
                em.wait_now(V, SD)
                for q in range(4):
                    bank = pb[q % 2]
                    if q >= 2 or tb > 0:
                        em.wait_now(Tn, SV)
                    for j in range(NFT):
                        MM(bank[:, 0:FTB], wdqT[:, j * 512 + q * 128:j * 512 + (q + 1) * 128],
                           xq_t(j)[:, tb * FTB:(tb + 1) * FTB], j == 0, j == NFT - 1)
                    em.wait_now(V, ST)
                    CP(ustg4[:, q * FTB:(q + 1) * FTB], bank[:, 0:FTB])
                    RED(stats[:, 6:7], ustg4[:, q * FTB:(q + 1) * FTB], AL.max, absval=True)
                    TT(AMAX[:, amax_col:amax_col + 1], AMAX[:, amax_col:amax_col + 1],
                       stats[:, 6:7], AL.max)
                em.wait_now(Sy, SV)
                dma(spill[os_ * 512:(os_ + 1) * 512, tb * FTB:(tb + 1) * FTB]
                    .rearrange("(q r) t -> r q t", r=128),
                    ustg4[:, 0:4 * FTB].rearrange("p (q t) -> p q t", t=FTB))
            barrier()

    ffn_matmul_weight(Wup, sp_upr, 1)
    ffn_matmul_weight(Wgate, sp_gater, 2)

    for mt in range(NFT):
        dma(wnat[:, 0:FF], Wdown[mt * 128:(mt + 1) * 128, :])
        em.wait_now(V, SD)
        dequant_tile(wnat[:, 0:FF], 96)
        em.wait_now(Tn, SV)
        for it in range(NIT):
            TR(pb[2 + (it % 8) // 4][:, (it % 4) * 128:(it % 4 + 1) * 128],
               wnat[:, it * 128:(it + 1) * 128])
            if it % 8 == 7:
                em.wait_now(V, ST)
                for b in range(2):
                    CP(stage[:, b * 512:(b + 1) * 512], pb[2 + b][:, 0:512])
                em.wait_now(Sy, SV)
                dma(sp_wd[(it - 7) * 128:(it + 1) * 128, mt * 128:(mt + 1) * 128]
                    .rearrange("(k r) c -> r k c", r=128),
                    stage[:, 0:1024].rearrange("p (k c) -> p k c", c=128))
                em.wait_now(Tn, SD)
                em.wait_now(V, SD)
        if mt % 3 == 2:
            barrier()
    barrier()

    cross_part_max(AMAX[:, 1:2], 0)
    cross_part_max(AMAX[:, 2:3], 1)
    allreduce_scales(1, 0, 2, [3, 6])
    barrier()

    class QSet:
        pass
    sV = QSet()
    sV.upt = B3[:, 0:TOWN]; sV.gat = B3[:, TOWN:2 * TOWN]
    sV.sig = B4[:, 0:TOWN]; sV.gst = B4[:, TOWN:2 * TOWN]
    sV.z = B6[:, 0:TOWN]; sV.g = B6[:, TOWN:2 * TOWN]
    sV.i16 = B5[:, 0:TOWN // 2].bitcast(I16)[:, 0:TOWN]
    sV.u16 = B5[:, TOWN // 2:TOWN].bitcast(U16)[:, 0:TOWN]
    sV.gi16 = B5[:, TOWN:TOWN + TOWN // 2].bitcast(I16)[:, 0:TOWN]
    sV.r0f = None
    sG = QSet()
    sG.upt = B0[:, 0:TOWN]; sG.gat = B0[:, TOWN:2 * TOWN]
    sG.sig = B1[:, 0:TOWN]; sG.gst = B1[:, TOWN:2 * TOWN]
    sG.z = B2[:, 0:TOWN]; sG.g = B2[:, TOWN:2 * TOWN]
    sG.i16 = B7[:, 0:TOWN // 2].bitcast(I16)[:, 0:TOWN]
    sG.u16 = B7[:, TOWN // 2:TOWN].bitcast(U16)[:, 0:TOWN]
    sG.gi16 = B7[:, TOWN:TOWN + TOWN // 2].bitcast(I16)[:, 0:TOWN]
    sG.r0f = B9[:, 0:TOWN]

    qlast = {id(sV): None, id(sG): None}

    def qs_wait_reuse(S, eng):
        snap = qlast[id(S)]
        if snap is not None:
            ep, cnts = snap
            if ep == em.nbar:
                for s_, v_ in cnts.items():
                    em.wait(Sy, s_, v_)

    def qs_record(S, eng):
        es = esem_name[eng]
        if em.unfenced[SEM_OWNER[es]] > 0:
            em.fence(SEM_OWNER[es], es)
        if em.unfenced["scalar"] > 0:
            em.fence("scalar", "ss")
        if em.unfenced["vector"] > 0:
            em.fence("vector", "sv")
        snap = {es: em.counts[es], "ss": em.counts["ss"], "sv": em.counts["sv"]}
        for ds in DSEMS:
            snap[ds] = em.counts[ds]
        qlast[id(S)] = (em.nbar, snap)

    esem_name = {V: "sv", Gp: "sg", Sc: "ss"}

    def p5_itile(it, eng, S):
        r0_, r1_ = it * 128, it * 128 + 128
        qs_wait_reuse(S, eng)
        dma(S.upt, sp_upr[r0_:r1_, :])
        dma(S.gat, sp_gater[r0_:r1_, :])
        em.wait_now(eng, SD)
        rs2 = SCAL[:, 4:5]; rs3 = SCAL[:, 7:8]; s3 = SCAL[:, 6:7]
        if eng == Gp:
            TT(S.z, S.upt, rs2.broadcast_to([128, TOWN]), AL.mult, eng=Gp)
            gp_round_to_int(S.i16, S.z, S.g, S.r0f, TOWN)
            TS(S.i16, S.z, 1.0, AL.mult, eng=Gp)
        else:
            TS(S.z, S.upt, rs2, AL.mult, eng=V)
            v_round_i16(S.i16, S.z, S.g, TOWN)
        em.wait_now(Sy, esem_name[eng] if False else ("sg" if eng == Gp else "sv"))
        dma(sp_upq[r0_:r1_, :], S.i16)
        if eng == Gp:
            TT(S.z, S.gat, rs3.broadcast_to([128, TOWN]), AL.mult, eng=Gp)
            gp_round_to_int(S.gi16, S.z, S.g, S.r0f, TOWN)
            TT(S.gat, S.z, s3.broadcast_to([128, TOWN]), AL.mult, eng=Gp)
        else:
            TS(S.z, S.gat, rs3, AL.mult, eng=V)
            v_round_i16(S.gi16, S.z, S.g, TOWN)
            TS(S.gat, S.gi16, s3, AL.mult, eng=V)
        em.wait_now(Sc, "sg" if eng == Gp else "sv")
        ACT(S.sig, S.gat, AF.Sigmoid)
        em.wait_now(eng, SS)
        if eng == Gp:
            TS(S.z, S.sig, 65536.0, AL.mult, eng=Gp)
            TS(S.z, S.z, 65535.0, AL.min, eng=Gp)
            TS(S.u16, S.z, 1.0, AL.mult, eng=Gp)
            TS(S.r0f, S.u16, 1.0, AL.mult, eng=Gp)
            TT(S.g, S.z, S.r0f, AL.subtract, eng=Gp)
            TS(S.z, S.g, 0.5, AL.is_gt, eng=Gp)
            TT(S.r0f, S.z, S.r0f, AL.add, eng=Gp)
            TS(S.g, S.g, -0.5, AL.is_ge, eng=Gp)
            TT(S.g, S.g, S.r0f, AL.add, eng=Gp)
            TS(S.g, S.g, -1.0, AL.add, eng=Gp)
            TS(S.sig, S.g, 1.0 / 65536.0, AL.mult, eng=Gp)
        else:
            TS(S.z, S.sig, 65536.0, AL.mult, 65535.0, AL.min, eng=V)
            TS(S.u16, S.z, 1.0, AL.mult, eng=V)
            TT(S.g, S.z, S.u16, AL.subtract, eng=V)
            STT(S.z, S.g, 0.5, S.u16, AL.is_gt, AL.add, eng=V)
            STT(S.g, S.g, -0.5, S.z, AL.is_ge, AL.add, eng=V)
            TS(S.sig, S.g, 1.0 / 65536.0, AL.mult, -1.0 / 65536.0, AL.add, eng=V)
        TT(S.gst, S.gat, S.sig, AL.mult, eng=eng)
        if eng == Gp:
            em.wait_now(V, "sg")
        RED(stats[:, 7:8], S.gst, AL.max, absval=True, eng=V)
        TT(AMAX[:, 3:4], AMAX[:, 3:4], stats[:, 7:8], AL.max, eng=V)
        em.wait_now(Sy, "sg" if eng == Gp else "sv")
        dma(sp_gs[r0_:r1_, :], S.gst)
        qs_record(S, eng)

    def p6_itile(it, eng, S):
        r0_, r1_ = it * 128, it * 128 + 128
        qs_wait_reuse(S, eng)
        dma(S.gst, sp_gs[r0_:r1_, :])
        dma(S.i16, sp_upq[r0_:r1_, :])
        em.wait_now(eng, SD)
        rs4 = SCAL[:, 10:11]; s4 = SCAL[:, 9:10]; s2 = SCAL[:, 3:4]
        if eng == Gp:
            TT(S.z, S.gst, rs4.broadcast_to([128, TOWN]), AL.mult, eng=Gp)
            gp_round_to_int(S.gi16, S.z, S.g, S.r0f, TOWN)
            TT(S.gat, S.z, s4.broadcast_to([128, TOWN]), AL.mult, eng=Gp)
            TS(S.z, S.i16, 1.0, AL.mult, eng=Gp)
            TT(S.upt, S.z, s2.broadcast_to([128, TOWN]), AL.mult, eng=Gp)
        else:
            TS(S.z, S.gst, rs4, AL.mult, eng=V)
            v_round_i16(S.gi16, S.z, S.g, TOWN)
            TS(S.gat, S.gi16, s4, AL.mult, eng=V)
            TS(S.upt, S.i16, s2, AL.mult, eng=V)
        TT(S.gst, S.gat, S.upt, AL.mult, eng=eng)
        if eng == Gp:
            em.wait_now(V, "sg")
        RED(stats[:, 8:9], S.gst, AL.max, absval=True, eng=V)
        TT(AMAX[:, 4:5], AMAX[:, 4:5], stats[:, 8:9], AL.max, eng=V)
        em.wait_now(Sy, "sg" if eng == Gp else "sv")
        dma(sp_o[r0_:r1_, :], S.gst)
        qs_record(S, eng)

    for it in range(NIT):
        p5_itile(it, V, sV if it % 2 == 0 else sG)
        if it % 3 == 2:
            barrier()

    barrier()
    cross_part_max(AMAX[:, 3:4], 0)
    allreduce_scales(2, 0, 1, [9])
    barrier()

    for it in range(NIT):
        p6_itile(it, V, sV if it % 2 == 0 else sG)
        if it % 3 == 2:
            barrier()

    barrier()
    cross_part_max(AMAX[:, 4:5], 0)
    allreduce_scales(3, 0, 1, [12])
    barrier()

    oq4 = B3[:, 0:4 * FTB]
    oq = [oq4[:, q * FTB:(q + 1) * FTB] for q in range(4)]
    wd = B7
    g2t = B5
    st_ = B6
    xrow = B0[:, 0:D]
    orow = B0[:, D:2 * D]
    prodst = B1

    for tb in range(NFTB):
        c0 = tb * FTB
        for isl in range(NOS):
            em.wait_now(Sy, ST)
            em.wait_now(V, ST)
            dma(oq4.rearrange("p (q t) -> p q t", t=FTB),
                sp_o[isl * 512:(isl + 1) * 512, c0:c0 + FTB]
                .rearrange("(q r) t -> r q t", r=128))
            dma(wd[:, 0:4 * D].rearrange("p (q f) -> p q f", f=D),
                sp_wd[isl * 512:(isl + 1) * 512, :].rearrange("(q r) f -> r q f", r=128))
            em.wait_now(V, SD)
            for q in range(4):
                quant4(oq[q], oq[q], 12, FTB, z=B6[:, 0:FTB], g=B6[:, FTB:2 * FTB],
                       i16=B5[:, 0:FTB // 2].bitcast(I16)[:, 0:FTB])
            em.wait_now(Tn, SV)
            for m in range(NFT):
                for q in range(4):
                    MM(pb[m][:, 0:FTB], ft_(wd, q, D)[:, m * 128:(m + 1) * 128],
                       oq[q], isl == 0 and q == 0, isl == NOS - 1 and q == 3)
            em.wait_now(V, ST)
            em.wait_now(Sy, ST)
        barrier()
        em.wait_now(Sc, ST)
        for m in range(NFT):
            ACT(ft_(g2t, m, FTB), pb[m][:, 0:FTB], AF.Silu)
        dma(st_[:, 0:NFT * FTB].rearrange("p (j t) -> p j t", t=FTB),
            sp_s[:, c0:c0 + FTB].rearrange("(j f) c -> f j c", f=128))
        em.wait_now(V, SD)
        em.wait_now(V, SS)
        for j in range(NFT):
            TT(ft_(g2t, j, FTB), ft_(st_, j, FTB), ft_(g2t, j, FTB), AL.mult)
        em.wait_now(Tn, SV)
        for tt in range(FTB // 128):
            for j in range(NFT):
                TR(pb[j // 4][:, (j % 4) * 128:(j % 4 + 1) * 128],
                   ft_(g2t, j, FTB)[:, tt * 128:(tt + 1) * 128])
            em.wait_now(V, ST)
            CP(prodst[:, 0:512], pb[0][:, 0:512])
            CP(prodst[:, 512:1024], pb[1][:, 0:512])
            row0 = c0 + tt * 128
            dma(xrow, x_own[1 + row0:1 + row0 + 128, :])
            em.wait_now(V, SD)
            TT(orow, prodst[:, 0:D], xrow, AL.add)
            em.wait_now(Sy, SV)
            dma(y_out[row0:row0 + 128, :], orow)
            em.wait_now(V, SD)
            em.wait_now(Tn, SD)

    if debug:
        dbg_specs = [("d_scal", SCAL[:, 0:32], [128, 32], None),
                     ("d_amax", AMAX[:, 0:8], [128, 8], None)]
        for nm, ap_, shp, _ in dbg_specs:
            od = dt(nm, shp, F32, kind="ExternalOutput").ap()
            dma(od, ap_)
        for nm, sp, shp in [("d_xn", sp_xn, [D, TC]), ("d_s", sp_s, [D, TOWN]),
                            ("d_upr", sp_upr, [FF, TOWN]), ("d_gater", sp_gater, [FF, TOWN]),
                            ("d_gs", sp_gs, [FF, TOWN]), ("d_o", sp_o, [FF, TOWN]),
                            ("d_k", sp_k, [D, TWARM + TOWN]), ("d_r", sp_r, [D, TOWN]),
                            ("d_g", sp_g, [D, TOWN])]:
            od = dt(nm, shp, F32, kind="ExternalOutput").ap()
            dma(od, sp)
    em.wait_now(Sy, SD)
    barrier()
    return nc, em


def finalize(nc, em):
    ctxs = []
    sems = {}
    esems = {}
    for name in em.counts:
        h = nc.semaphore(name)
        sems[name] = h.__enter__()
        ctxs.append(h)
    for e in ENGINES:
        h = nc.semaphore("ep_" + e)
        esems[e] = h.__enter__()
        ctxs.append(h)
    blk = nc.Block()
    block = blk.__enter__()
    em.replay(block, sems, esems, SEM_OWNER)
    blk.__exit__(None, None, None)
    for h in reversed(ctxs):
        h.__exit__(None, None, None)
    return nc


def make_core_inputs(x_full, params, core, n_cores, cfg: Cfg):
    TOWN, TWARM = cfg.TOWN, cfg.TWARM
    t0 = core * TOWN
    xo = np.zeros((cfg.NOP, D), np.float32)
    xo[1:1 + TOWN] = x_full[t0:t0 + TOWN]
    xo[0] = x_full[1] if core == 0 else x_full[t0 - 1]
    xw = np.zeros((cfg.NWP, D), np.float32)
    if core > 0 and TWARM > 0:
        xw[1:1 + TWARM] = x_full[t0 - TWARM:t0]
        xw[0] = x_full[t0 - TWARM - 1] if t0 - TWARM - 1 >= 0 else 0.0
    vc = np.stack([params[k].reshape(NFT, 128).T for k in
                   ("td", "tmk", "tmv", "tmr", "tmg", "g_ln", "b_ln")], 1)
    vc = vc.reshape(128, 7 * NFT, order="F") if False else np.concatenate(
        [params[k].reshape(NFT, 128).T for k in
         ("td", "tmk", "tmv", "tmr", "tmg", "g_ln", "b_ln")], axis=1)
    return {
        "x_own": xo, "x_warm": xw,
        "WrT": np.ascontiguousarray(params["Wr"].T),
        "WkT": np.ascontiguousarray(params["Wk"].T),
        "WvT": np.ascontiguousarray(params["Wv"].T),
        "WgT": np.ascontiguousarray(params["Wg"].T),
        "WoT": np.ascontiguousarray(params["Wo"].T),
        "Wup": params["w_up"], "Wgate": params["w_gate"], "Wdown": params["w_down"],
        "vec_cols": np.ascontiguousarray(vc, np.float32),
        "td_row": params["td"].reshape(1, D),
        "ident_c": np.eye(128, dtype=np.float32),
        "mask64_c": np.triu(np.ones((64, 64), np.float32)),
        "iota64_c": np.arange(64, dtype=np.float32).reshape(1, 64),
        "ones_row_c": np.ones((1, 128), np.float32),
        "meanw_c": np.full((128, 1), 1.0 / D, np.float32),
    }


from concourse.bass_utils import run_bass_kernel_spmd

_N_CORES = 8
_CFG = Cfg(TOWN=2048, TWARM=1024, TB=512, n_cores=_N_CORES)
_NC_CACHE = {}


def _get_program():
    if "nc" not in _NC_CACHE:
        nc, em = build(_CFG)
        finalize(nc, em)
        _NC_CACHE["nc"] = nc
    return _NC_CACHE["nc"]


def kernel(L0_attn_x, L0_attn_time_decay, L0_attn_time_mix_k, L0_attn_time_mix_v,
           L0_attn_time_mix_r, L0_attn_time_mix_g, L0_attn_W_r, L0_attn_W_k,
           L0_attn_W_v, L0_attn_W_g, L0_attn_W_o, L0_attn_ln_gamma,
           L0_attn_ln_beta, L0_ffn_w_gate, L0_ffn_w_up, L0_ffn_w_down):
    x_full = np.ascontiguousarray(np.asarray(L0_attn_x, np.float32).reshape(-1, D))
    params = {
        "td": np.asarray(L0_attn_time_decay, np.float32),
        "tmk": np.asarray(L0_attn_time_mix_k, np.float32),
        "tmv": np.asarray(L0_attn_time_mix_v, np.float32),
        "tmr": np.asarray(L0_attn_time_mix_r, np.float32),
        "tmg": np.asarray(L0_attn_time_mix_g, np.float32),
        "Wr": np.asarray(L0_attn_W_r, np.float32),
        "Wk": np.asarray(L0_attn_W_k, np.float32),
        "Wv": np.asarray(L0_attn_W_v, np.float32),
        "Wg": np.asarray(L0_attn_W_g, np.float32),
        "Wo": np.asarray(L0_attn_W_o, np.float32),
        "g_ln": np.asarray(L0_attn_ln_gamma, np.float32),
        "b_ln": np.asarray(L0_attn_ln_beta, np.float32),
        "w_gate": np.asarray(L0_ffn_w_gate, np.float32),
        "w_up": np.asarray(L0_ffn_w_up, np.float32),
        "w_down": np.asarray(L0_ffn_w_down, np.float32),
    }
    nc = _get_program()
    in_maps = [make_core_inputs(x_full, params, c, _N_CORES, _CFG)
               for c in range(_N_CORES)]
    res = run_bass_kernel_spmd(nc, in_maps, core_ids=list(range(_N_CORES)))
    out = np.concatenate([np.asarray(r["y"], np.float32) for r in res.results], 0)
    return out.reshape(1, _N_CORES * _CFG.TOWN, D)


# version 14
# speedup vs baseline: 1.6248x; 1.0023x over previous
import numpy as np
import concourse.bass as bass
import concourse.mybir as mybir
from concourse.alu_op_type import AluOpType as AL

F32 = mybir.dt.float32
I16 = mybir.dt.int16
U16 = mybir.dt.uint16
AF = mybir.ActivationFunctionType
AX = mybir.AxisListType

ENGINES = ("sync", "tensor", "vector", "scalar", "gpsimd")
SEM_OWNER = {"sd0": "sync", "sd1": "sync", "sd2": "sync", "sd3": "sync",
             "st": "tensor", "sv": "vector", "ss": "scalar", "sg": "gpsimd"}
DSEMS = ("sd0", "sd1", "sd2", "sd3")

D = 1024
NFT = 8
H = 16
DH = 64
FF = 3072
NIT = FF // 128
L = 64
QMAX = 32767.0
EPS = 1e-5


class Cfg:
    def __init__(self, TOWN=2048, TWARM=1024, TB=512, n_cores=8):
        assert TB == 512, "buffer layout assumes TB == 512"
        assert TOWN % TB == 0 and (TWARM % TB == 0 or TWARM == 0)
        self.TOWN, self.TWARM, self.TB, self.n_cores = TOWN, TWARM, TB, n_cores
        self.NOP = TOWN + 128
        self.NWP = TWARM + 128 if TWARM > 0 else 128
        self.TC = self.NWP + self.NOP
        self.n_own_tiles = self.NOP // 128
        self.n_warm_tiles = self.NWP // 128
        self.NTB = TOWN // 512 if TOWN % 512 == 0 else None
        assert TOWN % 512 == 0 or TOWN < 512
        self.FTB = 512 if TOWN >= 512 else TOWN
        self.n_ftb = TOWN // self.FTB


class Emitter:

    def __init__(self, nc):
        self.nc = nc
        self.items = {e: [] for e in ENGINES}
        self.counts = {}
        self.touched = {e: [] for e in ENGINES}
        self.unfenced = {e: 0 for e in ENGINES}
        self.nbar = 0
        self.maxcount = {}

    def new_sem(self, name):
        self.counts[name] = 0
        self.maxcount[name] = 0
        return name

    @staticmethod
    def _rng(ap):
        try:
            t = ap.tensor.name
            lo = int(ap.offset)
            sz = 1
            for st, ct in ap.ap:
                sz += abs(int(st)) * (int(ct) - 1)
            return (t, lo, lo + sz)
        except Exception:
            return (str(ap), 0, 1 << 60)

    def _hazard(self, eng, reads, writes):
        tl = self.touched[eng]
        def ov(a, b):
            return a[0] == b[0] and a[1] < b[2] and b[1] < a[2]
        for ap in reads:
            r = self._rng(ap)
            for (k, lo, hi, w) in tl:
                if w and ov(r, (k, lo, hi)):
                    return True
        for ap in writes:
            r = self._rng(ap)
            for (k, lo, hi, w) in tl:
                if ov(r, (k, lo, hi)):
                    return True
        return False

    def _note(self, eng, reads, writes):
        tl = self.touched[eng]
        for ap in reads:
            k, lo, hi = self._rng(ap)
            tl.append((k, lo, hi, False))
        for ap in writes:
            k, lo, hi = self._rng(ap)
            tl.append((k, lo, hi, True))

    def op(self, eng, fn, reads=(), writes=(), inc=None, inc_by=1):
        reads, writes = list(reads), list(writes)
        if eng != "sync" and self._hazard(eng, reads, writes):
            self.items[eng].append(("drain",))
            self.touched[eng] = []
        self._note(eng, reads, writes)
        if inc is not None:
            self.counts[inc] += inc_by
            self.maxcount[inc] = max(self.maxcount[inc], self.counts[inc])
            assert self.counts[inc] <= 240, (
                f"sem {inc} exceeds 240 within an epoch ({self.counts[inc]}); "
                f"insert a barrier (epoch {self.nbar})")
            self.items[eng].append(("op", fn, inc, inc_by))
        else:
            self.items[eng].append(("op", fn, None, 0))
            self.unfenced[eng] += 1

    def fence(self, eng, sem):
        if self.unfenced[eng] == 0:
            return
        self.unfenced[eng] = 0
        self.counts[sem] += 1
        self.maxcount[sem] = max(self.maxcount[sem], self.counts[sem])
        assert self.counts[sem] <= 240, f"sem {sem} epoch overflow"
        self.items[eng].append(("fenceinc", sem))

    def wait(self, eng, sem, val):
        if val > 0:
            self.items[eng].append(("wait", sem, val))

    def wait_now(self, eng, sem):
        if sem == "sd":
            for s in DSEMS:
                self.wait(eng, s, self.counts[s])
            return
        own = SEM_OWNER[sem]
        if self.unfenced[own] > 0:
            self.fence(own, sem)
        self.wait(eng, sem, self.counts[sem])

    def barrier(self):
        tok = dict(self.counts)
        k = self.nbar
        self.nbar += 1
        assert self.nbar <= 125, "too many barriers for 8-bit epoch sems"
        for e in ENGINES:
            self.items[e].append(("barrier", k, tok))
            self.touched[e] = []
            self.unfenced[e] = 0
        for s in self.counts:
            self.counts[s] = 0

    def replay(self, block, sems, esems, owner):
        em = self

        def mk(eng_name):
            def body(proxy):
                last = [None]
                last_inc = [False]

                def emit_fence(s):
                    if last[0] is not None and not last_inc[0]:
                        last[0].then_inc(sems[s], 1)
                        last_inc[0] = True
                    else:
                        proxy.drain()
                        proxy.nop().then_inc(sems[s], 1)
                        last[0] = None
                        last_inc[0] = False

                for it in em.items[eng_name]:
                    kind = it[0]
                    if kind == "drain":
                        proxy.drain()
                        last[0] = None
                        last_inc[0] = False
                    elif kind == "wait":
                        proxy.wait_ge(sems[it[1]], it[2])
                        last[0] = None
                        last_inc[0] = False
                    elif kind == "fenceinc":
                        emit_fence(it[1])
                    elif kind == "ewait":
                        proxy.wait_ge(esems[eng_name], it[1])
                        last[0] = None
                        last_inc[0] = False
                    elif kind == "barrier":
                        k, tok = it[1], it[2]
                        for s, v in tok.items():
                            if v > 0:
                                proxy.wait_ge(sems[s], v)
                        proxy.nop().then_inc(esems[eng_name], 1)
                        for f in ENGINES:
                            proxy.wait_ge(esems[f], 2 * k + 1)
                        for s, own in owner.items():
                            if own == eng_name:
                                proxy.sem_clear(sems[s])
                        proxy.drain()
                        proxy.nop().then_inc(esems[eng_name], 1)
                        for f in ENGINES:
                            proxy.wait_ge(esems[f], 2 * k + 2)
                        last[0] = None
                        last_inc[0] = False
                    else:
                        _, fn, b, c = it
                        instr = fn(proxy)
                        last[0] = instr
                        last_inc[0] = False
                        if b is not None and instr is not None:
                            instr.then_inc(sems[b], c)
            return body

        for e in ENGINES:
            getattr(block, e)(mk(e))


def build(cfg: Cfg, debug=False):
    nc = bass.Bass("TRN2", target_bir_lowering=False)
    TOWN, TWARM, TB = cfg.TOWN, cfg.TWARM, cfg.TB
    NOP, NWP, TC = cfg.NOP, cfg.NWP, cfg.TC
    FTB, NFTB = cfg.FTB, cfg.n_ftb

    dt = nc.dram_tensor
    x_own = dt("x_own", [NOP, D], F32, kind="ExternalInput").ap()
    x_warm = dt("x_warm", [NWP, D], F32, kind="ExternalInput").ap()
    WT = {n: dt(n, [D, D], F32, kind="ExternalInput").ap()
          for n in ("WrT", "WkT", "WvT", "WgT", "WoT")}
    Wup = dt("Wup", [FF, D], F32, kind="ExternalInput").ap()
    Wgate = dt("Wgate", [FF, D], F32, kind="ExternalInput").ap()
    Wdown = dt("Wdown", [D, FF], F32, kind="ExternalInput").ap()
    vec_cols = dt("vec_cols", [128, 7 * NFT], F32, kind="ExternalInput").ap()
    td_row = dt("td_row", [1, D], F32, kind="ExternalInput").ap()
    ident_i = dt("ident_c", [128, 128], F32, kind="ExternalInput").ap()
    mask64_i = dt("mask64_c", [64, 64], F32, kind="ExternalInput").ap()
    iota64_i = dt("iota64_c", [1, 64], F32, kind="ExternalInput").ap()
    ones_row_i = dt("ones_row_c", [1, 128], F32, kind="ExternalInput").ap()
    meanw_i = dt("meanw_c", [128, 1], F32, kind="ExternalInput").ap()
    y_out = dt("y", [TOWN, D], F32, kind="ExternalOutput").ap()

    din = lambda n, s, dtp=F32: dt(n, s, dtp, kind="Internal").ap()
    sp_xn = din("sp_xn", [D, TC])
    sp_k = din("sp_k", [D, TWARM + TOWN])
    sp_v = din("sp_v", [D, TWARM + TOWN])
    sp_r = din("sp_r", [D, TOWN])
    sp_g = din("sp_g", [D, TOWN])
    sp_s = din("sp_s", [D, TOWN])
    sp_upr = din("sp_upr", [FF, TOWN])
    sp_gater = din("sp_gater", [FF, TOWN])
    sp_upq = din("sp_upq", [FF, TOWN], I16)
    sp_gs = din("sp_gs", [FF, TOWN])
    sp_o = din("sp_o", [FF, TOWN])
    sp_wd = din("sp_wd", [FF, D])
    cc_in = [din(f"cc_in{i}", [1, 2]) for i in range(4)]
    cc_out = [din(f"cc_out{i}", [1, 2]) for i in range(4)]

    def alloc(name, cols, dtp=F32, parts=128):
        return nc.alloc_sbuf_tensor(name, [parts, cols], dtp).ap()

    ident = alloc("c_ident", 128)
    mask64 = alloc("c_mask", 64, parts=64)
    ones_row = alloc("c_ones", 128, parts=1)
    meanw = alloc("c_meanw", 1)
    vecs = alloc("c_vecs", 7 * NFT)

    e_c = alloc("c_e", 2 * NFT)
    iota_bc = alloc("c_iotabc", 64)
    SE = alloc("c_SE", 512)
    M_km = alloc("c_Mkm", 512)
    M_out = alloc("c_Mout", 512)
    M_stL = alloc("c_MstL", 512)
    M_k2L = alloc("c_Mk2L", D)
    SCAL = alloc("c_scal", 32)
    cEPS = alloc("c_eps", 1)
    AMAX = alloc("c_amax", 8)
    rowtmp = alloc("c_rowtmp", 1024, parts=1)
    St = alloc("St", 512)
    stats = alloc("p_stats", 16)

    B_W = alloc("b_w", NFT * D)
    B0 = alloc("b_0", NFT * (TB + 1))
    B1 = alloc("b_1", NFT * TB)
    B2 = alloc("b_2", NFT * TB)
    B3 = alloc("b_3", NFT * TB)
    B4 = alloc("b_4", NFT * TB)
    B5 = alloc("b_5", NFT * TB)
    B6 = alloc("b_6", NFT * TB)
    B7 = alloc("b_7", NFT * TB)
    B8 = alloc("b_8", 4 * D)
    B9 = alloc("b_9", 2 * D)

    pb = [nc.alloc_psum_tensor(f"pb{i}", [128, 512], F32).ap() for i in range(8)]

    em = Emitter(nc)
    for _s in DSEMS:
        em.new_sem(_s)
    ST = em.new_sem("st")
    SV = em.new_sem("sv")
    SS = em.new_sem("ss")
    SG = em.new_sem("sg")
    SD = "sd"

    V, Tn, Sc, Gp, Sy = "vector", "tensor", "scalar", "gpsimd", "sync"
    esem = {V: SV, Tn: ST, Sc: SS, Gp: SG, Sy: SD}

    _dma_i = [0]

    def dma(out, in_):
        s = DSEMS[_dma_i[0] % 4]
        _dma_i[0] += 1
        em.op(Sy, lambda E, o=out, i=in_: E.dma_start(out=o, in_=i), inc=s, inc_by=16)
        em.unfenced["sync"] = 0

    def barrier():
        em.barrier()

    def TT(out, a, b, op, eng=V):
        em.op(eng, lambda E, o=out, x=a, y=b: E.tensor_tensor(o, x, y, op=op),
              [a, b], [out])

    def TS(out, a, s1, op0, s2=None, op1=None, eng=V):
        rd = [a] + [s for s in (s1, s2) if isinstance(s, bass.AP)]
        if s2 is None:
            em.op(eng, lambda E, o=out, x=a: E.tensor_scalar(o, x, s1, None, op0=op0),
                  rd, [out])
        else:
            em.op(eng, lambda E, o=out, x=a: E.tensor_scalar(o, x, s1, s2, op0=op0, op1=op1),
                  rd, [out])

    def STT(out, in0, scal, in1, op0, op1, eng=V):
        rd = [in0, in1] + ([scal] if isinstance(scal, bass.AP) else [])
        em.op(eng, lambda E, o=out, a=in0, b=in1: E.scalar_tensor_tensor(
            o, a, scal, b, op0=op0, op1=op1), rd, [out])

    def RED(out, a, op, absval=False, eng=V):
        em.op(eng, lambda E, o=out, x=a: E.tensor_reduce(
            o, x, axis=AX.X, op=op, apply_absolute_value=absval),
            [a], [out])

    def CP(out, a, eng=V):
        em.op(eng, lambda E, o=out, x=a: E.tensor_copy(o, x), [a], [out])

    def ACT(out, a, func, bias=0.0, scale=1.0):
        rd = [a] + ([bias] if isinstance(bias, bass.AP) else [])
        em.op(Sc, lambda E, o=out, x=a: E.activation(o, x, func, bias=bias, scale=scale),
              rd, [out])

    def RECIP(out, a, eng=V):
        em.op(eng, lambda E, o=out, x=a: E.reciprocal(o, x), [a], [out])

    def MM(out, lhsT, rhs, start, stop):
        em.op(Tn, lambda E, o=out, l=lhsT, r=rhs: E.matmul(o, l, r, start=start, stop=stop),
              [lhsT, rhs], [out])

    def TR(out, in_):
        em.op(Tn, lambda E, o=out, i=in_: E.transpose(o, i, ident[:, :]),
              [in_, ident], [out])

    def MEMSET(buf, val=0.0, eng=V):
        em.op(eng, lambda E, o=buf: E.memset(o, val), [], [buf])

    ft_ = lambda buf, j, w: buf[:, j * w:(j + 1) * w]

    dma(ident[:, :], ident_i)
    dma(mask64[0:64, :], mask64_i)
    dma(rowtmp[0:1, 0:64], iota64_i)
    dma(ones_row[0:1, :], ones_row_i)
    dma(meanw[:, 0:1], meanw_i)
    dma(vecs[:, :], vec_cols)
    dma(B8[0:1, 0:D], td_row)
    for e in (V, Sc, Tn):
        em.wait_now(e, SD)

    c_td = vecs[:, 0:NFT]
    c_tmk = vecs[:, NFT:2 * NFT]
    c_tmv = vecs[:, 2 * NFT:3 * NFT]
    c_tmr = vecs[:, 3 * NFT:4 * NFT]
    c_tmg = vecs[:, 4 * NFT:5 * NFT]
    c_gln = vecs[:, 5 * NFT:6 * NFT]
    c_bln = vecs[:, 6 * NFT:7 * NFT]
    tdrow = B8[0:1, 0:D]
    erow = B8[0:1, D:2 * D]
    wLrow = B8[0:1, 2 * D:3 * D]

    MEMSET(cEPS[:, 0:1], EPS)
    ACT(e_c[:, 0:NFT], c_td, AF.Exp)
    ACT(e_c[:, NFT:2 * NFT], e_c[:, 0:NFT], AF.Exp, scale=float(-L))
    ACT(erow, tdrow, AF.Exp)
    ACT(wLrow, erow, AF.Exp, scale=float(-L))
    em.wait_now(Tn, SS)
    MM(pb[0][:, 0:64], ones_row[0:1, :], rowtmp[0:1, 0:64], True, True)
    em.wait_now(V, ST)
    CP(iota_bc[:, :], pb[0][:, 0:64])
    em.wait_now(V, SS)
    for j in range(NFT):
        TS(ft_(SE, j, 64), iota_bc[:, :], e_c[:, j:j + 1], AL.mult)
        TS(ft_(M_stL, j, 64), iota_bc[:, :], 0.0, AL.mult, e_c[:, NFT + j:NFT + j + 1], AL.add)
    em.wait_now(Sc, SV)
    ACT(M_km[:, :], SE[:, :], AF.Exp)
    ACT(M_out[:, :], SE[:, :], AF.Exp, scale=-1.0)
    em.wait_now(Tn, SS)
    MM(pb[1][:, 0:512], ones_row[0:1, :], wLrow[0:1, 0:512], True, True)
    MM(pb[2][:, 0:512], ones_row[0:1, :], wLrow[0:1, 512:1024], True, True)
    em.wait_now(V, ST)
    CP(M_k2L[:, 0:512], pb[1][:, 0:512])
    CP(M_k2L[:, 512:1024], pb[2][:, 0:512])
    MEMSET(St[:, :], 0.0)
    MEMSET(AMAX[:, :], 0.0)
    barrier()

    xt = B_W[:, 0:D]
    xm = B_W[:, D:2 * D]
    sqt = B_W[:, 2 * D:3 * D]
    xnt = B_W[:, 3 * D:4 * D]
    stage = B9[:, 0:D]

    def ln_tiles(src_ap, n_tiles, col0):
        for ti in range(n_tiles):
            dma(xt, src_ap[ti * 128:(ti + 1) * 128, :])
            em.wait_now(V, SD)
            RED(stats[:, 0:1], xt, AL.add)
            TS(stats[:, 1:2], stats[:, 0:1], 1.0 / D, AL.mult)
            TS(xm, xt, stats[:, 1:2], AL.subtract)
            em.wait_now(Sc, SV)
            ACT(sqt, xm, AF.Square)
            em.wait_now(V, SS)
            RED(stats[:, 2:3], sqt, AL.add)
            em.wait_now(Sc, SV)
            ACT(stats[:, 3:4], stats[:, 2:3], AF.Sqrt, bias=cEPS[:, 0:1], scale=1.0 / D)
            em.wait_now(V, SS)
            RECIP(stats[:, 4:5], stats[:, 3:4])
            TS(xnt, xm, stats[:, 4:5], AL.mult)
            RED(stats[:, 5:6], xnt, AL.max, absval=True)
            TT(AMAX[:, 0:1], AMAX[:, 0:1], stats[:, 5:6], AL.max)
            em.wait_now(Tn, SV)
            for j in range(NFT):
                TR(pb[j // 4][:, (j % 4) * 128:(j % 4 + 1) * 128], xnt[:, j * 128:(j + 1) * 128])
            em.wait_now(V, ST)
            CP(stage[:, 0:512], pb[0][:, 0:512])
            CP(stage[:, 512:1024], pb[1][:, 0:512])
            em.wait_now(Sy, SV)
            bc = (col0 + ti) * 128
            dma(sp_xn[:, bc:bc + 128].rearrange("(j f) c -> f j c", f=128),
                stage[:, :].rearrange("p (j c) -> p j c", c=128))
            em.wait_now(V, SD)
            em.wait_now(Tn, SD)
            if ti % 12 == 11:
                barrier()

    ln_tiles(x_warm, cfg.n_warm_tiles, 0)
    ln_tiles(x_own, cfg.n_own_tiles, cfg.n_warm_tiles)
    barrier()

    def cross_part_max(src_col_ap, dst_row_off):
        TR(pb[0][0:1, 0:128], src_col_ap)
        em.wait_now(V, ST)
        CP(rowtmp[0:1, 64:192], pb[0][0:1, 0:128])
        RED(rowtmp[0:1, dst_row_off:dst_row_off + 1], rowtmp[0:1, 64:192], AL.max)
        em.wait_now(Tn, SV)

    def allreduce_scales(idx, src_off, nvals, sc_cols):
        em.wait_now(Sy, SV)
        dma(cc_in[idx][0:1, 0:nvals], rowtmp[0:1, src_off:src_off + nvals])
        em.wait_now(Gp, SD)
        em.op(Gp, lambda E: E.collective_compute(
            "AllReduce", AL.max,
            replica_groups=[list(range(cfg.n_cores))],
            ins=[cc_in[idx][0:1, 0:nvals]],
            outs=[cc_out[idx][0:1, 0:nvals]]), inc=SG)
        em.wait_now(Sy, SG)
        dma(rowtmp[0:1, 200:200 + nvals], cc_out[idx][0:1, 0:nvals])
        em.wait_now(V, SD)
        for c in range(nvals):
            TS(rowtmp[0:1, 210 + c:211 + c], rowtmp[0:1, 200 + c:201 + c],
               1.0 / QMAX, AL.mult, 1e-12, AL.max)
        em.wait_now(Tn, SV)
        for c in range(nvals):
            MM(pb[0][:, c:c + 1], ones_row[0:1, :], rowtmp[0:1, 210 + c:211 + c], True, True)
        em.wait_now(V, ST)
        for c in range(nvals):
            col = sc_cols[c]
            CP(SCAL[:, col:col + 1], pb[0][:, c:c + 1])
            RECIP(SCAL[:, col + 1:col + 2], SCAL[:, col:col + 1])
            TS(SCAL[:, col + 2:col + 3], SCAL[:, col:col + 1], -1.0, AL.mult)

    cross_part_max(AMAX[:, 0:1], 0)
    allreduce_scales(0, 0, 1, [0])
    barrier()

    xn_g = B0
    xmix = B1
    projg = B2
    own0 = NWP + 1
    warm0 = 1

    def load_weightT(wt_ap):
        dma(B_W[:, 0:NFT * D].rearrange("p (j f) -> p j f", f=D),
            wt_ap.rearrange("(j r) f -> r j f", r=128))

    def proj_group(src_col0, n, mix_c, dst, dst0, sig):
        dma(xn_g[:, 0:NFT * (TB + 1)].rearrange("p (j t) -> p j t", t=TB + 1)[:, :, 0:n + 1],
            sp_xn[:, src_col0 - 1:src_col0 + n].rearrange("(j f) c -> f j c", f=128))
        em.wait_now(V, SD)
        for j in range(NFT):
            xg = ft_(xn_g, j, TB + 1)
            mx = ft_(xmix, j, TB)[:, 0:n]
            TT(mx, xg[:, 0:n], xg[:, 1:n + 1], AL.subtract)
            STT(mx, mx, mix_c[:, j:j + 1], xg[:, 1:n + 1], AL.mult, AL.add)
        em.wait_now(Tn, SV)
        for ot in range(NFT):
            bank = pb[ot % 2]
            if ot >= 2:
                em.wait_now(Tn, SS if sig else SV)
            for j in range(NFT):
                MM(bank[:, 0:n], ft_(B_W, j, D)[:, ot * 128:(ot + 1) * 128],
                   ft_(xmix, j, TB)[:, 0:n], j == 0, j == NFT - 1)
            po = ft_(projg, ot, TB)[:, 0:n]
            if sig:
                em.wait_now(Sc, ST)
                ACT(po, bank[:, 0:n], AF.Sigmoid)
            else:
                em.wait_now(V, ST)
                CP(po, bank[:, 0:n])
        em.wait_now(Sy, SS if sig else SV)
        dma(dst[:, dst0:dst0 + n].rearrange("(j f) c -> f j c", f=128),
            projg[:, 0:NFT * TB].rearrange("p (j t) -> p j t", t=TB)[:, :, 0:n])
        em.wait_now(Tn, SD)
        em.wait_now(V, SD) if not sig else em.wait_now(Sc, SD)

    for (wn, spill, mix_c, sig) in (("WkT", sp_k, c_tmk, False), ("WvT", sp_v, c_tmv, False),
                                    ("WrT", sp_r, c_tmr, False), ("WgT", sp_g, c_tmg, True)):
        load_weightT(WT[wn])
        em.wait_now(Tn, SD)
        kv = wn in ("WkT", "WvT")
        if kv and TWARM > 0:
            for gc in range(0, TWARM, TB):
                proj_group(warm0 + gc, min(TB, TWARM - gc), mix_c, spill, gc, sig)
        for gc in range(0, TOWN, TB):
            proj_group(own0 + gc, min(TB, TOWN - gc), mix_c, spill,
                       (TWARM + gc) if kv else gc, sig)
        barrier()

    load_weightT(WT["WoT"])
    barrier()

    k_g, v_g, r_g, gg_g = B3, B4, B5, B6
    ktm_g = B8
    vtm_sub = B0[:, 0:1024]
    k2tm_sub = B0[:, 1024:2048]
    kt_sub = B0[:, 2048:3072]
    scr512 = B0[:, 3072:3584]
    StEntry = B0[:, 3584:4096]
    Atl_g = B1
    Ug = B7
    attn_g = B2
    y_g = B7
    ym_g = B0
    s_g = B1

    def mixer_ph1(sg, own):
        t0g = sg * 128
        ktm = ktm_g[:, sg * 1024:(sg + 1) * 1024]
        for j in range(NFT):
            kv_ = ft_(k_g, j, TB)[:, t0g:t0g + 128].rearrange("p (c s) -> p c s", s=L)
            mv_ = ft_(M_km, j, L).unsqueeze(1).broadcast_to([128, 2, L])
            ov_ = kt_sub[:, j * 128:(j + 1) * 128].rearrange("p (c s) -> p c s", s=L)
            em.op(V, lambda E, o=ov_, a=kv_, b=mv_: E.tensor_tensor(o, a, b, op=AL.mult),
                  [ft_(k_g, j, TB)], [kt_sub])
        em.wait_now(Tn, SV)
        for j in range(NFT):
            TR(pb[2 + j // 4][:, (j % 4) * 128:(j % 4 + 1) * 128],
               kt_sub[:, j * 128:(j + 1) * 128])
        em.wait_now(V, ST)
        CP(ktm[:, 0:512], pb[2][:, 0:512])
        CP(ktm[:, 512:1024], pb[3][:, 0:512])
        em.wait_now(Tn, SV)
        for j in range(NFT):
            TR(pb[2 + j // 4][:, (j % 4) * 128:(j % 4 + 1) * 128],
               ft_(v_g, j, TB)[:, t0g:t0g + 128])
        em.wait_now(V, ST)
        CP(vtm_sub[:, 0:512], pb[2][:, 0:512])
        CP(vtm_sub[:, 512:1024], pb[3][:, 0:512])
        TT(k2tm_sub[:, :], ktm[:, :], M_k2L[:, :], AL.mult)
        em.wait_now(Tn, SV)
        for ci in range(2):
            cp_ = ci * 64
            c = sg * 2 + ci
            for j in range(NFT):
                for hp in range(2):
                    h = 2 * j + hp
                    MM(pb[7][hp * 64:hp * 64 + 64, j * 64:j * 64 + 64],
                       vtm_sub[cp_:cp_ + 64, h * 64:h * 64 + 64],
                       k2tm_sub[cp_:cp_ + 64, h * 64:h * 64 + 64],
                       True, True)
            em.wait_now(V, ST)
            CP(Ug[:, c * 512:(c + 1) * 512], pb[7][:, :])
            em.wait_now(Tn, SV)
        if own:
            for ci in range(2):
                cp_ = ci * 64
                tok0 = t0g + ci * 64
                for j in range(NFT):
                    for hp in range(2):
                        MM(pb[4 + j // 4][cp_:cp_ + 64,
                                          (j % 4) * 128 + hp * 64:(j % 4) * 128 + hp * 64 + 64],
                           ft_(v_g, j, TB)[hp * 64:hp * 64 + 64, tok0:tok0 + 64],
                           ft_(r_g, j, TB)[hp * 64:hp * 64 + 64, tok0:tok0 + 64],
                           True, True)
            em.wait_now(V, ST)
            for b in range(2):
                pv = pb[4 + b][:, 0:512].rearrange("p (k s) -> p k s", s=64)
                mk = mask64[0:64, :].unsqueeze(0).unsqueeze(2) if False else mask64[0:64, :]
                mkb = mk.unsqueeze(1).broadcast_to([64, 8, 64])
                ov = Atl_g[:, sg * 1024 + b * 512:sg * 1024 + (b + 1) * 512] \
                    .rearrange("p (k s) -> p k s", s=64)
                pv2 = pb[4 + b][0:64, 0:512].rearrange("p (k s) -> p k s", s=64)
                ov2 = Atl_g[0:64, sg * 1024 + b * 512:sg * 1024 + (b + 1) * 512] \
                    .rearrange("p (k s) -> p k s", s=64)
                pv3 = pb[4 + b][64:128, 0:512].rearrange("p (k s) -> p k s", s=64)
                ov3 = Atl_g[64:128, sg * 1024 + b * 512:sg * 1024 + (b + 1) * 512] \
                    .rearrange("p (k s) -> p k s", s=64)
                em.op(V, lambda E, o=ov2, a=pv2, m=mkb: E.tensor_tensor(o, a, m, op=AL.mult),
                      [pb[4 + b]], [Atl_g])
                em.op(V, lambda E, o=ov3, a=pv3, m=mkb: E.tensor_tensor(o, a, m, op=AL.mult),
                      [pb[4 + b]], [Atl_g])
            em.wait_now(Tn, SV)

    def mixer_scan(nch):
        CP(StEntry, St[:, :])
        TT(scr512, St[:, :], M_stL[:, :], AL.mult)
        TT(Ug[:, 0:512], scr512, Ug[:, 0:512], AL.add)
        for c in range(1, nch):
            TT(scr512, Ug[:, (c - 1) * 512:c * 512], M_stL[:, :], AL.mult)
            TT(Ug[:, c * 512:(c + 1) * 512], scr512, Ug[:, c * 512:(c + 1) * 512], AL.add)
        CP(St[:, :], Ug[:, (nch - 1) * 512:nch * 512])

    def mixer_ph3(nch):
        em.wait_now(Tn, SV)
        for c in range(nch):
            bank = pb[c % 2]
            if c >= 2:
                em.wait_now(Tn, SV)
            sg = c // 2
            cp_ = (c % 2) * 64
            st_src = StEntry if c == 0 else Ug[:, (c - 1) * 512:c * 512]
            for j in range(NFT):
                for hp in range(2):
                    h = 2 * j + hp
                    MM(bank[hp * 64:hp * 64 + 64, j * 64:j * 64 + 64],
                       ktm_g[:, sg * 1024:(sg + 1) * 1024][cp_:cp_ + 64, h * 64:h * 64 + 64],
                       Atl_g[cp_:cp_ + 64, sg * 1024 + j * 128 + hp * 64:
                             sg * 1024 + j * 128 + hp * 64 + 64],
                       True, False)
                    MM(bank[hp * 64:hp * 64 + 64, j * 64:j * 64 + 64],
                       st_src[hp * 64:hp * 64 + 64, j * 64:j * 64 + 64],
                       ft_(r_g, j, TB)[hp * 64:hp * 64 + 64, c * 64:c * 64 + 64],
                       False, True)
            em.wait_now(V, ST)
            pv = bank[:, 0:512].rearrange("p (j s) -> p j s", s=64)
            mo = M_out[:, :].rearrange("p (j s) -> p j s", s=64)
            ov = attn_g[:, 0:NFT * TB].rearrange("p (j t) -> p j t", t=TB)[:, :, c * 64:c * 64 + 64]
            em.op(V, lambda E, o=ov, a=pv, m=mo: E.tensor_tensor(o, a, m, op=AL.mult),
                  [bank, M_out], [attn_g])

    def mixer_group(kvc0, n, own, ownc0=0):
        dma(k_g[:, 0:NFT * TB].rearrange("p (j t) -> p j t", t=TB)[:, :, 0:n],
            sp_k[:, kvc0:kvc0 + n].rearrange("(j f) c -> f j c", f=128))
        dma(v_g[:, 0:NFT * TB].rearrange("p (j t) -> p j t", t=TB)[:, :, 0:n],
            sp_v[:, kvc0:kvc0 + n].rearrange("(j f) c -> f j c", f=128))
        if own:
            dma(r_g[:, 0:NFT * TB].rearrange("p (j t) -> p j t", t=TB)[:, :, 0:n],
                sp_r[:, ownc0:ownc0 + n].rearrange("(j f) c -> f j c", f=128))
        em.wait_now(V, SD)
        em.wait_now(Tn, SD)
        for sg in range(n // 128):
            mixer_ph1(sg, own)
        mixer_scan(n // L)
        if own:
            mixer_ph3(n // L)

    if TWARM > 0:
        for gi, gc in enumerate(range(0, TWARM, TB)):
            mixer_group(gc, min(TB, TWARM - gc), own=False)
            if gi % 2 == 1:
                barrier()
        barrier()

    for gc in range(0, TOWN, TB):
        n = min(TB, TOWN - gc)
        mixer_group(TWARM + gc, n, own=True, ownc0=gc)
        dma(gg_g[:, 0:NFT * TB].rearrange("p (j t) -> p j t", t=TB)[:, :, 0:n],
            sp_g[:, gc:gc + n].rearrange("(j f) c -> f j c", f=128))
        em.wait_now(V, SD)
        for j in range(NFT):
            TT(ft_(attn_g, j, TB)[:, 0:n], ft_(attn_g, j, TB)[:, 0:n],
               ft_(gg_g, j, TB)[:, 0:n], AL.mult)
        em.wait_now(Tn, SV)
        for ot in range(NFT):
            bank = pb[ot % 2]
            if ot >= 2:
                em.wait_now(Tn, SV)
            for j in range(NFT):
                MM(bank[:, 0:n], ft_(B_W, j, D)[:, ot * 128:(ot + 1) * 128],
                   ft_(attn_g, j, TB)[:, 0:n], j == 0, j == NFT - 1)
            em.wait_now(V, ST)
            CP(ft_(y_g, ot, TB)[:, 0:n], bank[:, 0:n])
        em.wait_now(Tn, SV)
        for j in range(NFT):
            MM(pb[2][0:1, 0:n], meanw[:, 0:1], ft_(y_g, j, TB)[:, 0:n], j == 0, j == NFT - 1)
        em.wait_now(V, ST)
        CP(rowtmp[0:1, 500:500 + n], pb[2][0:1, 0:n])
        em.wait_now(Tn, SV)
        MM(pb[3][:, 0:n], ones_row[0:1, :], rowtmp[0:1, 500:500 + n], True, True)
        em.wait_now(V, ST)
        for j in range(NFT):
            TT(ft_(ym_g, j, TB)[:, 0:n], ft_(y_g, j, TB)[:, 0:n], pb[3][:, 0:n], AL.subtract)
        em.wait_now(Sc, SV)
        for j in range(NFT):
            ACT(ft_(s_g, j, TB)[:, 0:n], ft_(ym_g, j, TB)[:, 0:n], AF.Square)
        em.wait_now(Tn, SS)
        for j in range(NFT):
            MM(pb[2][0:1, 0:n], meanw[:, 0:1], ft_(s_g, j, TB)[:, 0:n], j == 0, j == NFT - 1)
        em.wait_now(Sc, ST)
        ACT(rowtmp[0:1, 300:300 + n], pb[2][0:1, 0:n], AF.Sqrt, bias=cEPS[0:1, 0:1])
        em.wait_now(V, SS)
        RECIP(rowtmp[0:1, 300:300 + n], rowtmp[0:1, 300:300 + n])
        em.wait_now(Tn, SV)
        MM(pb[3][:, 0:n], ones_row[0:1, :], rowtmp[0:1, 300:300 + n], True, True)
        em.wait_now(V, ST)
        for j in range(NFT):
            sj = ft_(s_g, j, TB)[:, 0:n]
            TT(sj, ft_(ym_g, j, TB)[:, 0:n], pb[3][:, 0:n], AL.mult)
            TS(sj, sj, c_gln[:, j:j + 1], AL.mult, c_bln[:, j:j + 1], AL.add)
        em.wait_now(Sy, SV)
        dma(sp_s[:, gc:gc + n].rearrange("(j f) c -> f j c", f=128),
            s_g[:, 0:NFT * TB].rearrange("p (j t) -> p j t", t=TB)[:, :, 0:n])
        barrier()

    assert NFT * TB >= 2 * TOWN, "buffer layout assumes TB*8 >= 2*TOWN"
    xq_bufs = [B3, B4, B5, B6]
    def xq_t(j):
        return xq_bufs[j // 2][:, (j % 2) * TOWN:(j % 2 + 1) * TOWN]

    tz = B0[:, 0:TOWN]
    tg = B0[:, TOWN:2 * TOWN]
    ti16 = B1[:, 0:(TOWN + 1) // 2].bitcast(I16)[:, 0:TOWN]
    tu16 = B1[:, 0:(TOWN + 1) // 2].bitcast(U16)[:, 0:TOWN]

    def quant4(out, x, scol, n, eng=V, z=None, g=None, i16=None):
        s = SCAL[:, scol:scol + 1]
        rs = SCAL[:, scol + 1:scol + 2]
        ns = SCAL[:, scol + 2:scol + 3]
        z = z if z is not None else tz
        g = g if g is not None else tg
        i16 = i16 if i16 is not None else ti16
        TS(z[:, 0:n], x, rs, AL.mult, eng=eng)
        TS(i16[:, 0:n], x, rs, AL.mult, eng=eng)
        TT(g[:, 0:n], z[:, 0:n], i16[:, 0:n], AL.subtract, eng=eng)
        STT(z[:, 0:n], g[:, 0:n], 0.5, i16[:, 0:n], AL.is_gt, AL.add, eng=eng)
        STT(g[:, 0:n], g[:, 0:n], -0.5, z[:, 0:n], AL.is_ge, AL.add, eng=eng)
        TS(out, g[:, 0:n], s, AL.mult, ns, AL.add, eng=eng)

    def gp_round_to_int(out_i16, z, g, r0f, n):
        TS(out_i16[:, 0:n], z[:, 0:n], 1.0, AL.mult, eng=Gp)
        TS(r0f[:, 0:n], out_i16[:, 0:n], 1.0, AL.mult, eng=Gp)
        TT(g[:, 0:n], z[:, 0:n], r0f[:, 0:n], AL.subtract, eng=Gp)
        TS(z[:, 0:n], g[:, 0:n], 0.5, AL.is_gt, eng=Gp)
        TT(r0f[:, 0:n], z[:, 0:n], r0f[:, 0:n], AL.add, eng=Gp)
        TS(g[:, 0:n], g[:, 0:n], -0.5, AL.is_ge, eng=Gp)
        TT(g[:, 0:n], g[:, 0:n], r0f[:, 0:n], AL.add, eng=Gp)
        TS(z[:, 0:n], g[:, 0:n], -1.0, AL.add, eng=Gp)

    def v_round_i16(out_i16, z, g, n):
        TS(out_i16[:, 0:n], z[:, 0:n], 1.0, AL.mult, eng=V)
        TT(g[:, 0:n], z[:, 0:n], out_i16[:, 0:n], AL.subtract, eng=V)
        STT(z[:, 0:n], g[:, 0:n], 0.5, out_i16[:, 0:n], AL.is_gt, AL.add, eng=V)
        STT(g[:, 0:n], g[:, 0:n], -0.5, z[:, 0:n], AL.is_ge, AL.add, eng=V)
        TS(out_i16[:, 0:n], g[:, 0:n], 1.0, AL.mult, -1.0, AL.add, eng=V)

    for bidx, buf in enumerate(xq_bufs):
        dma(buf[:, 0:2 * TOWN].rearrange("p (j t) -> p j t", t=TOWN),
            sp_xn[bidx * 256:(bidx + 1) * 256, own0:own0 + TOWN]
            .rearrange("(j f) c -> f j c", f=128))
    em.wait_now(V, SD)
    for j in range(NFT):
        quant4(xq_t(j), xq_t(j), 0, TOWN)
    barrier()

    wnat = B1
    wdqT = B7
    ustg4 = B2[:, 0:4 * FTB]
    wsc = B2[:, 4 * FTB:4 * FTB + 224]
    wz = B0[:, 0:FF]
    wgs = B8[:, 0:FF]
    wi16 = B9[:, 0:FF // 2].bitcast(I16)[:, 0:FF]

    def dequant_tile(wtile, nblk):
        bm = wsc[:, 0:nblk]
        sw = wsc[:, 96:96 + nblk]
        em.op(V, lambda E, o=bm.unsqueeze(2), x=wtile.rearrange("p (b s) -> p b s", s=32):
              E.tensor_reduce(o, x, axis=AX.X, op=AL.max, apply_absolute_value=True),
              [wtile], [wsc])
        TS(sw, bm, 1.0 / 7.0, AL.mult, 1e-12, AL.max)
        rw = bm
        RECIP(rw, sw)
        rwb = rw.unsqueeze(2).broadcast_to([128, nblk, 32])
        swb = sw.unsqueeze(2).broadcast_to([128, nblk, 32])
        nw = nblk * 32
        wv = wtile.rearrange("p (b s) -> p b s", s=32)
        zv = wz[:, 0:nw].rearrange("p (b s) -> p b s", s=32)
        i16v = wi16[:, 0:nw].rearrange("p (b s) -> p b s", s=32)
        em.op(V, lambda E, o=zv, a=wv, b=rwb: E.tensor_tensor(o, a, b, op=AL.mult),
              [wtile, wsc], [wz])
        em.op(V, lambda E, o=i16v, a=wv, b=rwb: E.tensor_tensor(o, a, b, op=AL.mult),
              [wtile, wsc], [wi16])
        TT(wgs[:, 0:nw], wz[:, 0:nw], wi16[:, 0:nw], AL.subtract)
        STT(wz[:, 0:nw], wgs[:, 0:nw], 0.5, wi16[:, 0:nw], AL.is_gt, AL.add)
        STT(wgs[:, 0:nw], wgs[:, 0:nw], -0.5, wz[:, 0:nw], AL.is_ge, AL.add)
        TS(wz[:, 0:nw], wgs[:, 0:nw], 1.0, AL.mult, -1.0, AL.add)
        ov = wtile.rearrange("p (b s) -> p b s", s=32)
        yv = wz[:, 0:nw].rearrange("p (b s) -> p b s", s=32)
        em.op(V, lambda E, o=ov, a=yv, b=swb: E.tensor_tensor(o, a, b, op=AL.mult),
              [wz, wsc], [wtile])

    NOS = NIT // 4

    def ffn_matmul_weight(w_in, spill, amax_col):
        for os_ in range(NOS):
            em.wait_now(Sy, ST)
            dma(wnat[:, 0:4 * D].rearrange("p (q f) -> p q f", f=D),
                w_in[os_ * 512:(os_ + 1) * 512, :].rearrange("(q r) f -> r q f", r=128))
            em.wait_now(V, SD)
            for q in range(4):
                dequant_tile(ft_(wnat, q, D), 32)
                em.wait_now(Tn, SV)
                for j in range(NFT):
                    TR(pb[2 + j // 4][:, (j % 4) * 128:(j % 4 + 1) * 128],
                       ft_(wnat, q, D)[:, j * 128:(j + 1) * 128])
                em.wait_now(V, ST)
                for half in range(2):
                    for j4 in range(4):
                        j = half * 4 + j4
                        CP(wdqT[:, j * 512 + q * 128:j * 512 + (q + 1) * 128],
                           pb[2 + half][:, j4 * 128:(j4 + 1) * 128])
                em.wait_now(Tn, SV)
            for tb in range(NFTB):
                em.wait_now(V, SD)
                for q in range(4):
                    bank = pb[q % 2]
                    if q >= 2 or tb > 0:
                        em.wait_now(Tn, SV)
                    for j in range(NFT):
                        MM(bank[:, 0:FTB], wdqT[:, j * 512 + q * 128:j * 512 + (q + 1) * 128],
                           xq_t(j)[:, tb * FTB:(tb + 1) * FTB], j == 0, j == NFT - 1)
                    em.wait_now(V, ST)
                    CP(ustg4[:, q * FTB:(q + 1) * FTB], bank[:, 0:FTB])
                    RED(stats[:, 6:7], ustg4[:, q * FTB:(q + 1) * FTB], AL.max, absval=True)
                    TT(AMAX[:, amax_col:amax_col + 1], AMAX[:, amax_col:amax_col + 1],
                       stats[:, 6:7], AL.max)
                em.wait_now(Sy, SV)
                dma(spill[os_ * 512:(os_ + 1) * 512, tb * FTB:(tb + 1) * FTB]
                    .rearrange("(q r) t -> r q t", r=128),
                    ustg4[:, 0:4 * FTB].rearrange("p (q t) -> p q t", t=FTB))
            if os_ % 2 == 1:
                barrier()
        barrier()

    ffn_matmul_weight(Wup, sp_upr, 1)
    ffn_matmul_weight(Wgate, sp_gater, 2)

    for mt in range(NFT):
        dma(wnat[:, 0:FF], Wdown[mt * 128:(mt + 1) * 128, :])
        em.wait_now(V, SD)
        dequant_tile(wnat[:, 0:FF], 96)
        em.wait_now(Tn, SV)
        for it in range(NIT):
            TR(pb[2 + (it % 8) // 4][:, (it % 4) * 128:(it % 4 + 1) * 128],
               wnat[:, it * 128:(it + 1) * 128])
            if it % 8 == 7:
                em.wait_now(V, ST)
                for b in range(2):
                    CP(stage[:, b * 512:(b + 1) * 512], pb[2 + b][:, 0:512])
                em.wait_now(Sy, SV)
                dma(sp_wd[(it - 7) * 128:(it + 1) * 128, mt * 128:(mt + 1) * 128]
                    .rearrange("(k r) c -> r k c", r=128),
                    stage[:, 0:1024].rearrange("p (k c) -> p k c", c=128))
                em.wait_now(Tn, SD)
                em.wait_now(V, SD)
        if mt % 3 == 2:
            barrier()
    barrier()

    cross_part_max(AMAX[:, 1:2], 0)
    cross_part_max(AMAX[:, 2:3], 1)
    allreduce_scales(1, 0, 2, [3, 6])
    barrier()

    class QSet:
        pass
    sV = QSet()
    sV.upt = B3[:, 0:TOWN]; sV.gat = B3[:, TOWN:2 * TOWN]
    sV.sig = B4[:, 0:TOWN]; sV.gst = B4[:, TOWN:2 * TOWN]
    sV.z = B6[:, 0:TOWN]; sV.g = B6[:, TOWN:2 * TOWN]
    sV.i16 = B5[:, 0:TOWN // 2].bitcast(I16)[:, 0:TOWN]
    sV.u16 = B5[:, TOWN // 2:TOWN].bitcast(U16)[:, 0:TOWN]
    sV.gi16 = B5[:, TOWN:TOWN + TOWN // 2].bitcast(I16)[:, 0:TOWN]
    sV.r0f = None
    sG = QSet()
    sG.upt = B0[:, 0:TOWN]; sG.gat = B0[:, TOWN:2 * TOWN]
    sG.sig = B1[:, 0:TOWN]; sG.gst = B1[:, TOWN:2 * TOWN]
    sG.z = B2[:, 0:TOWN]; sG.g = B2[:, TOWN:2 * TOWN]
    sG.i16 = B7[:, 0:TOWN // 2].bitcast(I16)[:, 0:TOWN]
    sG.u16 = B7[:, TOWN // 2:TOWN].bitcast(U16)[:, 0:TOWN]
    sG.gi16 = B7[:, TOWN:TOWN + TOWN // 2].bitcast(I16)[:, 0:TOWN]
    sG.r0f = B9[:, 0:TOWN]

    qlast = {id(sV): None, id(sG): None}

    def qs_wait_reuse(S, eng):
        snap = qlast[id(S)]
        if snap is not None:
            ep, cnts = snap
            if ep == em.nbar:
                for s_, v_ in cnts.items():
                    em.wait(Sy, s_, v_)

    def qs_record(S, eng):
        es = esem_name[eng]
        if em.unfenced[SEM_OWNER[es]] > 0:
            em.fence(SEM_OWNER[es], es)
        if em.unfenced["scalar"] > 0:
            em.fence("scalar", "ss")
        if em.unfenced["vector"] > 0:
            em.fence("vector", "sv")
        snap = {es: em.counts[es], "ss": em.counts["ss"], "sv": em.counts["sv"]}
        for ds in DSEMS:
            snap[ds] = em.counts[ds]
        qlast[id(S)] = (em.nbar, snap)

    esem_name = {V: "sv", Gp: "sg", Sc: "ss"}

    def p5_itile(it, eng, S):
        r0_, r1_ = it * 128, it * 128 + 128
        qs_wait_reuse(S, eng)
        dma(S.upt, sp_upr[r0_:r1_, :])
        dma(S.gat, sp_gater[r0_:r1_, :])
        em.wait_now(eng, SD)
        rs2 = SCAL[:, 4:5]; rs3 = SCAL[:, 7:8]; s3 = SCAL[:, 6:7]
        if eng == Gp:
            TT(S.z, S.upt, rs2.broadcast_to([128, TOWN]), AL.mult, eng=Gp)
            gp_round_to_int(S.i16, S.z, S.g, S.r0f, TOWN)
            TS(S.i16, S.z, 1.0, AL.mult, eng=Gp)
        else:
            TS(S.z, S.upt, rs2, AL.mult, eng=V)
            v_round_i16(S.i16, S.z, S.g, TOWN)
        em.wait_now(Sy, esem_name[eng] if False else ("sg" if eng == Gp else "sv"))
        dma(sp_upq[r0_:r1_, :], S.i16)
        if eng == Gp:
            TT(S.z, S.gat, rs3.broadcast_to([128, TOWN]), AL.mult, eng=Gp)
            gp_round_to_int(S.gi16, S.z, S.g, S.r0f, TOWN)
            TT(S.gat, S.z, s3.broadcast_to([128, TOWN]), AL.mult, eng=Gp)
        else:
            TS(S.z, S.gat, rs3, AL.mult, eng=V)
            v_round_i16(S.gi16, S.z, S.g, TOWN)
            TS(S.gat, S.gi16, s3, AL.mult, eng=V)
        em.wait_now(Sc, "sg" if eng == Gp else "sv")
        ACT(S.sig, S.gat, AF.Sigmoid)
        em.wait_now(eng, SS)
        if eng == Gp:
            TS(S.z, S.sig, 65536.0, AL.mult, eng=Gp)
            TS(S.z, S.z, 65535.0, AL.min, eng=Gp)
            TS(S.u16, S.z, 1.0, AL.mult, eng=Gp)
            TS(S.r0f, S.u16, 1.0, AL.mult, eng=Gp)
            TT(S.g, S.z, S.r0f, AL.subtract, eng=Gp)
            TS(S.z, S.g, 0.5, AL.is_gt, eng=Gp)
            TT(S.r0f, S.z, S.r0f, AL.add, eng=Gp)
            TS(S.g, S.g, -0.5, AL.is_ge, eng=Gp)
            TT(S.g, S.g, S.r0f, AL.add, eng=Gp)
            TS(S.g, S.g, -1.0, AL.add, eng=Gp)
            TS(S.sig, S.g, 1.0 / 65536.0, AL.mult, eng=Gp)
        else:
            TS(S.z, S.sig, 65536.0, AL.mult, 65535.0, AL.min, eng=V)
            TS(S.u16, S.z, 1.0, AL.mult, eng=V)
            TT(S.g, S.z, S.u16, AL.subtract, eng=V)
            STT(S.z, S.g, 0.5, S.u16, AL.is_gt, AL.add, eng=V)
            STT(S.g, S.g, -0.5, S.z, AL.is_ge, AL.add, eng=V)
            TS(S.sig, S.g, 1.0 / 65536.0, AL.mult, -1.0 / 65536.0, AL.add, eng=V)
        TT(S.gst, S.gat, S.sig, AL.mult, eng=eng)
        if eng == Gp:
            em.wait_now(V, "sg")
        RED(stats[:, 7:8], S.gst, AL.max, absval=True, eng=V)
        TT(AMAX[:, 3:4], AMAX[:, 3:4], stats[:, 7:8], AL.max, eng=V)
        em.wait_now(Sy, "sg" if eng == Gp else "sv")
        dma(sp_gs[r0_:r1_, :], S.gst)
        qs_record(S, eng)

    def p6_itile(it, eng, S):
        r0_, r1_ = it * 128, it * 128 + 128
        qs_wait_reuse(S, eng)
        dma(S.gst, sp_gs[r0_:r1_, :])
        dma(S.i16, sp_upq[r0_:r1_, :])
        em.wait_now(eng, SD)
        rs4 = SCAL[:, 10:11]; s4 = SCAL[:, 9:10]; s2 = SCAL[:, 3:4]
        if eng == Gp:
            TT(S.z, S.gst, rs4.broadcast_to([128, TOWN]), AL.mult, eng=Gp)
            gp_round_to_int(S.gi16, S.z, S.g, S.r0f, TOWN)
            TT(S.gat, S.z, s4.broadcast_to([128, TOWN]), AL.mult, eng=Gp)
            TS(S.z, S.i16, 1.0, AL.mult, eng=Gp)
            TT(S.upt, S.z, s2.broadcast_to([128, TOWN]), AL.mult, eng=Gp)
        else:
            TS(S.z, S.gst, rs4, AL.mult, eng=V)
            v_round_i16(S.gi16, S.z, S.g, TOWN)
            TS(S.gat, S.gi16, s4, AL.mult, eng=V)
            TS(S.upt, S.i16, s2, AL.mult, eng=V)
        TT(S.gst, S.gat, S.upt, AL.mult, eng=eng)
        if eng == Gp:
            em.wait_now(V, "sg")
        RED(stats[:, 8:9], S.gst, AL.max, absval=True, eng=V)
        TT(AMAX[:, 4:5], AMAX[:, 4:5], stats[:, 8:9], AL.max, eng=V)
        em.wait_now(Sy, "sg" if eng == Gp else "sv")
        dma(sp_o[r0_:r1_, :], S.gst)
        qs_record(S, eng)

    for it in range(NIT):
        p5_itile(it, V, sV if it % 2 == 0 else sG)
        if it % 12 == 11:
            barrier()

    barrier()
    cross_part_max(AMAX[:, 3:4], 0)
    allreduce_scales(2, 0, 1, [9])
    barrier()

    for it in range(NIT):
        p6_itile(it, V, sV if it % 2 == 0 else sG)
        if it % 12 == 11:
            barrier()

    barrier()
    cross_part_max(AMAX[:, 4:5], 0)
    allreduce_scales(3, 0, 1, [12])
    barrier()

    oq4 = B3[:, 0:4 * FTB]
    oq = [oq4[:, q * FTB:(q + 1) * FTB] for q in range(4)]
    wd = B7
    g2t = B5
    st_ = B6
    xrow = B0[:, 0:D]
    orow = B0[:, D:2 * D]
    prodst = B1

    for tb in range(NFTB):
        c0 = tb * FTB
        for isl in range(NOS):
            em.wait_now(Sy, ST)
            em.wait_now(V, ST)
            dma(oq4.rearrange("p (q t) -> p q t", t=FTB),
                sp_o[isl * 512:(isl + 1) * 512, c0:c0 + FTB]
                .rearrange("(q r) t -> r q t", r=128))
            dma(wd[:, 0:4 * D].rearrange("p (q f) -> p q f", f=D),
                sp_wd[isl * 512:(isl + 1) * 512, :].rearrange("(q r) f -> r q f", r=128))
            em.wait_now(V, SD)
            for q in range(4):
                quant4(oq[q], oq[q], 12, FTB, z=B6[:, 0:FTB], g=B6[:, FTB:2 * FTB],
                       i16=B5[:, 0:FTB // 2].bitcast(I16)[:, 0:FTB])
            em.wait_now(Tn, SV)
            for m in range(NFT):
                for q in range(4):
                    MM(pb[m][:, 0:FTB], ft_(wd, q, D)[:, m * 128:(m + 1) * 128],
                       oq[q], isl == 0 and q == 0, isl == NOS - 1 and q == 3)
            em.wait_now(V, ST)
            em.wait_now(Sy, ST)
        barrier()
        em.wait_now(Sc, ST)
        for m in range(NFT):
            ACT(ft_(g2t, m, FTB), pb[m][:, 0:FTB], AF.Silu)
        dma(st_[:, 0:NFT * FTB].rearrange("p (j t) -> p j t", t=FTB),
            sp_s[:, c0:c0 + FTB].rearrange("(j f) c -> f j c", f=128))
        em.wait_now(V, SD)
        em.wait_now(V, SS)
        for j in range(NFT):
            TT(ft_(g2t, j, FTB), ft_(st_, j, FTB), ft_(g2t, j, FTB), AL.mult)
        em.wait_now(Tn, SV)
        for tt in range(FTB // 128):
            for j in range(NFT):
                TR(pb[j // 4][:, (j % 4) * 128:(j % 4 + 1) * 128],
                   ft_(g2t, j, FTB)[:, tt * 128:(tt + 1) * 128])
            em.wait_now(V, ST)
            CP(prodst[:, 0:512], pb[0][:, 0:512])
            CP(prodst[:, 512:1024], pb[1][:, 0:512])
            row0 = c0 + tt * 128
            dma(xrow, x_own[1 + row0:1 + row0 + 128, :])
            em.wait_now(V, SD)
            TT(orow, prodst[:, 0:D], xrow, AL.add)
            em.wait_now(Sy, SV)
            dma(y_out[row0:row0 + 128, :], orow)
            em.wait_now(V, SD)
            em.wait_now(Tn, SD)

    if debug:
        dbg_specs = [("d_scal", SCAL[:, 0:32], [128, 32], None),
                     ("d_amax", AMAX[:, 0:8], [128, 8], None)]
        for nm, ap_, shp, _ in dbg_specs:
            od = dt(nm, shp, F32, kind="ExternalOutput").ap()
            dma(od, ap_)
        for nm, sp, shp in [("d_xn", sp_xn, [D, TC]), ("d_s", sp_s, [D, TOWN]),
                            ("d_upr", sp_upr, [FF, TOWN]), ("d_gater", sp_gater, [FF, TOWN]),
                            ("d_gs", sp_gs, [FF, TOWN]), ("d_o", sp_o, [FF, TOWN]),
                            ("d_k", sp_k, [D, TWARM + TOWN]), ("d_r", sp_r, [D, TOWN]),
                            ("d_g", sp_g, [D, TOWN])]:
            od = dt(nm, shp, F32, kind="ExternalOutput").ap()
            dma(od, sp)
    em.wait_now(Sy, SD)
    barrier()
    return nc, em


def finalize(nc, em):
    ctxs = []
    sems = {}
    esems = {}
    for name in em.counts:
        h = nc.semaphore(name)
        sems[name] = h.__enter__()
        ctxs.append(h)
    for e in ENGINES:
        h = nc.semaphore("ep_" + e)
        esems[e] = h.__enter__()
        ctxs.append(h)
    blk = nc.Block()
    block = blk.__enter__()
    em.replay(block, sems, esems, SEM_OWNER)
    blk.__exit__(None, None, None)
    for h in reversed(ctxs):
        h.__exit__(None, None, None)
    return nc


def make_core_inputs(x_full, params, core, n_cores, cfg: Cfg):
    TOWN, TWARM = cfg.TOWN, cfg.TWARM
    t0 = core * TOWN
    xo = np.zeros((cfg.NOP, D), np.float32)
    xo[1:1 + TOWN] = x_full[t0:t0 + TOWN]
    xo[0] = x_full[1] if core == 0 else x_full[t0 - 1]
    xw = np.zeros((cfg.NWP, D), np.float32)
    if core > 0 and TWARM > 0:
        xw[1:1 + TWARM] = x_full[t0 - TWARM:t0]
        xw[0] = x_full[t0 - TWARM - 1] if t0 - TWARM - 1 >= 0 else 0.0
    vc = np.stack([params[k].reshape(NFT, 128).T for k in
                   ("td", "tmk", "tmv", "tmr", "tmg", "g_ln", "b_ln")], 1)
    vc = vc.reshape(128, 7 * NFT, order="F") if False else np.concatenate(
        [params[k].reshape(NFT, 128).T for k in
         ("td", "tmk", "tmv", "tmr", "tmg", "g_ln", "b_ln")], axis=1)
    return {
        "x_own": xo, "x_warm": xw,
        "WrT": np.ascontiguousarray(params["Wr"].T),
        "WkT": np.ascontiguousarray(params["Wk"].T),
        "WvT": np.ascontiguousarray(params["Wv"].T),
        "WgT": np.ascontiguousarray(params["Wg"].T),
        "WoT": np.ascontiguousarray(params["Wo"].T),
        "Wup": params["w_up"], "Wgate": params["w_gate"], "Wdown": params["w_down"],
        "vec_cols": np.ascontiguousarray(vc, np.float32),
        "td_row": params["td"].reshape(1, D),
        "ident_c": np.eye(128, dtype=np.float32),
        "mask64_c": np.triu(np.ones((64, 64), np.float32)),
        "iota64_c": np.arange(64, dtype=np.float32).reshape(1, 64),
        "ones_row_c": np.ones((1, 128), np.float32),
        "meanw_c": np.full((128, 1), 1.0 / D, np.float32),
    }


from concourse.bass_utils import run_bass_kernel_spmd

_N_CORES = 8
_CFG = Cfg(TOWN=2048, TWARM=1024, TB=512, n_cores=_N_CORES)
_NC_CACHE = {}


def _get_program():
    if "nc" not in _NC_CACHE:
        nc, em = build(_CFG)
        finalize(nc, em)
        _NC_CACHE["nc"] = nc
    return _NC_CACHE["nc"]


def kernel(L0_attn_x, L0_attn_time_decay, L0_attn_time_mix_k, L0_attn_time_mix_v,
           L0_attn_time_mix_r, L0_attn_time_mix_g, L0_attn_W_r, L0_attn_W_k,
           L0_attn_W_v, L0_attn_W_g, L0_attn_W_o, L0_attn_ln_gamma,
           L0_attn_ln_beta, L0_ffn_w_gate, L0_ffn_w_up, L0_ffn_w_down):
    x_full = np.ascontiguousarray(np.asarray(L0_attn_x, np.float32).reshape(-1, D))
    params = {
        "td": np.asarray(L0_attn_time_decay, np.float32),
        "tmk": np.asarray(L0_attn_time_mix_k, np.float32),
        "tmv": np.asarray(L0_attn_time_mix_v, np.float32),
        "tmr": np.asarray(L0_attn_time_mix_r, np.float32),
        "tmg": np.asarray(L0_attn_time_mix_g, np.float32),
        "Wr": np.asarray(L0_attn_W_r, np.float32),
        "Wk": np.asarray(L0_attn_W_k, np.float32),
        "Wv": np.asarray(L0_attn_W_v, np.float32),
        "Wg": np.asarray(L0_attn_W_g, np.float32),
        "Wo": np.asarray(L0_attn_W_o, np.float32),
        "g_ln": np.asarray(L0_attn_ln_gamma, np.float32),
        "b_ln": np.asarray(L0_attn_ln_beta, np.float32),
        "w_gate": np.asarray(L0_ffn_w_gate, np.float32),
        "w_up": np.asarray(L0_ffn_w_up, np.float32),
        "w_down": np.asarray(L0_ffn_w_down, np.float32),
    }
    nc = _get_program()
    in_maps = [make_core_inputs(x_full, params, c, _N_CORES, _CFG)
               for c in range(_N_CORES)]
    res = run_bass_kernel_spmd(nc, in_maps, core_ids=list(range(_N_CORES)))
    out = np.concatenate([np.asarray(r["y"], np.float32) for r in res.results], 0)
    return out.reshape(1, _N_CORES * _CFG.TOWN, D)


# version 15 (final)
# speedup vs baseline: 1.6828x; 1.0357x over previous
import numpy as np
import concourse.bass as bass
import concourse.mybir as mybir
from concourse.alu_op_type import AluOpType as AL

F32 = mybir.dt.float32
I16 = mybir.dt.int16
U16 = mybir.dt.uint16
AF = mybir.ActivationFunctionType
AX = mybir.AxisListType

ENGINES = ("sync", "tensor", "vector", "scalar", "gpsimd")
SEM_OWNER = {"sd0": "sync", "sd1": "sync", "sd2": "sync", "sd3": "sync",
             "st": "tensor", "sv": "vector", "ss": "scalar", "sg": "gpsimd"}
DSEMS = ("sd0", "sd1", "sd2", "sd3")

D = 1024
NFT = 8
H = 16
DH = 64
FF = 3072
NIT = FF // 128
L = 64
QMAX = 32767.0
EPS = 1e-5


class Cfg:
    def __init__(self, TOWN=2048, TWARM=1024, TB=512, n_cores=8):
        assert TB == 512, "buffer layout assumes TB == 512"
        assert TOWN % TB == 0 and (TWARM % TB == 0 or TWARM == 0)
        self.TOWN, self.TWARM, self.TB, self.n_cores = TOWN, TWARM, TB, n_cores
        self.NOP = TOWN + 128
        self.NWP = TWARM + 128 if TWARM > 0 else 128
        self.TC = self.NWP + self.NOP
        self.n_own_tiles = self.NOP // 128
        self.n_warm_tiles = self.NWP // 128
        self.NTB = TOWN // 512 if TOWN % 512 == 0 else None
        assert TOWN % 512 == 0 or TOWN < 512
        self.FTB = 512 if TOWN >= 512 else TOWN
        self.n_ftb = TOWN // self.FTB


class Emitter:

    def __init__(self, nc):
        self.nc = nc
        self.items = {e: [] for e in ENGINES}
        self.counts = {}
        self.touched = {e: [] for e in ENGINES}
        self.unfenced = {e: 0 for e in ENGINES}
        self.nbar = 0
        self.maxcount = {}

    def new_sem(self, name):
        self.counts[name] = 0
        self.maxcount[name] = 0
        return name

    @staticmethod
    def _rng(ap):
        try:
            t = ap.tensor.name
            lo = int(ap.offset)
            sz = 1
            for st, ct in ap.ap:
                sz += abs(int(st)) * (int(ct) - 1)
            return (t, lo, lo + sz)
        except Exception:
            return (str(ap), 0, 1 << 60)

    def _hazard(self, eng, reads, writes):
        tl = self.touched[eng]
        def ov(a, b):
            return a[0] == b[0] and a[1] < b[2] and b[1] < a[2]
        for ap in reads:
            r = self._rng(ap)
            for (k, lo, hi, w) in tl:
                if w and ov(r, (k, lo, hi)):
                    return True
        for ap in writes:
            r = self._rng(ap)
            for (k, lo, hi, w) in tl:
                if ov(r, (k, lo, hi)):
                    return True
        return False

    def _note(self, eng, reads, writes):
        tl = self.touched[eng]
        for ap in reads:
            k, lo, hi = self._rng(ap)
            tl.append((k, lo, hi, False))
        for ap in writes:
            k, lo, hi = self._rng(ap)
            tl.append((k, lo, hi, True))

    def op(self, eng, fn, reads=(), writes=(), inc=None, inc_by=1):
        reads, writes = list(reads), list(writes)
        if eng != "sync" and self._hazard(eng, reads, writes):
            self.items[eng].append(("drain",))
            self.touched[eng] = []
        self._note(eng, reads, writes)
        if inc is not None:
            self.counts[inc] += inc_by
            self.maxcount[inc] = max(self.maxcount[inc], self.counts[inc])
            assert self.counts[inc] <= 240, (
                f"sem {inc} exceeds 240 within an epoch ({self.counts[inc]}); "
                f"insert a barrier (epoch {self.nbar})")
            self.items[eng].append(("op", fn, inc, inc_by))
        else:
            self.items[eng].append(("op", fn, None, 0))
            self.unfenced[eng] += 1

    def fence(self, eng, sem):
        if self.unfenced[eng] == 0:
            return
        self.unfenced[eng] = 0
        self.counts[sem] += 1
        self.maxcount[sem] = max(self.maxcount[sem], self.counts[sem])
        assert self.counts[sem] <= 240, f"sem {sem} epoch overflow"
        self.items[eng].append(("fenceinc", sem))

    def wait(self, eng, sem, val):
        if val > 0:
            self.items[eng].append(("wait", sem, val))

    def wait_now(self, eng, sem):
        if sem == "sd":
            for s in DSEMS:
                self.wait(eng, s, self.counts[s])
            return
        own = SEM_OWNER[sem]
        if self.unfenced[own] > 0:
            self.fence(own, sem)
        self.wait(eng, sem, self.counts[sem])

    def barrier(self):
        tok = dict(self.counts)
        k = self.nbar
        self.nbar += 1
        assert self.nbar <= 125, "too many barriers for 8-bit epoch sems"
        for e in ENGINES:
            self.items[e].append(("barrier", k, tok))
            self.touched[e] = []
            self.unfenced[e] = 0
        for s in self.counts:
            self.counts[s] = 0

    def replay(self, block, sems, esems, owner):
        em = self

        def mk(eng_name):
            def body(proxy):
                last = [None]
                last_inc = [False]

                def emit_fence(s):
                    if last[0] is not None and not last_inc[0]:
                        last[0].then_inc(sems[s], 1)
                        last_inc[0] = True
                    else:
                        proxy.drain()
                        proxy.nop().then_inc(sems[s], 1)
                        last[0] = None
                        last_inc[0] = False

                for it in em.items[eng_name]:
                    kind = it[0]
                    if kind == "drain":
                        proxy.drain()
                        last[0] = None
                        last_inc[0] = False
                    elif kind == "wait":
                        proxy.wait_ge(sems[it[1]], it[2])
                        last[0] = None
                        last_inc[0] = False
                    elif kind == "fenceinc":
                        emit_fence(it[1])
                    elif kind == "ewait":
                        proxy.wait_ge(esems[eng_name], it[1])
                        last[0] = None
                        last_inc[0] = False
                    elif kind == "barrier":
                        k, tok = it[1], it[2]
                        for s, v in tok.items():
                            if v > 0:
                                proxy.wait_ge(sems[s], v)
                        proxy.nop().then_inc(esems[eng_name], 1)
                        for f in ENGINES:
                            proxy.wait_ge(esems[f], 2 * k + 1)
                        for s, own in owner.items():
                            if own == eng_name:
                                proxy.sem_clear(sems[s])
                        proxy.drain()
                        proxy.nop().then_inc(esems[eng_name], 1)
                        for f in ENGINES:
                            proxy.wait_ge(esems[f], 2 * k + 2)
                        last[0] = None
                        last_inc[0] = False
                    else:
                        _, fn, b, c = it
                        instr = fn(proxy)
                        last[0] = instr
                        last_inc[0] = False
                        if b is not None and instr is not None:
                            instr.then_inc(sems[b], c)
            return body

        for e in ENGINES:
            getattr(block, e)(mk(e))


def build(cfg: Cfg, debug=False):
    nc = bass.Bass("TRN2", target_bir_lowering=False)
    TOWN, TWARM, TB = cfg.TOWN, cfg.TWARM, cfg.TB
    NOP, NWP, TC = cfg.NOP, cfg.NWP, cfg.TC
    FTB, NFTB = cfg.FTB, cfg.n_ftb

    dt = nc.dram_tensor
    x_own = dt("x_own", [NOP, D], F32, kind="ExternalInput").ap()
    x_warm = dt("x_warm", [NWP, D], F32, kind="ExternalInput").ap()
    WT = {n: dt(n, [D, D], F32, kind="ExternalInput").ap()
          for n in ("WrT", "WkT", "WvT", "WgT", "WoT")}
    Wup = dt("Wup", [FF, D], F32, kind="ExternalInput").ap()
    Wgate = dt("Wgate", [FF, D], F32, kind="ExternalInput").ap()
    Wdown = dt("Wdown", [D, FF], F32, kind="ExternalInput").ap()
    vec_cols = dt("vec_cols", [128, 7 * NFT], F32, kind="ExternalInput").ap()
    td_row = dt("td_row", [1, D], F32, kind="ExternalInput").ap()
    ident_i = dt("ident_c", [128, 128], F32, kind="ExternalInput").ap()
    mask64_i = dt("mask64_c", [64, 64], F32, kind="ExternalInput").ap()
    iota64_i = dt("iota64_c", [1, 64], F32, kind="ExternalInput").ap()
    ones_row_i = dt("ones_row_c", [1, 128], F32, kind="ExternalInput").ap()
    meanw_i = dt("meanw_c", [128, 1], F32, kind="ExternalInput").ap()
    y_out = dt("y", [TOWN, D], F32, kind="ExternalOutput").ap()

    din = lambda n, s, dtp=F32: dt(n, s, dtp, kind="Internal").ap()
    sp_xn = din("sp_xn", [D, TC])
    sp_k = din("sp_k", [D, TWARM + TOWN])
    sp_v = din("sp_v", [D, TWARM + TOWN])
    sp_r = din("sp_r", [D, TOWN])
    sp_g = din("sp_g", [D, TOWN])
    sp_s = din("sp_s", [D, TOWN])
    sp_upr = din("sp_upr", [FF, TOWN])
    sp_gater = din("sp_gater", [FF, TOWN])
    sp_upq = din("sp_upq", [FF, TOWN], I16)
    sp_gs = din("sp_gs", [FF, TOWN])
    sp_o = din("sp_o", [FF, TOWN])
    sp_wd = din("sp_wd", [FF, D])
    cc_in = [din(f"cc_in{i}", [1, 2]) for i in range(4)]
    cc_out = [din(f"cc_out{i}", [1, 2]) for i in range(4)]

    def alloc(name, cols, dtp=F32, parts=128):
        return nc.alloc_sbuf_tensor(name, [parts, cols], dtp).ap()

    ident = alloc("c_ident", 128)
    mask64 = alloc("c_mask", 64, parts=64)
    ones_row = alloc("c_ones", 128, parts=1)
    meanw = alloc("c_meanw", 1)
    vecs = alloc("c_vecs", 7 * NFT)

    e_c = alloc("c_e", 2 * NFT)
    iota_bc = alloc("c_iotabc", 64)
    SE = alloc("c_SE", 512)
    M_km = alloc("c_Mkm", 512)
    M_out = alloc("c_Mout", 512)
    M_stL = alloc("c_MstL", 512)
    M_k2L = alloc("c_Mk2L", D)
    SCAL = alloc("c_scal", 32)
    cEPS = alloc("c_eps", 1)
    AMAX = alloc("c_amax", 8)
    rowtmp = alloc("c_rowtmp", 1024, parts=1)
    St = alloc("St", 512)
    stats = alloc("p_stats", 16)

    B_W = alloc("b_w", NFT * D)
    B0 = alloc("b_0", NFT * (TB + 1))
    B1 = alloc("b_1", NFT * TB)
    B2 = alloc("b_2", NFT * TB)
    B3 = alloc("b_3", NFT * TB)
    B4 = alloc("b_4", NFT * TB)
    B5 = alloc("b_5", NFT * TB)
    B6 = alloc("b_6", NFT * TB)
    B7 = alloc("b_7", NFT * TB)
    B8 = alloc("b_8", 4 * D)
    B9 = alloc("b_9", 2 * D)

    pb = [nc.alloc_psum_tensor(f"pb{i}", [128, 512], F32).ap() for i in range(8)]

    em = Emitter(nc)
    for _s in DSEMS:
        em.new_sem(_s)
    ST = em.new_sem("st")
    SV = em.new_sem("sv")
    SS = em.new_sem("ss")
    SG = em.new_sem("sg")
    SD = "sd"

    V, Tn, Sc, Gp, Sy = "vector", "tensor", "scalar", "gpsimd", "sync"
    esem = {V: SV, Tn: ST, Sc: SS, Gp: SG, Sy: SD}

    _dma_i = [0]

    def dma(out, in_):
        s = DSEMS[_dma_i[0] % 4]
        _dma_i[0] += 1
        em.op(Sy, lambda E, o=out, i=in_: E.dma_start(out=o, in_=i), inc=s, inc_by=16)
        em.unfenced["sync"] = 0

    def barrier():
        em.barrier()

    def TT(out, a, b, op, eng=V):
        em.op(eng, lambda E, o=out, x=a, y=b: E.tensor_tensor(o, x, y, op=op),
              [a, b], [out])

    def TS(out, a, s1, op0, s2=None, op1=None, eng=V):
        rd = [a] + [s for s in (s1, s2) if isinstance(s, bass.AP)]
        if s2 is None:
            em.op(eng, lambda E, o=out, x=a: E.tensor_scalar(o, x, s1, None, op0=op0),
                  rd, [out])
        else:
            em.op(eng, lambda E, o=out, x=a: E.tensor_scalar(o, x, s1, s2, op0=op0, op1=op1),
                  rd, [out])

    def STT(out, in0, scal, in1, op0, op1, eng=V):
        rd = [in0, in1] + ([scal] if isinstance(scal, bass.AP) else [])
        em.op(eng, lambda E, o=out, a=in0, b=in1: E.scalar_tensor_tensor(
            o, a, scal, b, op0=op0, op1=op1), rd, [out])

    def RED(out, a, op, absval=False, eng=V):
        em.op(eng, lambda E, o=out, x=a: E.tensor_reduce(
            o, x, axis=AX.X, op=op, apply_absolute_value=absval),
            [a], [out])

    def CP(out, a, eng=V):
        em.op(eng, lambda E, o=out, x=a: E.tensor_copy(o, x), [a], [out])

    def SCP(out, a):
        em.op(Sc, lambda E, o=out, x=a: E.copy(o, x), [a], [out])

    def ACT(out, a, func, bias=0.0, scale=1.0):
        rd = [a] + ([bias] if isinstance(bias, bass.AP) else [])
        em.op(Sc, lambda E, o=out, x=a: E.activation(o, x, func, bias=bias, scale=scale),
              rd, [out])

    def RECIP(out, a, eng=V):
        em.op(eng, lambda E, o=out, x=a: E.reciprocal(o, x), [a], [out])

    def MM(out, lhsT, rhs, start, stop):
        em.op(Tn, lambda E, o=out, l=lhsT, r=rhs: E.matmul(o, l, r, start=start, stop=stop),
              [lhsT, rhs], [out])

    def TR(out, in_):
        em.op(Tn, lambda E, o=out, i=in_: E.transpose(o, i, ident[:, :]),
              [in_, ident], [out])

    def MEMSET(buf, val=0.0, eng=V):
        em.op(eng, lambda E, o=buf: E.memset(o, val), [], [buf])

    ft_ = lambda buf, j, w: buf[:, j * w:(j + 1) * w]

    dma(ident[:, :], ident_i)
    dma(mask64[0:64, :], mask64_i)
    dma(rowtmp[0:1, 0:64], iota64_i)
    dma(ones_row[0:1, :], ones_row_i)
    dma(meanw[:, 0:1], meanw_i)
    dma(vecs[:, :], vec_cols)
    dma(B8[0:1, 0:D], td_row)
    for e in (V, Sc, Tn):
        em.wait_now(e, SD)

    c_td = vecs[:, 0:NFT]
    c_tmk = vecs[:, NFT:2 * NFT]
    c_tmv = vecs[:, 2 * NFT:3 * NFT]
    c_tmr = vecs[:, 3 * NFT:4 * NFT]
    c_tmg = vecs[:, 4 * NFT:5 * NFT]
    c_gln = vecs[:, 5 * NFT:6 * NFT]
    c_bln = vecs[:, 6 * NFT:7 * NFT]
    tdrow = B8[0:1, 0:D]
    erow = B8[0:1, D:2 * D]
    wLrow = B8[0:1, 2 * D:3 * D]

    MEMSET(cEPS[:, 0:1], EPS)
    ACT(e_c[:, 0:NFT], c_td, AF.Exp)
    ACT(e_c[:, NFT:2 * NFT], e_c[:, 0:NFT], AF.Exp, scale=float(-L))
    ACT(erow, tdrow, AF.Exp)
    ACT(wLrow, erow, AF.Exp, scale=float(-L))
    em.wait_now(Tn, SS)
    MM(pb[0][:, 0:64], ones_row[0:1, :], rowtmp[0:1, 0:64], True, True)
    em.wait_now(V, ST)
    CP(iota_bc[:, :], pb[0][:, 0:64])
    em.wait_now(V, SS)
    for j in range(NFT):
        TS(ft_(SE, j, 64), iota_bc[:, :], e_c[:, j:j + 1], AL.mult)
        TS(ft_(M_stL, j, 64), iota_bc[:, :], 0.0, AL.mult, e_c[:, NFT + j:NFT + j + 1], AL.add)
    em.wait_now(Sc, SV)
    ACT(M_km[:, :], SE[:, :], AF.Exp)
    ACT(M_out[:, :], SE[:, :], AF.Exp, scale=-1.0)
    em.wait_now(Tn, SS)
    MM(pb[1][:, 0:512], ones_row[0:1, :], wLrow[0:1, 0:512], True, True)
    MM(pb[2][:, 0:512], ones_row[0:1, :], wLrow[0:1, 512:1024], True, True)
    em.wait_now(V, ST)
    CP(M_k2L[:, 0:512], pb[1][:, 0:512])
    CP(M_k2L[:, 512:1024], pb[2][:, 0:512])
    MEMSET(St[:, :], 0.0)
    MEMSET(AMAX[:, :], 0.0)
    barrier()

    xt = B_W[:, 0:D]
    xm = B_W[:, D:2 * D]
    sqt = B_W[:, 2 * D:3 * D]
    xnt = B_W[:, 3 * D:4 * D]
    stage = B9[:, 0:D]

    def ln_tiles(src_ap, n_tiles, col0):
        for ti in range(n_tiles):
            dma(xt, src_ap[ti * 128:(ti + 1) * 128, :])
            em.wait_now(V, SD)
            RED(stats[:, 0:1], xt, AL.add)
            TS(stats[:, 1:2], stats[:, 0:1], 1.0 / D, AL.mult)
            TS(xm, xt, stats[:, 1:2], AL.subtract)
            em.wait_now(Sc, SV)
            ACT(sqt, xm, AF.Square)
            em.wait_now(V, SS)
            RED(stats[:, 2:3], sqt, AL.add)
            em.wait_now(Sc, SV)
            ACT(stats[:, 3:4], stats[:, 2:3], AF.Sqrt, bias=cEPS[:, 0:1], scale=1.0 / D)
            em.wait_now(V, SS)
            RECIP(stats[:, 4:5], stats[:, 3:4])
            TS(xnt, xm, stats[:, 4:5], AL.mult)
            RED(stats[:, 5:6], xnt, AL.max, absval=True)
            TT(AMAX[:, 0:1], AMAX[:, 0:1], stats[:, 5:6], AL.max)
            em.wait_now(Tn, SV)
            for j in range(NFT):
                TR(pb[j // 4][:, (j % 4) * 128:(j % 4 + 1) * 128], xnt[:, j * 128:(j + 1) * 128])
            em.wait_now(V, ST)
            CP(stage[:, 0:512], pb[0][:, 0:512])
            CP(stage[:, 512:1024], pb[1][:, 0:512])
            em.wait_now(Sy, SV)
            bc = (col0 + ti) * 128
            dma(sp_xn[:, bc:bc + 128].rearrange("(j f) c -> f j c", f=128),
                stage[:, :].rearrange("p (j c) -> p j c", c=128))
            em.wait_now(V, SD)
            em.wait_now(Tn, SD)
            if ti % 12 == 11:
                barrier()

    ln_tiles(x_warm, cfg.n_warm_tiles, 0)
    ln_tiles(x_own, cfg.n_own_tiles, cfg.n_warm_tiles)
    barrier()

    def cross_part_max(src_col_ap, dst_row_off):
        TR(pb[0][0:1, 0:128], src_col_ap)
        em.wait_now(V, ST)
        CP(rowtmp[0:1, 64:192], pb[0][0:1, 0:128])
        RED(rowtmp[0:1, dst_row_off:dst_row_off + 1], rowtmp[0:1, 64:192], AL.max)
        em.wait_now(Tn, SV)

    def allreduce_scales(idx, src_off, nvals, sc_cols):
        em.wait_now(Sy, SV)
        dma(cc_in[idx][0:1, 0:nvals], rowtmp[0:1, src_off:src_off + nvals])
        em.wait_now(Gp, SD)
        em.op(Gp, lambda E: E.collective_compute(
            "AllReduce", AL.max,
            replica_groups=[list(range(cfg.n_cores))],
            ins=[cc_in[idx][0:1, 0:nvals]],
            outs=[cc_out[idx][0:1, 0:nvals]]), inc=SG)
        em.wait_now(Sy, SG)
        dma(rowtmp[0:1, 200:200 + nvals], cc_out[idx][0:1, 0:nvals])
        em.wait_now(V, SD)
        for c in range(nvals):
            TS(rowtmp[0:1, 210 + c:211 + c], rowtmp[0:1, 200 + c:201 + c],
               1.0 / QMAX, AL.mult, 1e-12, AL.max)
        em.wait_now(Tn, SV)
        for c in range(nvals):
            MM(pb[0][:, c:c + 1], ones_row[0:1, :], rowtmp[0:1, 210 + c:211 + c], True, True)
        em.wait_now(V, ST)
        for c in range(nvals):
            col = sc_cols[c]
            CP(SCAL[:, col:col + 1], pb[0][:, c:c + 1])
            RECIP(SCAL[:, col + 1:col + 2], SCAL[:, col:col + 1])
            TS(SCAL[:, col + 2:col + 3], SCAL[:, col:col + 1], -1.0, AL.mult)

    cross_part_max(AMAX[:, 0:1], 0)
    allreduce_scales(0, 0, 1, [0])
    barrier()

    xn_g = B0
    xmix = B1
    projg = B2
    own0 = NWP + 1
    warm0 = 1

    def load_weightT(wt_ap):
        dma(B_W[:, 0:NFT * D].rearrange("p (j f) -> p j f", f=D),
            wt_ap.rearrange("(j r) f -> r j f", r=128))

    def proj_group(src_col0, n, mix_c, dst, dst0, sig):
        dma(xn_g[:, 0:NFT * (TB + 1)].rearrange("p (j t) -> p j t", t=TB + 1)[:, :, 0:n + 1],
            sp_xn[:, src_col0 - 1:src_col0 + n].rearrange("(j f) c -> f j c", f=128))
        em.wait_now(V, SD)
        for j in range(NFT):
            xg = ft_(xn_g, j, TB + 1)
            mx = ft_(xmix, j, TB)[:, 0:n]
            TT(mx, xg[:, 0:n], xg[:, 1:n + 1], AL.subtract)
            STT(mx, mx, mix_c[:, j:j + 1], xg[:, 1:n + 1], AL.mult, AL.add)
        em.wait_now(Tn, SV)
        for ot in range(NFT):
            bank = pb[ot % 2]
            if ot >= 2:
                em.wait_now(Tn, SS if sig else SV)
            for j in range(NFT):
                MM(bank[:, 0:n], ft_(B_W, j, D)[:, ot * 128:(ot + 1) * 128],
                   ft_(xmix, j, TB)[:, 0:n], j == 0, j == NFT - 1)
            po = ft_(projg, ot, TB)[:, 0:n]
            if sig:
                em.wait_now(Sc, ST)
                ACT(po, bank[:, 0:n], AF.Sigmoid)
            else:
                em.wait_now(V, ST)
                CP(po, bank[:, 0:n])
        em.wait_now(Sy, SS if sig else SV)
        dma(dst[:, dst0:dst0 + n].rearrange("(j f) c -> f j c", f=128),
            projg[:, 0:NFT * TB].rearrange("p (j t) -> p j t", t=TB)[:, :, 0:n])
        em.wait_now(Tn, SD)
        em.wait_now(V, SD) if not sig else em.wait_now(Sc, SD)

    for (wn, spill, mix_c, sig) in (("WkT", sp_k, c_tmk, False), ("WvT", sp_v, c_tmv, False),
                                    ("WrT", sp_r, c_tmr, False), ("WgT", sp_g, c_tmg, True)):
        load_weightT(WT[wn])
        em.wait_now(Tn, SD)
        kv = wn in ("WkT", "WvT")
        if kv and TWARM > 0:
            for gc in range(0, TWARM, TB):
                proj_group(warm0 + gc, min(TB, TWARM - gc), mix_c, spill, gc, sig)
        for gc in range(0, TOWN, TB):
            proj_group(own0 + gc, min(TB, TOWN - gc), mix_c, spill,
                       (TWARM + gc) if kv else gc, sig)
        barrier()

    load_weightT(WT["WoT"])
    barrier()

    k_g, v_g, r_g, gg_g = B3, B4, B5, B6
    ktm_g = B8
    vtm_sub = B0[:, 0:1024]
    k2tm_sub = B0[:, 1024:2048]
    kt_sub = B0[:, 2048:3072]
    scr512 = B0[:, 3072:3584]
    StEntry = B0[:, 3584:4096]
    Atl_g = B1
    Ug = B7
    attn_g = B2
    y_g = B7
    ym_g = B0
    s_g = B1

    def mixer_ph1(sg, own):
        t0g = sg * 128
        ktm = ktm_g[:, sg * 1024:(sg + 1) * 1024]
        for j in range(NFT):
            kv_ = ft_(k_g, j, TB)[:, t0g:t0g + 128].rearrange("p (c s) -> p c s", s=L)
            mv_ = ft_(M_km, j, L).unsqueeze(1).broadcast_to([128, 2, L])
            ov_ = kt_sub[:, j * 128:(j + 1) * 128].rearrange("p (c s) -> p c s", s=L)
            em.op(V, lambda E, o=ov_, a=kv_, b=mv_: E.tensor_tensor(o, a, b, op=AL.mult),
                  [ft_(k_g, j, TB)], [kt_sub])
        em.wait_now(Tn, SV)
        for j in range(NFT):
            TR(pb[2 + j // 4][:, (j % 4) * 128:(j % 4 + 1) * 128],
               kt_sub[:, j * 128:(j + 1) * 128])
        em.wait_now(V, ST)
        CP(ktm[:, 0:512], pb[2][:, 0:512])
        CP(ktm[:, 512:1024], pb[3][:, 0:512])
        em.wait_now(Tn, SV)
        for j in range(NFT):
            TR(pb[2 + j // 4][:, (j % 4) * 128:(j % 4 + 1) * 128],
               ft_(v_g, j, TB)[:, t0g:t0g + 128])
        em.wait_now(V, ST)
        CP(vtm_sub[:, 0:512], pb[2][:, 0:512])
        CP(vtm_sub[:, 512:1024], pb[3][:, 0:512])
        TT(k2tm_sub[:, :], ktm[:, :], M_k2L[:, :], AL.mult)
        em.wait_now(Tn, SV)
        for ci in range(2):
            cp_ = ci * 64
            c = sg * 2 + ci
            for j in range(NFT):
                for hp in range(2):
                    h = 2 * j + hp
                    MM(pb[7][hp * 64:hp * 64 + 64, j * 64:j * 64 + 64],
                       vtm_sub[cp_:cp_ + 64, h * 64:h * 64 + 64],
                       k2tm_sub[cp_:cp_ + 64, h * 64:h * 64 + 64],
                       True, True)
            em.wait_now(Sc, ST)
            SCP(Ug[:, c * 512:(c + 1) * 512], pb[7][:, :])
            em.wait_now(Tn, SS)
        if own:
            for ci in range(2):
                cp_ = ci * 64
                tok0 = t0g + ci * 64
                for j in range(NFT):
                    for hp in range(2):
                        MM(pb[4 + j // 4][cp_:cp_ + 64,
                                          (j % 4) * 128 + hp * 64:(j % 4) * 128 + hp * 64 + 64],
                           ft_(v_g, j, TB)[hp * 64:hp * 64 + 64, tok0:tok0 + 64],
                           ft_(r_g, j, TB)[hp * 64:hp * 64 + 64, tok0:tok0 + 64],
                           True, True)
            em.wait_now(V, ST)
            for b in range(2):
                pv = pb[4 + b][:, 0:512].rearrange("p (k s) -> p k s", s=64)
                mk = mask64[0:64, :].unsqueeze(0).unsqueeze(2) if False else mask64[0:64, :]
                mkb = mk.unsqueeze(1).broadcast_to([64, 8, 64])
                ov = Atl_g[:, sg * 1024 + b * 512:sg * 1024 + (b + 1) * 512] \
                    .rearrange("p (k s) -> p k s", s=64)
                pv2 = pb[4 + b][0:64, 0:512].rearrange("p (k s) -> p k s", s=64)
                ov2 = Atl_g[0:64, sg * 1024 + b * 512:sg * 1024 + (b + 1) * 512] \
                    .rearrange("p (k s) -> p k s", s=64)
                pv3 = pb[4 + b][64:128, 0:512].rearrange("p (k s) -> p k s", s=64)
                ov3 = Atl_g[64:128, sg * 1024 + b * 512:sg * 1024 + (b + 1) * 512] \
                    .rearrange("p (k s) -> p k s", s=64)
                em.op(V, lambda E, o=ov2, a=pv2, m=mkb: E.tensor_tensor(o, a, m, op=AL.mult),
                      [pb[4 + b]], [Atl_g])
                em.op(V, lambda E, o=ov3, a=pv3, m=mkb: E.tensor_tensor(o, a, m, op=AL.mult),
                      [pb[4 + b]], [Atl_g])
            em.wait_now(Tn, SV)

    def mixer_scan(nch):
        em.wait_now(V, SS)
        CP(StEntry, St[:, :])
        TT(scr512, St[:, :], M_stL[:, :], AL.mult)
        TT(Ug[:, 0:512], scr512, Ug[:, 0:512], AL.add)
        for c in range(1, nch):
            TT(scr512, Ug[:, (c - 1) * 512:c * 512], M_stL[:, :], AL.mult)
            TT(Ug[:, c * 512:(c + 1) * 512], scr512, Ug[:, c * 512:(c + 1) * 512], AL.add)
        CP(St[:, :], Ug[:, (nch - 1) * 512:nch * 512])

    def mixer_ph3(nch):
        em.wait_now(Tn, SV)
        for c in range(nch):
            bank = pb[c % 2]
            if c >= 2:
                em.wait_now(Tn, SS)
            sg = c // 2
            cp_ = (c % 2) * 64
            st_src = StEntry if c == 0 else Ug[:, (c - 1) * 512:c * 512]
            for j in range(NFT):
                for hp in range(2):
                    h = 2 * j + hp
                    MM(bank[hp * 64:hp * 64 + 64, j * 64:j * 64 + 64],
                       ktm_g[:, sg * 1024:(sg + 1) * 1024][cp_:cp_ + 64, h * 64:h * 64 + 64],
                       Atl_g[cp_:cp_ + 64, sg * 1024 + j * 128 + hp * 64:
                             sg * 1024 + j * 128 + hp * 64 + 64],
                       True, False)
                    MM(bank[hp * 64:hp * 64 + 64, j * 64:j * 64 + 64],
                       st_src[hp * 64:hp * 64 + 64, j * 64:j * 64 + 64],
                       ft_(r_g, j, TB)[hp * 64:hp * 64 + 64, c * 64:c * 64 + 64],
                       False, True)
            em.wait_now(Sc, ST)
            pv = bank[:, 0:512].rearrange("p (j s) -> p j s", s=64)
            ov = attn_g[:, 0:NFT * TB].rearrange("p (j t) -> p j t", t=TB)[:, :, c * 64:c * 64 + 64]
            em.op(Sc, lambda E, o=ov, a=pv: E.copy(o, a), [bank], [attn_g])

    def mixer_group(kvc0, n, own, ownc0=0):
        dma(k_g[:, 0:NFT * TB].rearrange("p (j t) -> p j t", t=TB)[:, :, 0:n],
            sp_k[:, kvc0:kvc0 + n].rearrange("(j f) c -> f j c", f=128))
        dma(v_g[:, 0:NFT * TB].rearrange("p (j t) -> p j t", t=TB)[:, :, 0:n],
            sp_v[:, kvc0:kvc0 + n].rearrange("(j f) c -> f j c", f=128))
        if own:
            dma(r_g[:, 0:NFT * TB].rearrange("p (j t) -> p j t", t=TB)[:, :, 0:n],
                sp_r[:, ownc0:ownc0 + n].rearrange("(j f) c -> f j c", f=128))
        em.wait_now(V, SD)
        em.wait_now(Tn, SD)
        for sg in range(n // 128):
            mixer_ph1(sg, own)
        mixer_scan(n // L)
        if own:
            mixer_ph3(n // L)

    if TWARM > 0:
        for gi, gc in enumerate(range(0, TWARM, TB)):
            mixer_group(gc, min(TB, TWARM - gc), own=False)
            if gi % 2 == 1:
                barrier()
        barrier()

    for gc in range(0, TOWN, TB):
        n = min(TB, TOWN - gc)
        mixer_group(TWARM + gc, n, own=True, ownc0=gc)
        dma(gg_g[:, 0:NFT * TB].rearrange("p (j t) -> p j t", t=TB)[:, :, 0:n],
            sp_g[:, gc:gc + n].rearrange("(j f) c -> f j c", f=128))
        em.wait_now(V, SD)
        em.wait_now(V, SS)
        for j in range(NFT):
            av = ft_(attn_g, j, TB)[:, 0:n].rearrange("p (c s) -> p c s", s=L)
            mv = ft_(M_out, j, L).unsqueeze(1).broadcast_to([128, n // L, L])
            em.op(V, lambda E, o=av, a=av, m=mv: E.tensor_tensor(o, a, m, op=AL.mult),
                  [ft_(attn_g, j, TB), M_out], [ft_(attn_g, j, TB)])
            TT(ft_(attn_g, j, TB)[:, 0:n], ft_(attn_g, j, TB)[:, 0:n],
               ft_(gg_g, j, TB)[:, 0:n], AL.mult)
        em.wait_now(Tn, SV)
        for ot in range(NFT):
            bank = pb[ot % 2]
            if ot >= 2:
                em.wait_now(Tn, SV)
            for j in range(NFT):
                MM(bank[:, 0:n], ft_(B_W, j, D)[:, ot * 128:(ot + 1) * 128],
                   ft_(attn_g, j, TB)[:, 0:n], j == 0, j == NFT - 1)
            em.wait_now(V, ST)
            CP(ft_(y_g, ot, TB)[:, 0:n], bank[:, 0:n])
        em.wait_now(Tn, SV)
        for j in range(NFT):
            MM(pb[2][0:1, 0:n], meanw[:, 0:1], ft_(y_g, j, TB)[:, 0:n], j == 0, j == NFT - 1)
        em.wait_now(V, ST)
        CP(rowtmp[0:1, 500:500 + n], pb[2][0:1, 0:n])
        em.wait_now(Tn, SV)
        MM(pb[3][:, 0:n], ones_row[0:1, :], rowtmp[0:1, 500:500 + n], True, True)
        em.wait_now(V, ST)
        for j in range(NFT):
            TT(ft_(ym_g, j, TB)[:, 0:n], ft_(y_g, j, TB)[:, 0:n], pb[3][:, 0:n], AL.subtract)
        em.wait_now(Sc, SV)
        for j in range(NFT):
            ACT(ft_(s_g, j, TB)[:, 0:n], ft_(ym_g, j, TB)[:, 0:n], AF.Square)
        em.wait_now(Tn, SS)
        for j in range(NFT):
            MM(pb[2][0:1, 0:n], meanw[:, 0:1], ft_(s_g, j, TB)[:, 0:n], j == 0, j == NFT - 1)
        em.wait_now(Sc, ST)
        ACT(rowtmp[0:1, 300:300 + n], pb[2][0:1, 0:n], AF.Sqrt, bias=cEPS[0:1, 0:1])
        em.wait_now(V, SS)
        RECIP(rowtmp[0:1, 300:300 + n], rowtmp[0:1, 300:300 + n])
        em.wait_now(Tn, SV)
        MM(pb[3][:, 0:n], ones_row[0:1, :], rowtmp[0:1, 300:300 + n], True, True)
        em.wait_now(V, ST)
        for j in range(NFT):
            sj = ft_(s_g, j, TB)[:, 0:n]
            TT(sj, ft_(ym_g, j, TB)[:, 0:n], pb[3][:, 0:n], AL.mult)
            TS(sj, sj, c_gln[:, j:j + 1], AL.mult, c_bln[:, j:j + 1], AL.add)
        em.wait_now(Sy, SV)
        dma(sp_s[:, gc:gc + n].rearrange("(j f) c -> f j c", f=128),
            s_g[:, 0:NFT * TB].rearrange("p (j t) -> p j t", t=TB)[:, :, 0:n])
        barrier()

    assert NFT * TB >= 2 * TOWN, "buffer layout assumes TB*8 >= 2*TOWN"
    xq_bufs = [B3, B4, B5, B6]
    def xq_t(j):
        return xq_bufs[j // 2][:, (j % 2) * TOWN:(j % 2 + 1) * TOWN]

    tz = B0[:, 0:TOWN]
    tg = B0[:, TOWN:2 * TOWN]
    ti16 = B1[:, 0:(TOWN + 1) // 2].bitcast(I16)[:, 0:TOWN]
    tu16 = B1[:, 0:(TOWN + 1) // 2].bitcast(U16)[:, 0:TOWN]

    def quant4(out, x, scol, n, eng=V, z=None, g=None, i16=None):
        s = SCAL[:, scol:scol + 1]
        rs = SCAL[:, scol + 1:scol + 2]
        ns = SCAL[:, scol + 2:scol + 3]
        z = z if z is not None else tz
        g = g if g is not None else tg
        i16 = i16 if i16 is not None else ti16
        TS(z[:, 0:n], x, rs, AL.mult, eng=eng)
        TS(i16[:, 0:n], x, rs, AL.mult, eng=eng)
        TT(g[:, 0:n], z[:, 0:n], i16[:, 0:n], AL.subtract, eng=eng)
        STT(z[:, 0:n], g[:, 0:n], 0.5, i16[:, 0:n], AL.is_gt, AL.add, eng=eng)
        STT(g[:, 0:n], g[:, 0:n], -0.5, z[:, 0:n], AL.is_ge, AL.add, eng=eng)
        TS(out, g[:, 0:n], s, AL.mult, ns, AL.add, eng=eng)

    def gp_round_to_int(out_i16, z, g, r0f, n):
        TS(out_i16[:, 0:n], z[:, 0:n], 1.0, AL.mult, eng=Gp)
        TS(r0f[:, 0:n], out_i16[:, 0:n], 1.0, AL.mult, eng=Gp)
        TT(g[:, 0:n], z[:, 0:n], r0f[:, 0:n], AL.subtract, eng=Gp)
        TS(z[:, 0:n], g[:, 0:n], 0.5, AL.is_gt, eng=Gp)
        TT(r0f[:, 0:n], z[:, 0:n], r0f[:, 0:n], AL.add, eng=Gp)
        TS(g[:, 0:n], g[:, 0:n], -0.5, AL.is_ge, eng=Gp)
        TT(g[:, 0:n], g[:, 0:n], r0f[:, 0:n], AL.add, eng=Gp)
        TS(z[:, 0:n], g[:, 0:n], -1.0, AL.add, eng=Gp)

    def v_round_i16(out_i16, z, g, n):
        TS(out_i16[:, 0:n], z[:, 0:n], 1.0, AL.mult, eng=V)
        TT(g[:, 0:n], z[:, 0:n], out_i16[:, 0:n], AL.subtract, eng=V)
        STT(z[:, 0:n], g[:, 0:n], 0.5, out_i16[:, 0:n], AL.is_gt, AL.add, eng=V)
        STT(g[:, 0:n], g[:, 0:n], -0.5, z[:, 0:n], AL.is_ge, AL.add, eng=V)
        TS(out_i16[:, 0:n], g[:, 0:n], 1.0, AL.mult, -1.0, AL.add, eng=V)

    for bidx, buf in enumerate(xq_bufs):
        dma(buf[:, 0:2 * TOWN].rearrange("p (j t) -> p j t", t=TOWN),
            sp_xn[bidx * 256:(bidx + 1) * 256, own0:own0 + TOWN]
            .rearrange("(j f) c -> f j c", f=128))
    em.wait_now(V, SD)
    for j in range(NFT):
        quant4(xq_t(j), xq_t(j), 0, TOWN)
    barrier()

    wnat = B1
    wdqT = B7
    ustg4 = B2[:, 0:4 * FTB]
    wsc = B2[:, 4 * FTB:4 * FTB + 224]
    wz = B0[:, 0:FF]
    wgs = B8[:, 0:FF]
    wi16 = B9[:, 0:FF // 2].bitcast(I16)[:, 0:FF]

    def dequant_tile(wtile, nblk):
        bm = wsc[:, 0:nblk]
        sw = wsc[:, 96:96 + nblk]
        em.op(V, lambda E, o=bm.unsqueeze(2), x=wtile.rearrange("p (b s) -> p b s", s=32):
              E.tensor_reduce(o, x, axis=AX.X, op=AL.max, apply_absolute_value=True),
              [wtile], [wsc])
        TS(sw, bm, 1.0 / 7.0, AL.mult, 1e-12, AL.max)
        rw = bm
        RECIP(rw, sw)
        rwb = rw.unsqueeze(2).broadcast_to([128, nblk, 32])
        swb = sw.unsqueeze(2).broadcast_to([128, nblk, 32])
        nw = nblk * 32
        wv = wtile.rearrange("p (b s) -> p b s", s=32)
        zv = wz[:, 0:nw].rearrange("p (b s) -> p b s", s=32)
        i16v = wi16[:, 0:nw].rearrange("p (b s) -> p b s", s=32)
        em.op(V, lambda E, o=zv, a=wv, b=rwb: E.tensor_tensor(o, a, b, op=AL.mult),
              [wtile, wsc], [wz])
        em.op(V, lambda E, o=i16v, a=wv, b=rwb: E.tensor_tensor(o, a, b, op=AL.mult),
              [wtile, wsc], [wi16])
        TT(wgs[:, 0:nw], wz[:, 0:nw], wi16[:, 0:nw], AL.subtract)
        STT(wz[:, 0:nw], wgs[:, 0:nw], 0.5, wi16[:, 0:nw], AL.is_gt, AL.add)
        STT(wgs[:, 0:nw], wgs[:, 0:nw], -0.5, wz[:, 0:nw], AL.is_ge, AL.add)
        TS(wz[:, 0:nw], wgs[:, 0:nw], 1.0, AL.mult, -1.0, AL.add)
        ov = wtile.rearrange("p (b s) -> p b s", s=32)
        yv = wz[:, 0:nw].rearrange("p (b s) -> p b s", s=32)
        em.op(V, lambda E, o=ov, a=yv, b=swb: E.tensor_tensor(o, a, b, op=AL.mult),
              [wz, wsc], [wtile])

    NOS = NIT // 4

    def ffn_matmul_weight(w_in, spill, amax_col):
        for os_ in range(NOS):
            em.wait_now(Sy, ST)
            dma(wnat[:, 0:4 * D].rearrange("p (q f) -> p q f", f=D),
                w_in[os_ * 512:(os_ + 1) * 512, :].rearrange("(q r) f -> r q f", r=128))
            em.wait_now(V, SD)
            for q in range(4):
                dequant_tile(ft_(wnat, q, D), 32)
                em.wait_now(Tn, SV)
                for j in range(NFT):
                    TR(pb[2 + j // 4][:, (j % 4) * 128:(j % 4 + 1) * 128],
                       ft_(wnat, q, D)[:, j * 128:(j + 1) * 128])
                em.wait_now(V, ST)
                for half in range(2):
                    for j4 in range(4):
                        j = half * 4 + j4
                        CP(wdqT[:, j * 512 + q * 128:j * 512 + (q + 1) * 128],
                           pb[2 + half][:, j4 * 128:(j4 + 1) * 128])
                em.wait_now(Tn, SV)
            for tb in range(NFTB):
                em.wait_now(V, SD)
                for q in range(4):
                    bank = pb[q % 2]
                    if q >= 2 or tb > 0:
                        em.wait_now(Tn, SV)
                    for j in range(NFT):
                        MM(bank[:, 0:FTB], wdqT[:, j * 512 + q * 128:j * 512 + (q + 1) * 128],
                           xq_t(j)[:, tb * FTB:(tb + 1) * FTB], j == 0, j == NFT - 1)
                    em.wait_now(V, ST)
                    CP(ustg4[:, q * FTB:(q + 1) * FTB], bank[:, 0:FTB])
                    RED(stats[:, 6:7], ustg4[:, q * FTB:(q + 1) * FTB], AL.max, absval=True)
                    TT(AMAX[:, amax_col:amax_col + 1], AMAX[:, amax_col:amax_col + 1],
                       stats[:, 6:7], AL.max)
                em.wait_now(Sy, SV)
                dma(spill[os_ * 512:(os_ + 1) * 512, tb * FTB:(tb + 1) * FTB]
                    .rearrange("(q r) t -> r q t", r=128),
                    ustg4[:, 0:4 * FTB].rearrange("p (q t) -> p q t", t=FTB))
            if os_ % 2 == 1:
                barrier()
        barrier()

    ffn_matmul_weight(Wup, sp_upr, 1)
    ffn_matmul_weight(Wgate, sp_gater, 2)

    for mt in range(NFT):
        dma(wnat[:, 0:FF], Wdown[mt * 128:(mt + 1) * 128, :])
        em.wait_now(V, SD)
        dequant_tile(wnat[:, 0:FF], 96)
        em.wait_now(Tn, SV)
        for it in range(NIT):
            TR(pb[2 + (it % 8) // 4][:, (it % 4) * 128:(it % 4 + 1) * 128],
               wnat[:, it * 128:(it + 1) * 128])
            if it % 8 == 7:
                em.wait_now(V, ST)
                for b in range(2):
                    CP(stage[:, b * 512:(b + 1) * 512], pb[2 + b][:, 0:512])
                em.wait_now(Sy, SV)
                dma(sp_wd[(it - 7) * 128:(it + 1) * 128, mt * 128:(mt + 1) * 128]
                    .rearrange("(k r) c -> r k c", r=128),
                    stage[:, 0:1024].rearrange("p (k c) -> p k c", c=128))
                em.wait_now(Tn, SD)
                em.wait_now(V, SD)
        if mt % 3 == 2:
            barrier()
    barrier()

    cross_part_max(AMAX[:, 1:2], 0)
    cross_part_max(AMAX[:, 2:3], 1)
    allreduce_scales(1, 0, 2, [3, 6])
    barrier()

    class QSet:
        pass
    sV = QSet()
    sV.upt = B3[:, 0:TOWN]; sV.gat = B3[:, TOWN:2 * TOWN]
    sV.sig = B4[:, 0:TOWN]; sV.gst = B4[:, TOWN:2 * TOWN]
    sV.z = B6[:, 0:TOWN]; sV.g = B6[:, TOWN:2 * TOWN]
    sV.i16 = B5[:, 0:TOWN // 2].bitcast(I16)[:, 0:TOWN]
    sV.u16 = B5[:, TOWN // 2:TOWN].bitcast(U16)[:, 0:TOWN]
    sV.gi16 = B5[:, TOWN:TOWN + TOWN // 2].bitcast(I16)[:, 0:TOWN]
    sV.r0f = None
    sG = QSet()
    sG.upt = B0[:, 0:TOWN]; sG.gat = B0[:, TOWN:2 * TOWN]
    sG.sig = B1[:, 0:TOWN]; sG.gst = B1[:, TOWN:2 * TOWN]
    sG.z = B2[:, 0:TOWN]; sG.g = B2[:, TOWN:2 * TOWN]
    sG.i16 = B7[:, 0:TOWN // 2].bitcast(I16)[:, 0:TOWN]
    sG.u16 = B7[:, TOWN // 2:TOWN].bitcast(U16)[:, 0:TOWN]
    sG.gi16 = B7[:, TOWN:TOWN + TOWN // 2].bitcast(I16)[:, 0:TOWN]
    sG.r0f = B9[:, 0:TOWN]

    qlast = {id(sV): None, id(sG): None}

    def qs_wait_reuse(S, eng):
        snap = qlast[id(S)]
        if snap is not None:
            ep, cnts = snap
            if ep == em.nbar:
                for s_, v_ in cnts.items():
                    em.wait(Sy, s_, v_)

    def qs_record(S, eng):
        es = esem_name[eng]
        if em.unfenced[SEM_OWNER[es]] > 0:
            em.fence(SEM_OWNER[es], es)
        if em.unfenced["scalar"] > 0:
            em.fence("scalar", "ss")
        if em.unfenced["vector"] > 0:
            em.fence("vector", "sv")
        snap = {es: em.counts[es], "ss": em.counts["ss"], "sv": em.counts["sv"]}
        for ds in DSEMS:
            snap[ds] = em.counts[ds]
        qlast[id(S)] = (em.nbar, snap)

    esem_name = {V: "sv", Gp: "sg", Sc: "ss"}

    def p5_itile(it, eng, S):
        r0_, r1_ = it * 128, it * 128 + 128
        qs_wait_reuse(S, eng)
        dma(S.upt, sp_upr[r0_:r1_, :])
        dma(S.gat, sp_gater[r0_:r1_, :])
        em.wait_now(eng, SD)
        rs2 = SCAL[:, 4:5]; rs3 = SCAL[:, 7:8]; s3 = SCAL[:, 6:7]
        if eng == Gp:
            TT(S.z, S.upt, rs2.broadcast_to([128, TOWN]), AL.mult, eng=Gp)
            gp_round_to_int(S.i16, S.z, S.g, S.r0f, TOWN)
            TS(S.i16, S.z, 1.0, AL.mult, eng=Gp)
        else:
            TS(S.z, S.upt, rs2, AL.mult, eng=V)
            v_round_i16(S.i16, S.z, S.g, TOWN)
        em.wait_now(Sy, esem_name[eng] if False else ("sg" if eng == Gp else "sv"))
        dma(sp_upq[r0_:r1_, :], S.i16)
        if eng == Gp:
            TT(S.z, S.gat, rs3.broadcast_to([128, TOWN]), AL.mult, eng=Gp)
            gp_round_to_int(S.gi16, S.z, S.g, S.r0f, TOWN)
            TT(S.gat, S.z, s3.broadcast_to([128, TOWN]), AL.mult, eng=Gp)
        else:
            TS(S.z, S.gat, rs3, AL.mult, eng=V)
            v_round_i16(S.gi16, S.z, S.g, TOWN)
            TS(S.gat, S.gi16, s3, AL.mult, eng=V)
        em.wait_now(Sc, "sg" if eng == Gp else "sv")
        ACT(S.sig, S.gat, AF.Sigmoid)
        em.wait_now(eng, SS)
        if eng == Gp:
            TS(S.z, S.sig, 65536.0, AL.mult, eng=Gp)
            TS(S.z, S.z, 65535.0, AL.min, eng=Gp)
            TS(S.u16, S.z, 1.0, AL.mult, eng=Gp)
            TS(S.r0f, S.u16, 1.0, AL.mult, eng=Gp)
            TT(S.g, S.z, S.r0f, AL.subtract, eng=Gp)
            TS(S.z, S.g, 0.5, AL.is_gt, eng=Gp)
            TT(S.r0f, S.z, S.r0f, AL.add, eng=Gp)
            TS(S.g, S.g, -0.5, AL.is_ge, eng=Gp)
            TT(S.g, S.g, S.r0f, AL.add, eng=Gp)
            TS(S.g, S.g, -1.0, AL.add, eng=Gp)
            TS(S.sig, S.g, 1.0 / 65536.0, AL.mult, eng=Gp)
        else:
            TS(S.z, S.sig, 65536.0, AL.mult, 65535.0, AL.min, eng=V)
            TS(S.u16, S.z, 1.0, AL.mult, eng=V)
            TT(S.g, S.z, S.u16, AL.subtract, eng=V)
            STT(S.z, S.g, 0.5, S.u16, AL.is_gt, AL.add, eng=V)
            STT(S.g, S.g, -0.5, S.z, AL.is_ge, AL.add, eng=V)
            TS(S.sig, S.g, 1.0 / 65536.0, AL.mult, -1.0 / 65536.0, AL.add, eng=V)
        TT(S.gst, S.gat, S.sig, AL.mult, eng=eng)
        if eng == Gp:
            em.wait_now(V, "sg")
        RED(stats[:, 7:8], S.gst, AL.max, absval=True, eng=V)
        TT(AMAX[:, 3:4], AMAX[:, 3:4], stats[:, 7:8], AL.max, eng=V)
        em.wait_now(Sy, "sg" if eng == Gp else "sv")
        dma(sp_gs[r0_:r1_, :], S.gst)
        qs_record(S, eng)

    def p6_itile(it, eng, S):
        r0_, r1_ = it * 128, it * 128 + 128
        qs_wait_reuse(S, eng)
        dma(S.gst, sp_gs[r0_:r1_, :])
        dma(S.i16, sp_upq[r0_:r1_, :])
        em.wait_now(eng, SD)
        rs4 = SCAL[:, 10:11]; s4 = SCAL[:, 9:10]; s2 = SCAL[:, 3:4]
        if eng == Gp:
            TT(S.z, S.gst, rs4.broadcast_to([128, TOWN]), AL.mult, eng=Gp)
            gp_round_to_int(S.gi16, S.z, S.g, S.r0f, TOWN)
            TT(S.gat, S.z, s4.broadcast_to([128, TOWN]), AL.mult, eng=Gp)
            TS(S.z, S.i16, 1.0, AL.mult, eng=Gp)
            TT(S.upt, S.z, s2.broadcast_to([128, TOWN]), AL.mult, eng=Gp)
        else:
            TS(S.z, S.gst, rs4, AL.mult, eng=V)
            v_round_i16(S.gi16, S.z, S.g, TOWN)
            TS(S.gat, S.gi16, s4, AL.mult, eng=V)
            TS(S.upt, S.i16, s2, AL.mult, eng=V)
        TT(S.gst, S.gat, S.upt, AL.mult, eng=eng)
        if eng == Gp:
            em.wait_now(V, "sg")
        RED(stats[:, 8:9], S.gst, AL.max, absval=True, eng=V)
        TT(AMAX[:, 4:5], AMAX[:, 4:5], stats[:, 8:9], AL.max, eng=V)
        em.wait_now(Sy, "sg" if eng == Gp else "sv")
        dma(sp_o[r0_:r1_, :], S.gst)
        qs_record(S, eng)

    for it in range(NIT):
        p5_itile(it, V, sV if it % 2 == 0 else sG)
        if it % 12 == 11:
            barrier()

    barrier()
    cross_part_max(AMAX[:, 3:4], 0)
    allreduce_scales(2, 0, 1, [9])
    barrier()

    for it in range(NIT):
        p6_itile(it, V, sV if it % 2 == 0 else sG)
        if it % 12 == 11:
            barrier()

    barrier()
    cross_part_max(AMAX[:, 4:5], 0)
    allreduce_scales(3, 0, 1, [12])
    barrier()

    oq4 = [B3[:, 0:4 * FTB], B4[:, 0:4 * FTB]]
    wdb = [B7, B2[:, 0:4 * D]]
    g2t = B5
    st_ = B6[:, 0:NFT * FTB]
    xrow = B0[:, 0:D]
    orow = B0[:, D:2 * D]
    prodst = B1[:, 0:D]
    q7z = B8[:, 0:FTB]
    q7g = B8[:, FTB:2 * FTB]
    q7i = B8[:, 2 * FTB:2 * FTB + FTB // 2].bitcast(I16)[:, 0:FTB]
    mmfence = {}

    def p7_load_quant(tb, isl):
        b = isl % 2
        c0 = tb * FTB
        dma(oq4[b].rearrange("p (q t) -> p q t", t=FTB),
            sp_o[isl * 512:(isl + 1) * 512, c0:c0 + FTB]
            .rearrange("(q r) t -> r q t", r=128))
        dma(wdb[b][:, 0:4 * D].rearrange("p (q f) -> p q f", f=D),
            sp_wd[isl * 512:(isl + 1) * 512, :].rearrange("(q r) f -> r q f", r=128))
        em.wait_now(V, SD)
        for q in range(4):
            quant4(oq4[b][:, q * FTB:(q + 1) * FTB], oq4[b][:, q * FTB:(q + 1) * FTB],
                   12, FTB, z=q7z, g=q7g, i16=q7i)

    for tb in range(NFTB):
        c0 = tb * FTB
        p7_load_quant(tb, 0)
        for isl in range(NOS):
            b = isl % 2
            em.wait_now(Tn, SV)
            for m in range(NFT):
                for q in range(4):
                    MM(pb[m][:, 0:FTB], ft_(wdb[b], q, D)[:, m * 128:(m + 1) * 128],
                       oq4[b][:, q * FTB:(q + 1) * FTB],
                       isl == 0 and q == 0, isl == NOS - 1 and q == 3)
            if em.unfenced["tensor"] > 0:
                em.fence("tensor", "st")
            mmfence[isl] = em.counts["st"]
            if isl + 1 < NOS:
                if isl >= 1:
                    em.wait(Sy, "st", mmfence[isl - 1])
                    em.wait(V, "st", mmfence[isl - 1])
                p7_load_quant(tb, isl + 1)
        barrier()
        em.wait_now(Sc, ST)
        for m in range(NFT):
            ACT(ft_(g2t, m, FTB), pb[m][:, 0:FTB], AF.Silu)
        dma(st_[:, 0:NFT * FTB].rearrange("p (j t) -> p j t", t=FTB),
            sp_s[:, c0:c0 + FTB].rearrange("(j f) c -> f j c", f=128))
        em.wait_now(V, SD)
        em.wait_now(V, SS)
        for j in range(NFT):
            TT(ft_(g2t, j, FTB), ft_(st_, j, FTB), ft_(g2t, j, FTB), AL.mult)
        em.wait_now(Tn, SV)
        for tt in range(FTB // 128):
            for j in range(NFT):
                TR(pb[j // 4][:, (j % 4) * 128:(j % 4 + 1) * 128],
                   ft_(g2t, j, FTB)[:, tt * 128:(tt + 1) * 128])
            em.wait_now(V, ST)
            CP(prodst[:, 0:512], pb[0][:, 0:512])
            CP(prodst[:, 512:1024], pb[1][:, 0:512])
            row0 = c0 + tt * 128
            dma(xrow, x_own[1 + row0:1 + row0 + 128, :])
            em.wait_now(V, SD)
            TT(orow, prodst[:, 0:D], xrow, AL.add)
            em.wait_now(Sy, SV)
            dma(y_out[row0:row0 + 128, :], orow)
            em.wait_now(V, SD)
            em.wait_now(Tn, SD)

    if debug:
        dbg_specs = [("d_scal", SCAL[:, 0:32], [128, 32], None),
                     ("d_amax", AMAX[:, 0:8], [128, 8], None)]
        for nm, ap_, shp, _ in dbg_specs:
            od = dt(nm, shp, F32, kind="ExternalOutput").ap()
            dma(od, ap_)
        for nm, sp, shp in [("d_xn", sp_xn, [D, TC]), ("d_s", sp_s, [D, TOWN]),
                            ("d_upr", sp_upr, [FF, TOWN]), ("d_gater", sp_gater, [FF, TOWN]),
                            ("d_gs", sp_gs, [FF, TOWN]), ("d_o", sp_o, [FF, TOWN]),
                            ("d_k", sp_k, [D, TWARM + TOWN]), ("d_r", sp_r, [D, TOWN]),
                            ("d_g", sp_g, [D, TOWN])]:
            od = dt(nm, shp, F32, kind="ExternalOutput").ap()
            dma(od, sp)
    em.wait_now(Sy, SD)
    barrier()
    return nc, em


def finalize(nc, em):
    ctxs = []
    sems = {}
    esems = {}
    for name in em.counts:
        h = nc.semaphore(name)
        sems[name] = h.__enter__()
        ctxs.append(h)
    for e in ENGINES:
        h = nc.semaphore("ep_" + e)
        esems[e] = h.__enter__()
        ctxs.append(h)
    blk = nc.Block()
    block = blk.__enter__()
    em.replay(block, sems, esems, SEM_OWNER)
    blk.__exit__(None, None, None)
    for h in reversed(ctxs):
        h.__exit__(None, None, None)
    return nc


def make_core_inputs(x_full, params, core, n_cores, cfg: Cfg):
    TOWN, TWARM = cfg.TOWN, cfg.TWARM
    t0 = core * TOWN
    xo = np.zeros((cfg.NOP, D), np.float32)
    xo[1:1 + TOWN] = x_full[t0:t0 + TOWN]
    xo[0] = x_full[1] if core == 0 else x_full[t0 - 1]
    xw = np.zeros((cfg.NWP, D), np.float32)
    if core > 0 and TWARM > 0:
        xw[1:1 + TWARM] = x_full[t0 - TWARM:t0]
        xw[0] = x_full[t0 - TWARM - 1] if t0 - TWARM - 1 >= 0 else 0.0
    vc = np.stack([params[k].reshape(NFT, 128).T for k in
                   ("td", "tmk", "tmv", "tmr", "tmg", "g_ln", "b_ln")], 1)
    vc = vc.reshape(128, 7 * NFT, order="F") if False else np.concatenate(
        [params[k].reshape(NFT, 128).T for k in
         ("td", "tmk", "tmv", "tmr", "tmg", "g_ln", "b_ln")], axis=1)
    return {
        "x_own": xo, "x_warm": xw,
        "WrT": np.ascontiguousarray(params["Wr"].T),
        "WkT": np.ascontiguousarray(params["Wk"].T),
        "WvT": np.ascontiguousarray(params["Wv"].T),
        "WgT": np.ascontiguousarray(params["Wg"].T),
        "WoT": np.ascontiguousarray(params["Wo"].T),
        "Wup": params["w_up"], "Wgate": params["w_gate"], "Wdown": params["w_down"],
        "vec_cols": np.ascontiguousarray(vc, np.float32),
        "td_row": params["td"].reshape(1, D),
        "ident_c": np.eye(128, dtype=np.float32),
        "mask64_c": np.triu(np.ones((64, 64), np.float32)),
        "iota64_c": np.arange(64, dtype=np.float32).reshape(1, 64),
        "ones_row_c": np.ones((1, 128), np.float32),
        "meanw_c": np.full((128, 1), 1.0 / D, np.float32),
    }


from concourse.bass_utils import run_bass_kernel_spmd

_N_CORES = 8
_CFG = Cfg(TOWN=2048, TWARM=1024, TB=512, n_cores=_N_CORES)
_NC_CACHE = {}


def _get_program():
    if "nc" not in _NC_CACHE:
        nc, em = build(_CFG)
        finalize(nc, em)
        _NC_CACHE["nc"] = nc
    return _NC_CACHE["nc"]


def kernel(L0_attn_x, L0_attn_time_decay, L0_attn_time_mix_k, L0_attn_time_mix_v,
           L0_attn_time_mix_r, L0_attn_time_mix_g, L0_attn_W_r, L0_attn_W_k,
           L0_attn_W_v, L0_attn_W_g, L0_attn_W_o, L0_attn_ln_gamma,
           L0_attn_ln_beta, L0_ffn_w_gate, L0_ffn_w_up, L0_ffn_w_down):
    x_full = np.ascontiguousarray(np.asarray(L0_attn_x, np.float32).reshape(-1, D))
    params = {
        "td": np.asarray(L0_attn_time_decay, np.float32),
        "tmk": np.asarray(L0_attn_time_mix_k, np.float32),
        "tmv": np.asarray(L0_attn_time_mix_v, np.float32),
        "tmr": np.asarray(L0_attn_time_mix_r, np.float32),
        "tmg": np.asarray(L0_attn_time_mix_g, np.float32),
        "Wr": np.asarray(L0_attn_W_r, np.float32),
        "Wk": np.asarray(L0_attn_W_k, np.float32),
        "Wv": np.asarray(L0_attn_W_v, np.float32),
        "Wg": np.asarray(L0_attn_W_g, np.float32),
        "Wo": np.asarray(L0_attn_W_o, np.float32),
        "g_ln": np.asarray(L0_attn_ln_gamma, np.float32),
        "b_ln": np.asarray(L0_attn_ln_beta, np.float32),
        "w_gate": np.asarray(L0_ffn_w_gate, np.float32),
        "w_up": np.asarray(L0_ffn_w_up, np.float32),
        "w_down": np.asarray(L0_ffn_w_down, np.float32),
    }
    nc = _get_program()
    in_maps = [make_core_inputs(x_full, params, c, _N_CORES, _CFG)
               for c in range(_N_CORES)]
    res = run_bass_kernel_spmd(nc, in_maps, core_ids=list(range(_N_CORES)))
    out = np.concatenate([np.asarray(r["y"], np.float32) for r in res.results], 0)
    return out.reshape(1, _N_CORES * _CFG.TOWN, D)
